# version 09
# baseline (speedup 1.0000x reference)
import numpy as np
import ml_dtypes
import concourse.bass as bass
import concourse.mybir as mybir
from concourse.bass_utils import run_bass_kernel_spmd

F32 = mybir.dt.float32
BF16 = mybir.dt.bfloat16
AF = mybir.ActivationFunctionType
ALU = mybir.AluOpType

MAGIC = 12582912.0
_BF = ml_dtypes.bfloat16


def make_cfg(tok=2048):
    c = {}
    c["TOK"] = tok
    c["TB"] = 4 * tok
    c["QT"] = 512
    c["NQT"] = tok // 512
    c["NBQT"] = (4 * tok) // 512
    c["KB"] = (4 * tok) // 128
    c["DCH"] = 8
    c["FCH"] = 32
    return c


CFG = make_cfg(2048)
FUSED = True


def _bias8(b):
    return np.ascontiguousarray(np.asarray(b, np.float32).reshape(-1, 128).T)


def _host_consts(cfg):
    QT = cfg["QT"]
    masks = np.zeros((128, 4 * QT), dtype=np.float32)
    for i in range(4):
        d = 128 * i
        for kk in range(128):
            if kk + d < QT:
                masks[kk, QT * i + kk + d:QT * i + QT] = 1.0
    return dict(
        c_ident=np.eye(128, dtype=np.float32),
        c_ones_col=np.ones((128, 1), dtype=np.float32),
        c_ones_row=np.ones((1, 128), dtype=np.float32),
        c_masks=masks,
    )


def _ring(w):
    w = np.asarray(w, np.float32)
    din, dout = w.shape
    nd = dout // 128
    return np.ascontiguousarray(
        w.reshape(din // 128, 128, nd, 128).transpose(2, 1, 0, 3).reshape(nd * 128, din))


def _prep_weights(I):
    d = {}
    d["w_q0"] = _ring(I["L0_attn_Wq"])
    d["w_k0"] = _ring(I["L0_attn_Wk"])
    d["w_v0"] = _ring(I["L0_attn_Wv"])
    d["w_l0"] = _ring(I["L0_attn_Wl"])
    d["w_o0"] = _ring(I["L0_attn_Wo"])
    for nm, k in [("b_q0", "L0_attn_bq"), ("b_k0", "L0_attn_bk"), ("b_v0", "L0_attn_bv"),
                  ("b_l0", "L0_attn_bl"), ("b_o0", "L0_attn_bo"), ("b_o1", "L1_attn_bo")]:
        d[nm] = _bias8(I[k])
    d["w_1T"] = np.ascontiguousarray(np.asarray(I["L0_ffn_W1"], dtype=np.float32).T)
    d["w_2T"] = np.ascontiguousarray(np.asarray(I["L0_ffn_W2"], dtype=np.float32).T)
    d["b_1"] = _bias8(I["L0_ffn_b1"])
    d["b_2"] = _bias8(I["L0_ffn_b2"])
    d["w_q1"] = _ring(
        np.transpose(np.asarray(I["L1_attn_Wq"], dtype=np.float32), (1, 0, 2)).reshape(1024, 1024))
    d["w_k1"] = _ring(
        np.transpose(np.asarray(I["L1_attn_Wk"], dtype=np.float32), (1, 0, 2)).reshape(1024, 1024))
    d["w_v1"] = _ring(
        np.transpose(np.asarray(I["L1_attn_Wv"], dtype=np.float32), (1, 0, 2)).reshape(1024, 1024))
    d["w_o1"] = _ring(I["L1_attn_Wo"])
    d["w_r1T"] = _ring(np.asarray(I["L1_ffn_W_r"], dtype=np.float32).T)
    d["w_k1T"] = _ring(np.asarray(I["L1_ffn_W_k"], dtype=np.float32).T)
    d["w_v1T"] = _ring(np.asarray(I["L1_ffn_W_v"], dtype=np.float32).T)
    d["mu_r"] = _bias8(I["L1_ffn_mu_r"])
    d["mu_k"] = _bias8(I["L1_ffn_mu_k"])
    return d


def _slice_x(x, core, cfg):
    TOK = cfg["TOK"]
    b, r = core // 4, core % 4
    xs = np.asarray(x[b, TOK * r:TOK * (r + 1), :], dtype=np.float32)
    return np.ascontiguousarray(xs.T)


class P:
    def __init__(self, nc, cfg, stack):
        self.nc = nc
        self.cfg = cfg
        self.block = None
        self.stack = stack
        self.sems = {}
        self.cnt = {}
        self.dmak = {}
        self.wmark = {}
        self.B = {}
        self.D = {}

    def sem(self, name):
        h = self.stack.enter_context(self.nc.semaphore(name))
        self.sems[name] = h
        self.cnt[name] = 0
        return h

    def inc(self, instr, s, n=1):
        self.cnt[s] += n
        instr.then_inc(self.sems[s], n)
        return (s, self.cnt[s])

    NDS = 4

    def dma(self, q, stream, out, in_, **kw):
        k = self.dmak.setdefault(stream, 0)
        sem = stream + str(k % self.NDS)
        prev = 16 * (k // self.NDS)
        if prev:
            q.wait_ge(self.sems[sem], prev)
        ins = q.dma_start(out=out, in_=in_, **kw)
        self.dmak[stream] = k + 1
        self.cnt[sem] += 16
        instr_tok = (sem, self.cnt[sem])
        ins.then_inc(self.sems[sem], 16)
        return instr_tok

    def dtok(self, stream):
        return tuple(self.tok(stream + str(i)) for i in range(self.NDS))

    def dma_i(self, g, stream, out, in_, idx_ap):
        from concourse.bass import IndirectOffsetOnAxis
        k = self.dmak.setdefault(stream, 0)
        sem = stream + str(k % self.NDS)
        prev = 16 * (k // self.NDS)
        if prev:
            g.wait_ge(self.sems[sem], prev)
        ins = g.indirect_dma_start(out, None, in_,
                                   IndirectOffsetOnAxis(ap=idx_ap, axis=0))
        self.dmak[stream] = k + 1
        self.cnt[sem] += 16
        ins.then_inc(self.sems[sem], 16)
        return (sem, self.cnt[sem])

    def tok(self, s):
        return (s, self.cnt[s])

    def wait(self, eng, *toks):
        seen = {}
        for t in toks:
            if t is None:
                continue
            if isinstance(t, (list, tuple)) and t and isinstance(t[0], (list, tuple)):
                for tt in t:
                    if tt is None:
                        continue
                    s, v = tt
                    if seen.get(s, -1) < v:
                        seen[s] = v
                continue
            s, v = t
            if seen.get(s, -1) < v:
                seen[s] = v
        try:
            ename = eng.engine
        except AttributeError:
            ename = id(eng)
        wm = self.wmark.setdefault(str(ename), {})
        for s, v in seen.items():
            if v > 0 and wm.get(s, 0) < v:
                eng.wait_ge(self.sems[s], v)
                wm[s] = v

    def V(self, v, op, *args, waits=(), **kw):
        self.wait(v, *waits)
        ins = getattr(v, op)(*args, **kw)
        return self.inc(ins, "dve")

    def S(self, s, op, *args, waits=(), **kw):
        self.wait(s, *waits)
        ins = getattr(s, op)(*args, **kw)
        return self.inc(ins, "act")

    def G(self, g, op, *args, waits=(), **kw):
        self.wait(g, *waits)
        ins = getattr(g, op)(*args, **kw)
        return self.inc(ins, "gp")

    _uid = 0

    def _uname(self, name):
        P._uid += 1
        return f"{name}__{P._uid}"

    def sb(self, name, shape, dtype, stack=None):
        h = (stack or self.stack).enter_context(
            self.nc.sbuf_tensor(self._uname(name), shape, dtype))
        self.B[name] = h
        return h

    def ps(self, name, shape, dtype=F32, stack=None):
        h = (stack or self.stack).enter_context(
            self.nc.psum_tensor(self._uname(name), shape, dtype))
        self.B[name] = h
        return h


def seg_load_x(P_, src_ap, dst="x_sb"):
    b = P_.block
    TOK = P_.cfg["TOK"]

    tk = []
    @b.sync
    def _(s):
        tk.append(P_.dma(s, "a",
                         P_.B[dst][:].rearrange("p (c t) -> p c t", t=TOK),
                         src_ap.rearrange("(c p) t -> p c t", p=128)))
    return tk[0]


def seg_consts(P_, names_shapes):
    b = P_.block
    nc = P_.nc

    tk = []
    @b.sync
    def _(s):
        for dname, sname, shape, dtype in names_shapes:
            h = P_.stack.enter_context(P_.nc.sbuf_tensor(P_._uname("sb_" + sname), shape, dtype))
            P_.B[sname] = h
            tk.append(P_.dma(s, "a", h[:], P_.D[dname]))
    return tk[-1]


def seg_castload(P_, dram_name, sbuf_name, shape, stack=None):
    b = P_.block

    tk = []
    @b.gpsimd
    def _(g):
        P_.sb(sbuf_name, shape, BF16, stack=stack)
        tk.append(P_.dma(g, "w", P_.B[sbuf_name][:], P_.D[dram_name]))
    return tk[0]


def _ln_tile(P_, src_ap_fn, it, dep_toks, out_bf_fn=None, add_x_cols=None):
    b = P_.block
    cfg = P_.cfg
    QT, DCH, TOK = cfg["QT"], cfg["DCH"], cfg["TOK"]
    sq, st, bc = P_.B["sq_sb"], P_.B["st_sb"], P_.B["bc_sb"]
    onesb, onesf = P_.B["c_ones_col_bf"], P_.B["c_ones_row_f"]
    pstat = P_.B["pg0"]
    pbc = P_.B["pg1"]
    x_sb = P_.B.get("x_sb")

    cast_toks = []
    @b.vector
    def _(v):
        for c in range(DCH):
            t1 = P_.V(v, "tensor_copy", sq[:, (2 * c) * QT:(2 * c + 1) * QT], src_ap_fn(c),
                      waits=dep_toks if c == 0 else ())
            t2 = P_.V(v, "tensor_tensor", sq[:, (2 * c + 1) * QT:(2 * c + 2) * QT],
                      src_ap_fn(c), src_ap_fn(c), ALU.mult)
            cast_toks.append(t2)

    mm_tok = []
    @b.tensor
    def _(t):
        P_.wait(t, cast_toks[-1])
        for c in range(DCH):
            t.matmul(pstat[0:1, 0:QT], onesb[:], sq[:, (2 * c) * QT:(2 * c + 1) * QT],
                     start=(c == 0), stop=(c == DCH - 1))
        for c in range(DCH):
            ins = t.matmul(pstat[0:1, QT:2 * QT], onesb[:],
                           sq[:, (2 * c + 1) * QT:(2 * c + 2) * QT],
                           start=(c == 0), stop=(c == DCH - 1))
        mm_tok.append(P_.inc(ins, "pe"))

    sm = []
    @b.vector
    def _(v):
        a = P_.V(v, "tensor_scalar", st[0:1, 0:QT], pstat[0:1, 0:QT],
                 1.0 / 1024.0, None, ALU.mult, waits=(mm_tok[0],))
        bt = P_.V(v, "tensor_scalar", st[0:1, QT:2 * QT], pstat[0:1, QT:2 * QT],
                  1.0 / 1024.0, None, ALU.mult)
        c2 = P_.V(v, "tensor_tensor", st[0:1, 2 * QT:3 * QT], st[0:1, 0:QT],
                  st[0:1, 0:QT], ALU.mult, waits=(a,))
        d2 = P_.V(v, "tensor_tensor", st[0:1, 2 * QT:3 * QT], st[0:1, QT:2 * QT],
                  st[0:1, 2 * QT:3 * QT], ALU.subtract, waits=(bt, c2))
        sm.append(d2)

    rs = []
    @b.scalar
    def _(s):
        t1 = P_.S(s, "activation", st[0:1, 3 * QT:4 * QT], st[0:1, 2 * QT:3 * QT], AF.Ln,
                  bias=P_.B["eps_sb"][0:1, 0:1], waits=(sm[0],))
        rs.append(P_.S(s, "activation", st[0:1, 3 * QT:4 * QT], st[0:1, 3 * QT:4 * QT],
                       AF.Exp, scale=-0.5, waits=(t1,)))

    bc_tok = []
    @b.tensor
    def _(t):
        P_.wait(t, rs[0], P_.tok("dve"))
        t.matmul(pbc[0:128, 0:QT], onesf[:], st[0:1, 0:QT], start=True, stop=True)
        ins = t.matmul(pbc[0:128, QT:2 * QT], onesf[:], st[0:1, 3 * QT:4 * QT],
                       start=True, stop=True)
        bc_tok.append(P_.inc(ins, "pe"))

    last = []
    @b.vector
    def _(v):
        mb = P_.V(v, "tensor_copy", bc[:, 0:QT], pbc[0:128, 0:QT], waits=(bc_tok[0],))
        rb = P_.V(v, "tensor_copy", bc[:, QT:2 * QT], pbc[0:128, QT:2 * QT])
        tk = rb
        for c in range(DCH):
            t1 = P_.V(v, "tensor_tensor", bc[:, 2 * QT:3 * QT], src_ap_fn(c), bc[:, 0:QT],
                      ALU.subtract, waits=(mb, rb, tk))
            if out_bf_fn is not None:
                tk = P_.V(v, "tensor_tensor", out_bf_fn(c), bc[:, 2 * QT:3 * QT],
                          bc[:, QT:2 * QT], ALU.mult, waits=(t1,))
            else:
                t2 = P_.V(v, "tensor_tensor", bc[:, 3 * QT:4 * QT], bc[:, 2 * QT:3 * QT],
                          bc[:, QT:2 * QT], ALU.mult, waits=(t1,))
                xa = x_sb[:, c * TOK + add_x_cols[0] + it * QT:c * TOK + add_x_cols[0] + (it + 1) * QT]
                tk = P_.V(v, "tensor_tensor", xa, xa, bc[:, 3 * QT:4 * QT], ALU.add, waits=(t2,))
        last.append(tk)
    return last[0], bc_tok[0]


def seg_prenorm(P_, src_name, dep_toks, qrange=None, dst_name="ln_at", dst_off=0):
    cfg = P_.cfg
    TOK, QT = cfg["TOK"], cfg["QT"]
    if qrange is None:
        qrange = (0, TOK)
    src = P_.B[src_name]
    dst = P_.B[dst_name]
    tk = None
    for it in range((qrange[1] - qrange[0]) // QT):
        q0 = qrange[0] + it * QT
        tk, _ = _ln_tile(
            P_,
            lambda c, q0=q0: src[:, c * TOK + q0:c * TOK + q0 + QT],
            it,
            dep_toks,
            out_bf_fn=lambda c, q0=q0: dst[:, dst_off + c * TOK + q0:dst_off + c * TOK + q0 + QT],
        )
        dep_toks = (tk,)
    return tk


def seg_postnorm_residual(P_, src_name, src_w, dep_toks, qcols):
    cfg = P_.cfg
    QT = cfg["QT"]
    src = P_.B[src_name]
    tk = None
    for it in range((qcols[1] - qcols[0]) // QT):
        tk, _ = _ln_tile(
            P_,
            lambda c, it=it: src[:, c * src_w + it * QT:c * src_w + (it + 1) * QT],
            it,
            dep_toks,
            out_bf_fn=None,
            add_x_cols=(qcols[0],),
        )
        dep_toks = (tk,)
    return tk


def seg_proj(P_, src_name, w_dram_name, n_dout, *, src_w=None, bias=None, act=None,
             out_sbuf=None, out_sbuf_w=None, out_f32=False, out_dram=None,
             dep_toks=(), w_is_bf16=False, single_group=False, transpose_dram=None,
             evac_mult=None, evac_relu_sq=None, src_off=0, qw=None, evac_dep=(),
             w_dep=(), w_sbuf=None, w_tok=None, ring_slots=None, ring_dep=None):
    b = P_.block
    cfg = P_.cfg
    TOK, QT, DCH = cfg["TOK"], cfg["QT"], cfg["DCH"]
    if src_w is None:
        src_w = TOK
    if qw is None:
        qw = src_w
    nq = qw // QT
    src = P_.B[src_name]
    wd = P_.D[w_dram_name]
    din_ch = wd.shape[1] // 128
    assert wd.shape[0] == n_dout * 128, (w_dram_name, wd.shape, n_dout)
    wr = P_.B["w_ring"] if w_sbuf is None else None
    wrW = P_.B.get("_ring_grpw", 0)
    wsb = P_.B[w_sbuf] if w_sbuf is not None else None
    pg = [P_.B["pg0"], P_.B["pg0" if single_group else "pg1"]]
    ident = P_.B.get("c_ident_bf")
    ngroups = 1 if single_group else 2
    if ring_slots is None:
        ring_slots = (0, 0) if single_group else (0, wrW)
    if ring_dep is None:
        ring_dep = P_.tok("pe")

    wtoks = [None] * n_dout
    mmtoks = [None] * n_dout
    evtoks = [None] * n_dout
    stoks = [None] * n_dout
    last_tok = [None]
    psum_dep = (P_.tok("dve"), P_.tok("act"))
    dstage_dep = P_.dtok("a")

    for j in range(n_dout):
        g = j % ngroups
        so_ = ring_slots[g]
        if wr is not None:
            assert so_ + din_ch * 128 <= wr.shape[1], (w_dram_name, so_, din_ch)
        if w_sbuf is not None:
            wtoks[j] = w_tok
        elif w_is_bf16:
            @b.sync
            def _(s, j=j, g=g, so_=so_):
                if mmtoks[j - ngroups] is not None:
                    P_.wait(s, mmtoks[j - ngroups])
                else:
                    P_.wait(s, ring_dep)
                P_.wait(s, *w_dep)
                wtoks[j] = P_.dma(
                    s, "wb",
                    wr[:, so_:so_ + din_ch * 128],
                    wd[128 * j:128 * (j + 1), :])
        else:
            @b.gpsimd
            def _(gq, j=j, g=g, so_=so_):
                if mmtoks[j - ngroups] is not None:
                    P_.wait(gq, mmtoks[j - ngroups])
                else:
                    P_.wait(gq, ring_dep)
                P_.wait(gq, *w_dep)
                wtoks[j] = P_.dma(
                    gq, "w",
                    wr[:, so_:so_ + din_ch * 128],
                    wd[128 * j:128 * (j + 1), :])

        @b.tensor
        def _(t, j=j, g=g, so_=so_):
            P_.wait(t, wtoks[j], *dep_toks)
            if evtoks[j - ngroups] is not None:
                P_.wait(t, evtoks[j - ngroups])
            else:
                P_.wait(t, *psum_dep)
            for c in range(din_ch):
                if w_sbuf is not None:
                    lhs = wsb[:, j * din_ch * 128 + c * 128:j * din_ch * 128 + (c + 1) * 128]
                else:
                    lhs = wr[:, so_ + c * 128:so_ + (c + 1) * 128]
                for qb in range(nq):
                    ins = t.matmul(pg[g][:, qb * QT:(qb + 1) * QT],
                                   lhs,
                                   src[:, src_off + c * src_w:src_off + c * src_w + qw][:, qb * QT:(qb + 1) * QT],
                                   start=(c == 0), stop=(c == din_ch - 1))
            mmtoks[j] = P_.inc(ins, "pe")

        bias_ap = P_.B[bias][:, j:j + 1] if bias else None
        if act is not None:
            @b.scalar
            def _(s, j=j, g=g):
                P_.wait(s, mmtoks[j], *evac_dep)
                kw = {}
                if bias_ap is not None:
                    kw["bias"] = bias_ap
                dst = P_.B[out_sbuf][:, j * out_sbuf_w:j * out_sbuf_w + qw]
                ins = s.activation(dst, pg[g][:, 0:qw], act, **kw)
                evtoks[j] = P_.inc(ins, "act")
                last_tok[0] = evtoks[j]
        elif transpose_dram is not None:
            stg = P_.B["tstage"]
            trp = P_.B["tr_ps"]
            @b.vector
            def _(v, j=j, g=g):
                w = (mmtoks[j],) + tuple(evac_dep) + (P_.dtok("a"),)
                if bias_ap is not None:
                    evtoks[j] = P_.V(v, "tensor_scalar", stg[:, 0:qw], pg[g][:, 0:qw],
                                     bias_ap, None, ALU.add, waits=w)
                else:
                    evtoks[j] = P_.V(v, "tensor_copy", stg[:, 0:qw], pg[g][:, 0:qw], waits=w)
            ntile = qw // 128
            prev_store = [None]
            for i in range(ntile):
                trt = []
                @b.tensor
                def _(t, i=i, j=j):
                    P_.wait(t, evtoks[j], P_.tok("dve"))
                    ins = t.transpose(trp[:, 0:128], P_.B["tstage"][:, 128 * i:128 * (i + 1)],
                                      ident[:])
                    trt.append(P_.inc(ins, "pe"))
                cp = []
                @b.vector
                def _(v, i=i):
                    w = (trt[0],) + ((prev_store[0],) if prev_store[0] else ())
                    cp.append(P_.V(v, "tensor_copy", P_.B["tstage2"][:, 0:128], trp[:, 0:128],
                                   waits=w))
                @b.sync
                def _(s, i=i, j=j):
                    P_.wait(s, cp[0])
                    prev_store[0] = P_.dma(
                        s, "a", transpose_dram[128 * i:128 * (i + 1), 128 * j:128 * (j + 1)],
                        P_.B["tstage2"][:, 0:128])
                    last_tok[0] = prev_store[0]
        elif evac_mult is not None or evac_relu_sq is not None:
            @b.vector
            def _(v, j=j, g=g):
                w = (mmtoks[j],) + tuple(evac_dep)
                if out_dram is not None:
                    w = w + ((stoks[j - 2],) if stoks[j - 2] is not None else (dstage_dep,))
                if evac_mult is not None:
                    mbuf, mw = evac_mult
                    evtoks[j] = P_.V(v, "tensor_tensor", _out_ap(P_, out_sbuf, out_sbuf_w, j, qw, out_dram),
                                     pg[g][:, 0:qw], P_.B[mbuf][:, j * mw:j * mw + qw],
                                     ALU.mult, waits=w)
                else:
                    w2 = w + ((evtoks[j - 1],) if (j >= 1 and evtoks[j - 1]) else ())
                    t1 = P_.V(v, "tensor_scalar", P_.B["tstage"][:, 0:qw], pg[g][:, 0:qw],
                              0.0, None, ALU.max, waits=w2)
                    evtoks[j] = P_.V(v, "tensor_tensor", _out_ap(P_, out_sbuf, out_sbuf_w, j, qw, out_dram),
                                     P_.B["tstage"][:, 0:qw], P_.B["tstage"][:, 0:qw],
                                     ALU.mult, waits=(t1,))
                last_tok[0] = evtoks[j]
        else:
            @b.scalar
            def _(s, j=j, g=g):
                w = (mmtoks[j],) + tuple(evac_dep)
                if out_dram is not None:
                    w = w + ((stoks[j - 2],) if stoks[j - 2] is not None else (dstage_dep,))
                kw = {}
                if bias_ap is not None:
                    kw["bias"] = bias_ap
                evtoks[j] = P_.S(s, "activation",
                                 _out_ap(P_, out_sbuf, out_sbuf_w, j, qw, out_dram),
                                 pg[g][:, 0:qw], AF.Identity, waits=w, **kw)
                last_tok[0] = evtoks[j]
            if out_dram is not None:
                @b.sync
                def _(s, j=j):
                    P_.wait(s, evtoks[j])
                    sl = (j % 2) * (P_.B["dstage"].shape[1] // 2)
                    stoks[j] = P_.dma(s, "a", out_dram[128 * j:128 * (j + 1), :],
                                      P_.B["dstage"][:, sl:sl + qw])
                    last_tok[0] = stoks[j]
    return last_tok[0], evtoks, mmtoks


def _out_ap(P_, out_sbuf, out_sbuf_w, j, qw, out_dram):
    if out_dram is not None:
        sl = (j % 2) * (P_.B["dstage"].shape[1] // 2)
        return P_.B["dstage"][:, sl:sl + qw]
    return P_.B[out_sbuf][:, j * out_sbuf_w:j * out_sbuf_w + qw]


def _attn_pair(P_, *, QP, KPA, KPB, VP, qtiles, out_fn, ldtok_fn, pre_tok):
    b = P_.block
    QT = P_.cfg["QT"]
    sc = [P_.B["sc0"], P_.B["sc1"]]
    av = [P_.B["avA"], P_.B["avB"]]
    ex = P_.B["ex_sb"]
    avs = P_.B["av_sb"]
    rec = P_.B["rec_sb"]
    bcs = P_.B["bc_sb"]
    masks = P_.B.get("c_masks_bf")
    onesf = P_.B["c_ones_row_f"]
    GW = avs.shape[1] // 2
    gper = GW // QT

    gbase = [0]
    exp_raw = {}
    exp_fin = {}
    sc_tok = {}
    av_tok = {}

    ngrp = (len(qtiles) + gper - 1) // gper
    flush_tok = pre_tok
    for fg in range(ngrp):
        grp = qtiles[fg * gper:(fg + 1) * gper]
        evac_tok = None
        for qi, qt_ in enumerate(grp):
            slots_all = [(k, h) for k in range(qt_["kmax"]) for h in (0, 1)]
            groups = [slots_all[i:i + 3] for i in range(0, len(slots_all), 3)]
            Gn = len(groups)
            g0 = gbase[0]
            guard = evac_tok if evac_tok is not None else flush_tok
            d0 = qt_["diag0"]

            order = []
            pend = []
            for gl in range(Gn):
                order.append(("sc", gl))
                pend.append(gl)
                if len(pend) == 2:
                    order.append(("av", pend.pop(0)))
            for gl in pend:
                order.append(("av", gl))
            pe0 = P_.cnt["pe"]
            for i, (kind, gl) in enumerate(order):
                if kind == "sc":
                    sc_tok[g0 + gl] = ("pe", pe0 + i + 1)
                else:
                    av_tok[g0 + gl] = ("pe", pe0 + i + 1)
            act0 = P_.cnt["act"]
            dve0 = P_.cnt["dve"]
            ndve = 0
            for gl, slots in enumerate(groups):
                exp_raw[g0 + gl] = ("act", act0 + gl + 1)
                nm = sum(1 for k, h in slots if d0 is not None and k >= d0)
                if nm:
                    ndve += nm
                    exp_fin[g0 + gl] = ("dve", dve0 + ndve)
                else:
                    exp_fin[g0 + gl] = exp_raw[g0 + gl]

            @b.tensor
            def _(t):
                for kind, gl in order:
                    gi = g0 + gl
                    slots = groups[gl]
                    mk = max(k for k, h in slots)
                    if kind == "sc":
                        w = [ldtok_fn(mk, "sc")]
                        if gi - 2 >= 0:
                            w.append(exp_fin.get(gi - 2))
                        P_.wait(t, *w)
                        for si, (k, h) in enumerate(slots):
                            ins = t.matmul(sc[gi % 2][:, si * QT:(si + 1) * QT],
                                           (KPA if h == 0 else KPB)[:, 128 * k:128 * (k + 1)],
                                           QP[:, qt_["qcols"]:qt_["qcols"] + QT],
                                           start=True, stop=True)
                        tok = P_.inc(ins, "pe")
                        assert tok == sc_tok[gi], (tok, sc_tok[gi])
                    else:
                        P_.wait(t, exp_fin[gi], guard, ldtok_fn(mk, "av"))
                        for si, (k, h) in enumerate(slots):
                            ins = t.matmul(av[h][0:65, :],
                                           VP[:, 130 * k + 65 * h:130 * k + 65 * h + 65],
                                           ex[:, (gi % 2) * 1536 + si * QT:(gi % 2) * 1536 + (si + 1) * QT],
                                           start=(k == 0), stop=(k == qt_["kmax"] - 1))
                        tok = P_.inc(ins, "pe")
                        assert tok == av_tok[gi], (tok, av_tok[gi])

            @b.scalar
            def _(s):
                for gl, slots in enumerate(groups):
                    gi = g0 + gl
                    w = [sc_tok[gi]]
                    if gi - 2 >= 0:
                        w.append(av_tok.get(gi - 2))
                    tok = P_.S(s, "activation",
                               ex[:, (gi % 2) * 1536:(gi % 2) * 1536 + len(slots) * QT],
                               sc[gi % 2][:, 0:len(slots) * QT], AF.Exp, scale=0.125,
                               waits=w)
                    assert tok == exp_raw[gi], (tok, exp_raw[gi])

            if d0 is not None:
                @b.vector
                def _(v):
                    for gl, slots in enumerate(groups):
                        gi = g0 + gl
                        tk = None
                        for si, (k, h) in enumerate(slots):
                            if k >= d0:
                                sl = ex[:, (gi % 2) * 1536 + si * QT:(gi % 2) * 1536 + (si + 1) * QT]
                                tk = P_.V(v, "tensor_tensor", sl, sl,
                                          masks[:, (k - d0) * QT:(k - d0 + 1) * QT],
                                          ALU.mult, waits=(exp_raw[gi],))
                        if tk is not None:
                            assert tk == exp_fin[gi], (tk, exp_fin[gi])

            gbase[0] += Gn
            col = qi * QT
            et = []
            @b.vector
            def _(v):
                P_.wait(v, av_tok[gbase[0] - 1], guard)
                P_.V(v, "tensor_copy", avs[0:65, col:col + QT], av[0][0:65, :])
                et.append(P_.V(v, "tensor_copy", avs[0:65, GW + col:GW + col + QT],
                               av[1][0:65, :]))
            evac_tok = et[0]
        gw = len(grp) * QT
        rt = []
        @b.scalar
        def _(s):
            t1 = P_.S(s, "activation", rec[0:1, 0:gw], avs[64:65, 0:gw], AF.Ln,
                      waits=(evac_tok,))
            t2 = P_.S(s, "activation", rec[0:1, GW:GW + gw], avs[64:65, GW:GW + gw],
                      AF.Ln, waits=(t1,))
            t3 = P_.S(s, "activation", rec[0:1, 0:gw], rec[0:1, 0:gw], AF.Exp,
                      scale=-1.0, waits=(t2,))
            rt.append(P_.S(s, "activation", rec[0:1, GW:GW + gw], rec[0:1, GW:GW + gw],
                           AF.Exp, scale=-1.0, waits=(t3,)))
        peF = P_.cnt["pe"]
        items = [(h, qi) for h in (0, 1) for qi in range(len(grp))]
        bct = {}
        for i, (h, qi) in enumerate(items):
            bct[(h, qi)] = ("pe", peF + i + 1)
        dveF = P_.cnt["dve"]
        c1_tok = {}
        for i, (h, qi) in enumerate(items):
            c1_tok[(h, qi)] = ("dve", dveF + 2 * i + 1)
        @b.tensor
        def _(t):
            P_.wait(t, rt[0], P_.tok("dve"))
            for i, (h, qi) in enumerate(items):
                if qi > 0:
                    P_.wait(t, c1_tok[(h, qi - 1)])
                ins = t.matmul(av[h][0:64, :], onesf[:, 0:64],
                               rec[0:1, h * GW + qi * QT:h * GW + qi * QT + QT],
                               start=True, stop=True)
                tok = P_.inc(ins, "pe")
                assert tok == bct[(h, qi)]
        @b.vector
        def _(v):
            prev = None
            for h in (0, 1):
                for qi, qt_ in enumerate(grp):
                    w = [bct[(h, qi)]] + ([prev] if prev else [])
                    c1 = P_.V(v, "tensor_copy", bcs[0:64, 0:QT], av[h][0:64, :], waits=w)
                    assert c1 == c1_tok[(h, qi)]
                    prev = P_.V(
                        v, "tensor_tensor", out_fn(h, qt_["out0"]),
                        avs[0:64, h * GW + qi * QT:h * GW + qi * QT + QT],
                        bcs[0:64, 0:QT], ALU.mult, waits=(c1,))
            P_.B["_flushtok"] = prev
        flush_tok = P_.B["_flushtok"]
    return flush_tok


def seg_attn_l0(P_, dep_toks, mid_hook=None):
    b = P_.block
    cfg = P_.cfg
    TOK, QT, KB, NQT = cfg["TOK"], cfg["QT"], cfg["KB"], cfg["NQT"]
    gK, gV, qd = P_.D["gK"], P_.D.get("gV"), P_.D["q_dram"]
    gks = P_.D.get("gKs", 1024)
    gvf = P_.D.get("gVf", None)
    kpa, kpb, vp, qp = P_.B["kpa"], P_.B["kpb"], P_.B["vp"], P_.B["qp"]
    ln_at = P_.B["ln_at"]
    kpc = KB // 4

    prev_pair_tok = dep_toks[0] if dep_toks else None
    last = None
    ons = []
    depk = tuple(t for t in dep_toks if t is not None and t[0] != "ccAv")
    depv = tuple(t for t in dep_toks if t is not None and t[0] == "ccAv")
    @b.vector
    def _(v):
        ons.append(P_.V(v, "memset",
                        vp[:].rearrange("p (c s) -> p c s", s=130)[:, :, 64:65], 1.0,
                        waits=depk))
        ons.append(P_.V(v, "memset",
                        vp[:].rearrange("p (c s) -> p c s", s=130)[:, :, 129:130], 1.0))
    for p in range(8):
        ldt = {}
        @b.sync
        def _(s):
            P_.wait(s, *depk)
            if last is not None:
                P_.wait(s, last)
            else:
                P_.wait(s, ons[1])
            P_.dma(s, "a", qp[:, 0:TOK], qd[128 * p:128 * (p + 1), :])
            for rr in range(4):
                P_.dma(s, "a", kpa[0:64, TOK * rr:TOK * (rr + 1)],
                       gK[gks * rr + 128 * p:gks * rr + 128 * p + 64, :])
                P_.dma(s, "a", kpb[64:128, TOK * rr:TOK * (rr + 1)],
                       gK[gks * rr + 128 * p + 64:gks * rr + 128 * p + 128, :])
            ldt["k"] = P_.dtok("a")
            P_.wait(s, *depv)
            for rr in range(4):
                gv_rr = gvf(rr) if gvf is not None else gV[TOK * rr:TOK * (rr + 1), :]
                for h in (0, 1):
                    P_.dma(
                        s, "a",
                        vp[:].rearrange("p (c s) -> p c s", s=130)
                            [:, rr * kpc:(rr + 1) * kpc, 65 * h:65 * h + 64],
                        gv_rr[:, 128 * p + 64 * h:128 * p + 64 * h + 64]
                            .rearrange("(c q) e -> q c e", q=128))
                ldt[rr] = P_.dtok("a")

        def ldtok(k, kind, ldt=ldt):
            if kind == "sc":
                return ldt["k"]
            return ldt[min(k // kpc, 3)]

        qtiles = [dict(qcols=qt * QT, kmax=KB, diag0=None, out0=p * TOK + qt * QT)
                  for qt in range(NQT)]

        def out_fn(h, out0):
            base = 0 if h == 0 else 64
            return ln_at[base:base + 64, out0:out0 + QT]

        last = _attn_pair(P_, QP=qp, KPA=kpa, KPB=kpb, VP=vp, qtiles=qtiles,
                          out_fn=out_fn, ldtok_fn=ldtok, pre_tok=prev_pair_tok)
        prev_pair_tok = last
        if p == 0 and mid_hook is not None:
            prev_pair_tok = mid_hook(last)
        dep_toks = ()
    return last


def seg_attn_l1(P_, dep_toks, loader=None, mid_hook=None):
    b = P_.block
    cfg = P_.cfg
    TB, QT, NBQT = cfg["TB"], cfg["QT"], cfg["NBQT"]
    KB = cfg["KB"]
    kpa, kpb, vp, qp = P_.B["kpa"], P_.B["kpb"], P_.B["vp"], P_.B["qp"]
    ln_at = P_.B["ln_at"]

    last = None
    ons = []
    @b.vector
    def _(v):
        P_.V(v, "memset", vp[:].rearrange("p (c s) -> p c s", s=130)[:, :, 64:65], 1.0,
             waits=dep_toks)
        ons.append(P_.V(v, "memset",
                        vp[:].rearrange("p (c s) -> p c s", s=130)[:, :, 129:130], 1.0))
    for bp in range(2):
        if loader is not None:
            ldtok = loader(bp, dep_toks + (ons[-1],), last)
        else:
            @b.sync
            def _(s):
                P_.wait(s, *dep_toks)
                P_.wait(s, last if last is not None else ons[-1])
                P_.dma(s, "a", qp[:, 0:TB], P_.D["qprs"][128 * bp:128 * (bp + 1), :])
                P_.dma(s, "a", kpa[0:64, 0:TB], P_.D["kprs"][128 * bp:128 * bp + 64, :])
                P_.dma(s, "a", kpb[64:128, 0:TB], P_.D["kprs"][128 * bp + 64:128 * bp + 128, :])
                for h in (0, 1):
                    P_.dma(
                        s, "a",
                        vp[:].rearrange("p (c s) -> p c s", s=130)[:, :, 65 * h:65 * h + 64],
                        P_.D["vprs"][TB * bp:TB * (bp + 1), 64 * h:64 * h + 64]
                            .rearrange("(c q) e -> q c e", q=128))
            tk0 = P_.dtok("a")
            ldtok = lambda k, kind, tk0=tk0: tk0

        qtiles = [dict(qcols=qt * QT, kmax=4 * (qt + 1), diag0=4 * qt,
                       out0=bp * TB + qt * QT) for qt in range(NBQT)]

        def out_fn(h, out0):
            base = 0 if h == 0 else 64
            return ln_at[base:base + 64, out0:out0 + QT]

        last = _attn_pair(P_, QP=qp, KPA=kpa, KPB=kpb, VP=vp, qtiles=qtiles,
                          out_fn=out_fn, ldtok_fn=ldtok, pre_tok=None)
        if bp == 0 and mid_hook is not None:
            mid_hook(last)
        dep_toks = ()
    return last


def seg_ternary_abs(P_, w_dram_name, dep_toks, stage_off=0, po=0):
    b = P_.block
    wd = P_.D[w_dram_name]
    W = wd.shape[1]
    NCH = wd.shape[0] // 128
    tw, pr = P_.B["tw_sb"], P_.B["pr_sb"]
    so = stage_off
    CW = min(W, 512)
    subs = [(r, c0) for r in range(NCH) for c0 in range(0, W, CW)]
    trash = P_.B["tabs_tr"]
    ptoks = []
    for i, (r, c0) in enumerate(subs):
        lt = []
        @b.gpsimd
        def _(g):
            if i >= 2:
                P_.wait(g, ptoks[i - 2])
            P_.wait(g, *dep_toks)
            lt.append(P_.dma(g, "tg", tw[:, so + (i % 2) * CW:so + (i % 2) * CW + CW],
                             wd[128 * r:128 * (r + 1), c0:c0 + CW]))
        @b.scalar
        def _(s):
            ptoks.append(P_.S(s, "activation", trash[:, 0:CW],
                              tw[:, so + (i % 2) * CW:so + (i % 2) * CW + CW],
                              AF.Abs, accum_out=pr[:, po + i:po + i + 1],
                              waits=(lt[0],)))
    return dict(ptok=ptoks[-1], subs=subs, NCH=NCH, W=W, CW=CW, so=so, po=po,
                wd=wd)


def seg_ternary_finish(P_, ctx, mm_ps, guard):
    b = P_.block
    pr = P_.B["pr_sb"]
    onesf32c = P_.B["c_ones_col_f"]
    onesf = P_.B["c_ones_row_f"]
    po, NS = ctx["po"], len(ctx["subs"])
    N = float(ctx["NCH"]) * 128.0 * ctx["W"]
    mm = []
    @b.tensor
    def _(t):
        P_.wait(t, ctx["ptok"], guard)
        ins = t.matmul(mm_ps[0:1, 0:NS], onesf32c[:, 0:1], pr[:, po:po + NS],
                       start=True, stop=True)
        mm.append(P_.inc(ins, "pe"))
    sc = []
    @b.vector
    def _(v):
        cm = P_.V(v, "reduce_sum", pr[0:1, po + 33:po + 34], mm_ps[0:1, 0:NS],
                  axis=mybir.AxisListType.X, waits=(mm[0],))
        t1 = P_.V(v, "tensor_scalar", pr[0:1, po + 34:po + 35], pr[0:1, po + 33:po + 34],
                  1.0 / N, 1e-8, ALU.mult, ALU.max, waits=(cm,))
        sc.append(P_.V(v, "reciprocal", pr[0:1, po + 35:po + 36],
                       pr[0:1, po + 34:po + 35], waits=(t1,)))
    bc = []
    @b.tensor
    def _(t):
        P_.wait(t, sc[0])
        ins = t.matmul(mm_ps[0:128, 64:65], onesf[:], pr[0:1, po + 35:po + 36],
                       start=True, stop=True)
        bc.append(P_.inc(ins, "pe"))
    rv = []
    @b.vector
    def _(v):
        rv.append(P_.V(v, "tensor_copy", pr[:, po + 36:po + 37], mm_ps[0:128, 64:65],
                       waits=(bc[0],)))
    return rv[0]


def seg_ternary_apply(P_, ctx, wt_dram_name, rv):
    b = P_.block
    wtd = P_.D[wt_dram_name]
    tw, twb, pr = P_.B["tw_sb"], P_.B["twb_sb"], P_.B["pr_sb"]
    so, po, CW = ctx["so"], ctx["po"], ctx["CW"]
    wd = ctx["wd"]
    subs = ctx["subs"]
    rvec = pr[:, po + 36:po + 37]
    stores = []
    lts = []

    @b.gpsimd
    def _(g):
        P_.wait(g, rv)
        for i in range(min(2, len(subs))):
            r, c0 = subs[i]
            lts.append(P_.dma(g, "tg", tw[:, so + (i % 2) * CW:so + (i % 2) * CW + CW],
                              wd[128 * r:128 * (r + 1), c0:c0 + CW]))
        for i, (r, c0) in enumerate(subs):
            sl = tw[:, so + (i % 2) * CW:so + (i % 2) * CW + CW]
            slb = twb[:, so + (i % 2) * CW:so + (i % 2) * CW + CW]
            t1 = P_.G(g, "tensor_scalar", sl, sl, rvec, MAGIC, ALU.mult, ALU.add,
                      waits=(lts[i],))
            t2 = P_.G(g, "tensor_scalar", sl, sl, MAGIC, 1.0, ALU.subtract, ALU.min,
                      waits=(t1,))
            w3 = (t2,) if i < 2 else (t2, stores[i - 2])
            t3 = P_.G(g, "tensor_scalar", slb, sl, -1.0, None, ALU.max, waits=w3)
            st4 = []
            for jj in range(CW // 128):
                rr0 = (c0 // 128 + jj) * 128
                st4.append(P_.dma(g, "tg", wtd[rr0:rr0 + 128, 128 * r:128 * (r + 1)],
                                  twb[:, so + (i % 2) * CW + 128 * jj:so + (i % 2) * CW + 128 * (jj + 1)]))
            stores.append(tuple(st4))
            if i + 2 < len(subs):
                r2, c02 = subs[i + 2]
                lts.append(P_.dma(g, "tg", tw[:, so + (i % 2) * CW:so + (i % 2) * CW + CW],
                                  wd[128 * r2:128 * (r2 + 1), c02:c02 + CW]))
    return P_.dtok("tg")


def seg_ffn1_mix(P_, qt, dep_toks, xoff=0):
    b = P_.block
    cfg = P_.cfg
    TOK, QT, DCH = cfg["TOK"], cfg["QT"], cfg["DCH"]
    ln = P_.B["ln_at"]
    xr, xk = P_.B["xr_sb"], P_.B["xk_sb"]
    bnd = P_.B["bnd_sb"]
    mu_r, mu_k = P_.B["mu_r"], P_.B["mu_k"]
    sc1 = P_.B["mx_sc"]
    q0 = qt * QT
    toks = []
    @b.vector
    def _(v):
        tk = None
        for c in range(DCH):
            a = ln[:, c * TOK + q0:c * TOK + q0 + QT]
            xrs = xr[:, xoff + c * QT:xoff + (c + 1) * QT]
            xks = xk[:, xoff + c * QT:xoff + (c + 1) * QT]
            if q0 > 0:
                ap = ln[:, c * TOK + q0 - 1:c * TOK + q0 + QT - 1]
                d = P_.V(v, "tensor_tensor", sc1[:, 0:QT], a, ap, ALU.subtract,
                         waits=tuple(dep_toks) + ((tk,) if tk else ()))
                m1 = P_.V(v, "tensor_scalar", sc1[:, QT:2 * QT], sc1[:, 0:QT],
                          mu_r[:, c:c + 1], None, ALU.mult, waits=(d,))
                a1 = P_.V(v, "tensor_tensor", xrs, sc1[:, QT:2 * QT],
                          ap, ALU.add, waits=(m1,))
                m2 = P_.V(v, "tensor_scalar", sc1[:, QT:2 * QT], sc1[:, 0:QT],
                          mu_k[:, c:c + 1], None, ALU.mult, waits=(a1,))
                tk = P_.V(v, "tensor_tensor", xks, sc1[:, QT:2 * QT],
                          ap, ALU.add, waits=(m2,))
            else:
                ap = ln[:, c * TOK:c * TOK + QT - 1]
                aa = ln[:, c * TOK + 1:c * TOK + QT]
                d = P_.V(v, "tensor_tensor", sc1[:, 1:QT], aa, ap, ALU.subtract,
                         waits=tuple(dep_toks) + ((tk,) if tk else ()))
                m1 = P_.V(v, "tensor_scalar", sc1[:, QT + 1:2 * QT], sc1[:, 1:QT],
                          mu_r[:, c:c + 1], None, ALU.mult, waits=(d,))
                a1 = P_.V(v, "tensor_tensor", xrs[:, 1:QT], sc1[:, QT + 1:2 * QT],
                          ap, ALU.add, waits=(m1,))
                m2 = P_.V(v, "tensor_scalar", sc1[:, QT + 1:2 * QT], sc1[:, 1:QT],
                          mu_k[:, c:c + 1], None, ALU.mult, waits=(a1,))
                tk = P_.V(v, "tensor_tensor", xks[:, 1:QT], sc1[:, QT + 1:2 * QT],
                          ap, ALU.add, waits=(m2,))
                a0 = ln[:, c * TOK:c * TOK + 1]
                d0 = P_.V(v, "tensor_tensor", sc1[:, 0:1], a0, bnd[:, c:c + 1],
                          ALU.subtract, waits=(tk,))
                n1 = P_.V(v, "tensor_scalar", sc1[:, 1:2], sc1[:, 0:1], mu_r[:, c:c + 1],
                          None, ALU.mult, waits=(d0,))
                a0 = P_.V(v, "tensor_tensor", xrs[:, 0:1], sc1[:, 1:2],
                          bnd[:, c:c + 1], ALU.add, waits=(n1,))
                n2 = P_.V(v, "tensor_scalar", sc1[:, 1:2], sc1[:, 0:1], mu_k[:, c:c + 1],
                          None, ALU.mult, waits=(a0,))
                tk = P_.V(v, "tensor_tensor", xks[:, 0:1], sc1[:, 1:2],
                          bnd[:, c:c + 1], ALU.add, waits=(n2,))
        toks.append(tk)
    return toks[0]


def seg_store_x(P_, dst_ap, dep_toks, src="x_sb"):
    b = P_.block
    TOK = P_.cfg["TOK"]

    tk = []
    @b.sync
    def _(s):
        P_.wait(s, *dep_toks)
        tk.append(P_.dma(s, "a", dst_ap.rearrange("(c p) t -> p c t", p=128),
                         P_.B[src][:].rearrange("p (c t) -> p c t", t=TOK)))
    return tk[0]


def seg_wo_ln(P_, w_name, bias_name, dep_toks, rs=None):
    cfg = P_.cfg
    TOK = cfg["TOK"]
    QW = 512
    tk = None
    for qp in range(TOK // QW):
        sl = rd = None
        if rs is not None:
            sl, sk, rd = rs.next()
        lt, _, mmo = seg_proj(P_, "ln_at", w_name, 8, bias=bias_name,
                              out_sbuf="ev_sb", out_sbuf_w=QW, qw=QW,
                              src_off=qp * QW, src_w=TOK,
                              dep_toks=dep_toks if qp == 0 else (tk,),
                              evac_dep=(tk,) if tk else (),
                              ring_slots=sl, ring_dep=rd)
        if rs is not None:
            rs.done(sk, mmo)
        tk = seg_postnorm_residual(P_, "ev_sb", QW, (lt,), (qp * QW, (qp + 1) * QW))
    return tk


def _alloc_common(P_, st):
    cfg = P_.cfg
    TOK, QT = cfg["TOK"], cfg["QT"]
    P_.sb("x_sb", [128, 8 * TOK], F32, stack=st)
    P_.sb("ln_at", [128, 8 * TOK], BF16, stack=st)
    P_.sb("st_sb", [1, 4 * QT], F32, stack=st)
    P_.sb("bc_sb", [128, 4 * QT], F32, stack=st)
    for stem in ("a", "wb", "t", "w", "tg"):
        for i in range(P.NDS):
            P_.sem(stem + str(i))
    for s in ("pe", "act", "dve", "gp"):
        P_.sem(s)


def _alloc_proj(P_, st, ring_ch=8, dstage=True, ngrp=2, ring_cols=None):
    cfg = P_.cfg
    TOK, QT = cfg["TOK"], cfg["QT"]
    P_.sb("sq_sb", [128, 16 * QT], BF16, stack=st)
    if ring_cols is None:
        ring_cols = ngrp * ring_ch * 128
    P_.sb("w_ring", [128, ring_cols], BF16, stack=st)
    P_.B["_ring_grpw"] = ring_ch * 128
    if dstage:
        P_.sb("dstage", [128, 2 * TOK], BF16, stack=st)


class RingSched:

    def __init__(self, P_, pairs):
        self.P = P_
        self.pairs = pairs
        self.last = [None] * len(pairs)
        self.i = 0

    def next(self):
        k = self.i % len(self.pairs)
        self.i += 1
        return self.pairs[k], k, self.last[k]

    def done(self, k, mmtoks):
        self.last[k] = mmtoks[-1]


def _load_consts_biases(P_, names, masks=True):
    nl = []
    for n in names:
        shp = list(P_.D[n].shape)
        nl.append((n, n, shp, F32))
    t1 = seg_consts(P_, nl)
    t2 = seg_castload(P_, "c_ones_col", "c_ones_col_bf", [128, 1])
    t3 = seg_castload(P_, "c_ident", "c_ident_bf", [128, 128])
    t4 = t3
    if masks:
        t4 = seg_castload(P_, "c_masks", "c_masks_bf", list(P_.D["c_masks"].shape))
    P_.B["c_ones_row_f"] = P_.B["c_ones_row"]
    P_.B["c_ones_col_f"] = P_.B["c_ones_col"]
    ep = []
    b = P_.block
    @b.vector
    def _(v):
        P_.sb("eps_sb", [1, 1], F32)
        ep.append(P_.V(v, "memset", P_.B["eps_sb"][:], 1e-5))
    return (t1, t2, t3, t4, ep[0])


def _din(name_shape):
    return name_shape[0] // 128


def build_launchA(cfg):
    from contextlib import ExitStack
    nc = bass.Bass("TRN2", target_bir_lowering=False, debug=False)
    TOK = cfg["TOK"]
    with ExitStack() as st:
        P_ = P(nc, cfg, st)
        for n, shp in [("xT", [1024, TOK]), ("w_q0", [1024, 1024]), ("w_k0", [1024, 1024]),
                       ("w_v0", [1024, 1024]), ("w_l0", [1024, 1024]),
                       ("b_q0", [128, 8]), ("b_k0", [128, 8]), ("b_v0", [128, 8]),
                       ("b_l0", [128, 8]), ("c_ident", [128, 128]), ("c_ones_col", [128, 1]),
                       ("c_ones_row", [1, 128]), ("c_masks", [128, 4 * cfg["QT"]])]:
            P_.D[n] = nc.dram_tensor(n, shp, F32, kind="ExternalInput").ap()
        qT = nc.dram_tensor("qT", [1024, TOK], BF16, kind="ExternalOutput").ap()
        kT = nc.dram_tensor("kT", [1024, TOK], BF16, kind="ExternalOutput").ap()
        vTok = nc.dram_tensor("vTok", [TOK, 1024], BF16, kind="ExternalOutput").ap()
        with nc.Block() as blk:
            P_.block = blk
            _alloc_common(P_, st)
            _alloc_proj(P_, st)
            P_.sb("kpre", [128, 8 * TOK], BF16)
            P_.sb("tstage", [128, TOK], BF16)
            P_.sb("tstage2", [128, 128], BF16)
            ct = _load_consts_biases(
                P_, ["b_q0", "b_k0", "b_v0", "b_l0", "c_ones_col", "c_ones_row"],
                masks=False)
            t0 = seg_load_x(P_, P_.D["xT"])
            from contextlib import ExitStack as ES2
            with ES2() as st_p1:
                P_.ps("pg0", [128, min(TOK, 2048)], stack=st_p1)
                P_.ps("pg1", [128, min(TOK, 2048)], stack=st_p1)
                tln = seg_prenorm(P_, "x_sb", (t0,) + ct)
                tq, _, _ = seg_proj(P_, "ln_at", "w_q0", 8, bias="b_q0", out_dram=qT,
                                    dep_toks=(tln,))
                tkp, _, mmk = seg_proj(P_, "ln_at", "w_k0", 8, bias="b_k0",
                                       out_sbuf="kpre", out_sbuf_w=TOK, dep_toks=(tln,))
                tk, _, mml = seg_proj(P_, "kpre", "w_l0", 8, bias="b_l0", out_dram=kT,
                                      dep_toks=(tkp,))
                tvp, _, _ = seg_proj(P_, "ln_at", "w_v0", 8, bias="b_v0",
                                     out_sbuf="kpre", out_sbuf_w=TOK, dep_toks=(tln,),
                                     evac_dep=(mml[-1],))
            with ES2() as st_p2:
                P_.ps("pg0", [128, min(TOK, 2048)], stack=st_p2)
                P_.ps("tr_ps", [128, 256], BF16, stack=st_p2)
                tv, _, _ = seg_proj(P_, "kpre", "w_l0", 8, bias="b_l0",
                                    transpose_dram=vTok, single_group=True,
                                    dep_toks=(tvp,), evac_dep=(tvp,))
            @blk.sync
            def _(s):
                for stem in ("a", "wb", "t"):
                    for i in range(P.NDS):
                        if P_.cnt[stem + str(i)]:
                            s.wait_ge(P_.sems[stem + str(i)], P_.cnt[stem + str(i)])
            @blk.gpsimd
            def _(g):
                for i in range(P.NDS):
                    if P_.cnt["w" + str(i)]:
                        g.wait_ge(P_.sems["w" + str(i)], P_.cnt["w" + str(i)])
    return nc


def build_launchB(cfg):
    from contextlib import ExitStack
    nc = bass.Bass("TRN2", target_bir_lowering=False, debug=False)
    TOK, TB, QT, KB = cfg["TOK"], cfg["TB"], cfg["QT"], cfg["KB"]
    with ExitStack() as st:
        P_ = P(nc, cfg, st)
        for n, shp in [("xT", [1024, TOK]), ("qT", None), ("gK", [4096, TOK]),
                       ("gV", [TB, 1024]),
                       ("w_o0", [1024, 1024]), ("b_o0", [128, 8]),
                       ("w_1T", [1024, 4096]), ("w_2T", [4096, 1024]),
                       ("b_1", [128, 32]), ("b_2", [128, 8]),
                       ("w_q1", [1024, 1024]), ("w_k1", [1024, 1024]), ("w_v1", [1024, 1024]),
                       ("c_ident", [128, 128]), ("c_ones_col", [128, 1]),
                       ("c_ones_row", [1, 128]), ("c_masks", [128, 4 * QT])]:
            if n in ("qT", "gK", "gV"):
                shp = [1024, TOK] if n == "qT" else shp
                P_.D[n] = nc.dram_tensor(n, shp, BF16, kind="ExternalInput").ap()
            else:
                P_.D[n] = nc.dram_tensor(n, shp, F32, kind="ExternalInput").ap()
        P_.D["q_dram"] = P_.D["qT"]
        x2T = nc.dram_tensor("x2T", [1024, TOK], F32, kind="ExternalOutput").ap()
        qT2 = nc.dram_tensor("qT2", [1024, TOK], BF16, kind="ExternalOutput").ap()
        kT2 = nc.dram_tensor("kT2", [1024, TOK], BF16, kind="ExternalOutput").ap()
        vTok2 = nc.dram_tensor("vTok2", [TOK, 1024], BF16, kind="ExternalOutput").ap()
        P_.D["w1t"] = nc.dram_tensor("w1t", [1024, 4096], BF16).ap()
        P_.D["w2t"] = nc.dram_tensor("w2t", [4096, 1024], BF16).ap()
        with nc.Block() as blk:
            P_.block = blk
            _alloc_common(P_, st)
            ct = _load_consts_biases(
                P_, ["b_o0", "b_1", "b_2", "c_ones_col", "c_ones_row"])
            t0 = seg_load_x(P_, P_.D["xT"])
            from contextlib import ExitStack as ES2
            with ES2() as st_t:
                P_.sb("tw_sb", [128, 4 * 512], F32, stack=st_t)
                P_.sb("twb_sb", [128, 4 * 512], BF16, stack=st_t)
                P_.sb("pr_sb", [128, 160], F32, stack=st_t)
                tctx1 = seg_ternary_abs(P_, "w_1T", ct, stage_off=0, po=0)
                tctx2 = seg_ternary_abs(P_, "w_2T", ct, stage_off=2 * 512, po=80)
                tern_stores = []

                def tern_hook(last):
                    rv1 = seg_ternary_finish(P_, tctx1, P_.B["avA"], last)
                    rv2 = seg_ternary_finish(P_, tctx2, P_.B["avB"], rv1)
                    tern_stores.append(seg_ternary_apply(P_, tctx1, "w1t", rv1))
                    tern_stores.append(seg_ternary_apply(P_, tctx2, "w2t", rv2))
                    return rv2

                with ES2() as st_a:
                    P_.sb("kpa", [128, KB * 128], BF16, stack=st_a)
                    P_.sb("kpb", [128, KB * 128], BF16, stack=st_a)
                    P_.sb("vp", [128, KB * 130], BF16, stack=st_a)
                    P_.sb("qp", [128, TOK], BF16, stack=st_a)
                    P_.sb("av_sb", [128, 2 * 2 * QT], F32, stack=st_a)
                    P_.B["rec_sb"] = P_.B["st_sb"]
                    P_.sb("ex_sb", [128, 2 * 1536], BF16, stack=st_a)
                    P_.ps("sc0", [128, 1536], stack=st_a)
                    P_.ps("sc1", [128, 1536], stack=st_a)
                    P_.ps("avA", [128, QT], stack=st_a)
                    P_.ps("avB", [128, QT], stack=st_a)
                    zt = []
                    @blk.vector
                    def _(v):
                        zt.append(P_.V(v, "memset", P_.B["kpa"][64:128, :], 0.0,
                                       waits=(P_.dtok("a"), P_.dtok("wb"), P_.tok("pe"))))
                        zt.append(P_.V(v, "memset", P_.B["kpb"][0:64, :], 0.0))
                    ta = seg_attn_l0(P_, (t0, zt[1], ct[0]), mid_hook=tern_hook)
            with ES2() as st_w:
                _alloc_proj(P_, st_w, ring_ch=32)
                P_.sb("ev_sb", [128, 8 * 512], F32, stack=st_w)
                P_.ps("pg0", [128, min(TOK, 2048)], stack=st_w)
                P_.ps("pg1", [128, min(TOK, 2048)], stack=st_w)
                tx1 = seg_wo_ln(P_, "w_o0", "b_o0", (ta, t0, P_.dtok("t"), P_.tok("gp")))
                tln2 = seg_prenorm(P_, "x_sb", (tx1,))
                tk = tln2
                for qt in range(cfg["NQT"]):
                    P_.sb("h_sb", [128, 32 * QT], BF16, stack=st_w) if qt == 0 else None
                    th, _, _ = seg_proj(P_, "ln_at", "w1t", 32, bias="b_1", act=AF.Gelu,
                                        out_sbuf="h_sb", out_sbuf_w=QT, qw=QT,
                                        src_off=qt * QT, src_w=TOK, w_is_bf16=True,
                                        dep_toks=(tk,) + tuple(tern_stores))
                    to, _, _ = seg_proj(P_, "h_sb", "w2t", 8, bias="b_2",
                                        out_sbuf="ev_sb", out_sbuf_w=QT, qw=QT,
                                        src_w=QT, w_is_bf16=True, dep_toks=(th,),
                                        evac_dep=(tk,))
                    tk = seg_postnorm_residual(P_, "ev_sb", QT, (to,),
                                               (qt * QT, (qt + 1) * QT))
                tln3 = seg_prenorm(P_, "x_sb", (tk,))
                tx2 = seg_store_x(P_, x2T, (tk,))
                tq2, _, _ = seg_proj(P_, "ln_at", "w_q1", 8, out_dram=qT2, dep_toks=(tln3,))
                tk2, _, mmk2 = seg_proj(P_, "ln_at", "w_k1", 8, out_dram=kT2, dep_toks=(tln3,))
            with ES2() as st_v:
                P_.sb("w_ring", [128, 2 * 8 * 128], BF16, stack=st_v)
                P_.sb("tstage", [128, TOK], BF16, stack=st_v)
                P_.sb("tstage2", [128, 128], BF16, stack=st_v)
                P_.ps("pg0", [128, min(TOK, 2048)], stack=st_v)
                P_.ps("tr_ps", [128, 256], BF16, stack=st_v)
                tv2, _, _ = seg_proj(P_, "ln_at", "w_v1", 8, transpose_dram=vTok2,
                                     single_group=True, dep_toks=(tln3,))
            @blk.sync
            def _(s):
                for stem in ("a", "wb", "t"):
                    for i in range(P.NDS):
                        if P_.cnt[stem + str(i)]:
                            s.wait_ge(P_.sems[stem + str(i)], P_.cnt[stem + str(i)])
            @blk.gpsimd
            def _(g):
                for i in range(P.NDS):
                    if P_.cnt["w" + str(i)]:
                        g.wait_ge(P_.sems["w" + str(i)], P_.cnt["w" + str(i)])
    return nc


def build_launchC1(cfg):
    from contextlib import ExitStack
    nc = bass.Bass("TRN2", target_bir_lowering=False, debug=False)
    TOK, TB, QT, KB = cfg["TOK"], cfg["TB"], cfg["QT"], cfg["KB"]
    with ExitStack() as st:
        P_ = P(nc, cfg, st)
        P_.D["qprs"] = nc.dram_tensor("qprs", [256, TB], BF16, kind="ExternalInput").ap()
        P_.D["kprs"] = nc.dram_tensor("kprs", [256, TB], BF16, kind="ExternalInput").ap()
        P_.D["vprs"] = nc.dram_tensor("vprs", [2 * TB, 128], BF16, kind="ExternalInput").ap()
        for n, shp in [("c_ident", [128, 128]), ("c_ones_col", [128, 1]),
                       ("c_ones_row", [1, 128]), ("c_masks", [128, 4 * QT])]:
            P_.D[n] = nc.dram_tensor(n, shp, F32, kind="ExternalInput").ap()
        a4 = nc.dram_tensor("attn4T", [256, TB], BF16, kind="ExternalOutput").ap()
        with nc.Block() as blk:
            P_.block = blk
            TOKSAVE = cfg["TOK"]
            _alloc_common_small(P_, st, cfg)
            ct = _load_consts_biases(P_, ["c_ones_col", "c_ones_row"])
            P_.sb("kpa", [128, KB * 128], BF16)
            P_.sb("kpb", [128, KB * 128], BF16)
            P_.sb("vp", [128, KB * 130], BF16)
            P_.sb("qp", [128, TB], BF16)
            P_.sb("av_sb", [128, 2 * 2 * QT], F32)
            P_.sb("rec_sb", [1, 2 * 2 * QT], F32)
            P_.sb("ex_sb", [128, 2 * 1536], BF16)
            P_.sb("ln_at", [128, 2 * TB], BF16)
            P_.ps("sc0", [128, 1536])
            P_.ps("sc1", [128, 1536])
            P_.ps("avA", [128, QT])
            P_.ps("avB", [128, QT])
            zt = []
            @blk.vector
            def _(v):
                zt.append(P_.V(v, "memset", P_.B["kpa"][64:128, :], 0.0))
                zt.append(P_.V(v, "memset", P_.B["kpb"][0:64, :], 0.0))
            ta = seg_attn_l1(P_, (zt[1],) + ct)
            for bp in range(2):
                @blk.sync
                def _(s):
                    P_.wait(s, ta)
                    P_.dma(s, "a", a4[128 * bp:128 * (bp + 1), :],
                           P_.B["ln_at"][:, bp * TB:(bp + 1) * TB])
            @blk.sync
            def _(s):
                for stem in ("a", "wb", "t"):
                    for i in range(P.NDS):
                        if P_.cnt[stem + str(i)]:
                            s.wait_ge(P_.sems[stem + str(i)], P_.cnt[stem + str(i)])
            @blk.gpsimd
            def _(g):
                for i in range(P.NDS):
                    if P_.cnt["w" + str(i)]:
                        g.wait_ge(P_.sems["w" + str(i)], P_.cnt["w" + str(i)])
    return nc


def _alloc_common_small(P_, st, cfg):
    QT = cfg["QT"]
    P_.sb("bc_sb", [128, 4 * QT], F32, stack=st)
    for stem in ("a", "wb", "t", "w"):
        for i in range(P.NDS):
            P_.sem(stem + str(i))
    for s in ("pe", "act", "dve", "gp"):
        P_.sem(s)


def build_launchC2(cfg):
    from contextlib import ExitStack
    nc = bass.Bass("TRN2", target_bir_lowering=False, debug=False)
    TOK, QT = cfg["TOK"], cfg["QT"]
    with ExitStack() as st:
        P_ = P(nc, cfg, st)
        P_.D["xT"] = nc.dram_tensor("xT", [1024, TOK], F32, kind="ExternalInput").ap()
        P_.D["attnT"] = nc.dram_tensor("attnT", [1024, TOK], BF16, kind="ExternalInput").ap()
        for n, shp in [("w_o1", [1024, 1024]), ("b_o1", [128, 8]),
                       ("c_ident", [128, 128]), ("c_ones_col", [128, 1]),
                       ("c_ones_row", [1, 128]), ("c_masks", [128, 4 * QT])]:
            P_.D[n] = nc.dram_tensor(n, shp, F32, kind="ExternalInput").ap()
        x3T = nc.dram_tensor("x3T", [1024, TOK], F32, kind="ExternalOutput").ap()
        ln4T = nc.dram_tensor("ln4T", [1024, TOK], BF16, kind="ExternalOutput").ap()
        with nc.Block() as blk:
            P_.block = blk
            _alloc_common(P_, st)
            _alloc_proj(P_, st)
            P_.sb("ev_sb", [128, 8 * 512], F32)
            P_.ps("pg0", [128, min(TOK, 2048)])
            P_.ps("pg1", [128, min(TOK, 2048)])
            ct = _load_consts_biases(P_, ["b_o1", "c_ones_col", "c_ones_row"],
                                     masks=False)
            t0 = seg_load_x(P_, P_.D["xT"])
            tat = []
            @blk.sync
            def _(s):
                tat.append(P_.dma(s, "a",
                                  P_.B["ln_at"][:].rearrange("p (c t) -> p c t", t=TOK),
                                  P_.D["attnT"].rearrange("(c p) t -> p c t", p=128)))
            tx3 = seg_wo_ln(P_, "w_o1", "b_o1", (t0, tat[0]) + ct)
            ts3 = seg_store_x(P_, x3T, (tx3,))
            tln4 = seg_prenorm(P_, "x_sb", (tx3,))
            @blk.sync
            def _(s):
                P_.wait(s, tln4)
                P_.dma(s, "a", ln4T.rearrange("(c p) t -> p c t", p=128),
                       P_.B["ln_at"][:].rearrange("p (c t) -> p c t", t=TOK))
            @blk.sync
            def _(s):
                for stem in ("a", "wb", "t"):
                    for i in range(P.NDS):
                        if P_.cnt[stem + str(i)]:
                            s.wait_ge(P_.sems[stem + str(i)], P_.cnt[stem + str(i)])
            @blk.gpsimd
            def _(g):
                for i in range(P.NDS):
                    if P_.cnt["w" + str(i)]:
                        g.wait_ge(P_.sems["w" + str(i)], P_.cnt["w" + str(i)])
    return nc


def build_launchD(cfg):
    from contextlib import ExitStack
    nc = bass.Bass("TRN2", target_bir_lowering=False, debug=False)
    TOK, QT = cfg["TOK"], cfg["QT"]
    with ExitStack() as st:
        P_ = P(nc, cfg, st)
        P_.D["xT"] = nc.dram_tensor("xT", [1024, TOK], F32, kind="ExternalInput").ap()
        P_.D["ln4T"] = nc.dram_tensor("ln4T", [1024, TOK], BF16, kind="ExternalInput").ap()
        P_.D["bnd"] = nc.dram_tensor("bnd", [128, 8], BF16, kind="ExternalInput").ap()
        for n, shp in [("w_r1T", [1024, 1024]), ("w_k1T", [1024, 1024]),
                       ("w_v1T", [1024, 1024]), ("mu_r", [128, 8]), ("mu_k", [128, 8]),
                       ("c_ident", [128, 128]), ("c_ones_col", [128, 1]),
                       ("c_ones_row", [1, 128]), ("c_masks", [128, 4 * QT])]:
            P_.D[n] = nc.dram_tensor(n, shp, F32, kind="ExternalInput").ap()
        outT = nc.dram_tensor("outT", [1024, TOK], F32, kind="ExternalOutput").ap()
        with nc.Block() as blk:
            P_.block = blk
            _alloc_common(P_, st)
            _alloc_proj(P_, st, dstage=False)
            P_.sb("ev_sb", [128, 8 * QT], F32)
            P_.sb("xr_sb", [128, 8 * QT], BF16)
            P_.sb("xk_sb", [128, 8 * QT], BF16)
            P_.sb("r_sb", [128, 8 * QT], F32)
            P_.sb("ka_sb", [128, 8 * QT], BF16)
            P_.sb("tstage", [128, QT], BF16)
            P_.ps("pg0", [128, min(TOK, 2048)])
            P_.ps("pg1", [128, min(TOK, 2048)])
            ct = _load_consts_biases(P_, ["mu_r", "mu_k", "c_ones_col", "c_ones_row"],
                                          masks=False)
            t0 = seg_load_x(P_, P_.D["xT"])
            lt = []
            @blk.sync
            def _(s):
                lt.append(P_.dma(s, "a",
                                 P_.B["ln_at"][:].rearrange("p (c t) -> p c t", t=TOK),
                                 P_.D["ln4T"].rearrange("(c p) t -> p c t", p=128)))
            bl = []
            @blk.sync
            def _(s):
                P_.sb("bnd_sb", [128, 8], BF16)
                bl.append(P_.dma(s, "a", P_.B["bnd_sb"][:], P_.D["bnd"]))
            tk = None
            for qt in range(cfg["NQT"]):
                tm = seg_ffn1_mix(P_, qt, (lt[0], bl[0]) + ct + ((tk,) if tk else ()))
                tr, _, _ = seg_proj(P_, "xr_sb", "w_r1T", 8, act=AF.Sigmoid,
                                    out_sbuf="r_sb", out_sbuf_w=QT, qw=QT, src_w=QT,
                                    dep_toks=(tm,))
                tka, _, _ = seg_proj(P_, "xk_sb", "w_k1T", 8, evac_relu_sq=True,
                                     out_sbuf="ka_sb", out_sbuf_w=QT, qw=QT, src_w=QT,
                                     dep_toks=(tm,), evac_dep=(tk,) if tk else ())
                to, _, _ = seg_proj(P_, "ka_sb", "w_v1T", 8, evac_mult=("r_sb", QT),
                                    out_sbuf="ev_sb", out_sbuf_w=QT, qw=QT, src_w=QT,
                                    dep_toks=(tka,), evac_dep=((tk,) if tk else ()) + (tr,))
                tk = seg_postnorm_residual(P_, "ev_sb", QT, (to,),
                                           (qt * QT, (qt + 1) * QT))
            tso = seg_store_x(P_, outT, (tk,))
            @blk.sync
            def _(s):
                for stem in ("a", "wb", "t"):
                    for i in range(P.NDS):
                        if P_.cnt[stem + str(i)]:
                            s.wait_ge(P_.sems[stem + str(i)], P_.cnt[stem + str(i)])
            @blk.gpsimd
            def _(g):
                for i in range(P.NDS):
                    if P_.cnt["w" + str(i)]:
                        g.wait_ge(P_.sems["w" + str(i)], P_.cnt["w" + str(i)])
    return nc


_NC_CACHE = {}
_EXEC = None


def _run(key, build_fn, cfg, in_maps):
    nc = _NC_CACHE.get(key)
    if nc is None:
        nc = build_fn(cfg)
        _NC_CACHE[key] = nc
    if _EXEC is not None:
        return _EXEC(nc, in_maps)
    res = run_bass_kernel_spmd(nc, in_maps, core_ids=list(range(8)))
    return res.results


def _pairs_of(core):
    return (core % 4, 4 + core % 4)


def _fused_idx(cfg, c):
    TOK, TB, KB = cfg["TOK"], cfg["TB"], cfg["KB"]
    VG = TOK // 1024
    GPB = TOK // 64
    p0 = c % 4
    kq = np.zeros((128, 24), dtype=np.uint32)
    for bp in range(2):
        pair = p0 + 4 * bp
        for rr in range(4):
            kq[:, bp * 12 + rr] = 1024 * rr + 128 * pair + np.arange(128)
            kq[0:64, bp * 12 + 4 + rr] = 1024 * rr + 128 * pair + np.arange(64)
            kq[64:128, bp * 12 + 4 + rr] = kq[0:64, bp * 12 + 4 + rr]
            kq[0:64, bp * 12 + 8 + rr] = 1024 * rr + 128 * pair + 64 + np.arange(64)
            kq[64:128, bp * 12 + 8 + rr] = kq[0:64, bp * 12 + 8 + rr]
    vv = np.zeros((128, 256), dtype=np.uint32)
    for bp in range(2):
        pair = p0 + 4 * bp
        for ka in range(KB):
            t = 128 * ka + np.arange(128)
            rr = t // TOK
            tl = t % TOK
            row = 1024 * rr + tl // VG
            for h in (0, 1):
                vv[:, bp * 128 + 2 * ka + h] = GPB * row + 16 * (tl % VG) + 2 * pair + h
    cc = np.zeros((128, 8), dtype=np.uint32)
    for ch in range(8):
        row = 128 * (ch % 4) + np.arange(128)
        cc[:, ch] = 4 * row + (c % 4)
    bi = np.zeros((128, 8), dtype=np.uint32)
    if c % 4 != 0:
        for ccx in range(8):
            bi[:, ccx] = 1024 * (c - 1) + 128 * ccx + np.arange(128)
    bv = np.zeros((128, 8), dtype=np.float32) if c % 4 == 0 else np.ones((128, 8), dtype=np.float32)
    return dict(l1kq_idx=kq, l1v_idx=vv, cc_idx=cc, bnd_idx=bi, bnd_valid=bv)


def kernel(**inputs):
    cfg = CFG
    TOK, TB = cfg["TOK"], cfg["TB"]
    W = _prep_weights(inputs)
    C = _host_consts(cfg)
    x = np.asarray(inputs["L0_attn_x"], dtype=np.float32)
    xTs = [_slice_x(x, c, cfg) for c in range(8)]

    if FUSED:
        wnames = ["w_q0", "w_k0", "w_v0", "w_l0", "w_o0", "b_q0", "b_k0", "b_v0",
                  "b_l0", "b_o0", "w_1T", "w_2T", "b_1", "b_2", "w_q1", "w_k1",
                  "w_v1", "w_o1", "b_o1", "w_r1T", "w_k1T", "w_v1T", "mu_r", "mu_k"]
        inF = []
        for c in range(8):
            d = {k: W[k] for k in wnames}
            d.update({k: C[k] for k in ("c_ident", "c_ones_col", "c_ones_row", "c_masks")})
            d["xT"] = xTs[c]
            d.update(_fused_idx(cfg, c))
            inF.append(d)
        rF = _run("F", build_fused, cfg, inF)
        out = np.empty((2, 4 * TOK, 1024), dtype=np.float32)
        for c in range(8):
            b, r = c // 4, c % 4
            out[b, TOK * r:TOK * (r + 1), :] = np.asarray(rF[c]["outT"], dtype=np.float32).T
        return out

    def m(names, extra):
        base = {k: W[k] for k in names if k in W}
        base.update({k: C[k] for k in ("c_ident", "c_ones_col", "c_ones_row", "c_masks")})
        base.update(extra)
        return base

    inA = [m(["w_q0", "w_k0", "w_v0", "w_l0", "b_q0", "b_k0", "b_v0", "b_l0"],
             {"xT": xTs[c]}) for c in range(8)]
    rA = _run("A", build_launchA, cfg, inA)

    gK = [np.concatenate([np.asarray(rA[4 * b + rr]["kT"]) for rr in range(4)], axis=0)
          for b in range(2)]
    gV = [np.concatenate([np.asarray(rA[4 * b + rr]["vTok"]) for rr in range(4)], axis=0)
          for b in range(2)]

    inB = [m(["w_o0", "b_o0", "w_1T", "w_2T", "b_1", "b_2", "w_q1", "w_k1", "w_v1"],
             {"xT": xTs[c], "qT": np.asarray(rA[c]["qT"]),
              "gK": gK[c // 4], "gV": gV[c // 4]}) for c in range(8)]
    rB = _run("B", build_launchB, cfg, inB)

    Kb = [np.concatenate([np.asarray(rB[4 * b + rr]["kT2"]) for rr in range(4)], axis=1)
          for b in range(2)]
    Qb = [np.concatenate([np.asarray(rB[4 * b + rr]["qT2"]) for rr in range(4)], axis=1)
          for b in range(2)]
    Vb = [np.concatenate([np.asarray(rB[4 * b + rr]["vTok2"]) for rr in range(4)], axis=0)
          for b in range(2)]
    inC1 = []
    for c in range(8):
        b = c // 4
        p0, p1 = _pairs_of(c)
        inC1.append(m([], {
            "qprs": np.ascontiguousarray(
                np.concatenate([Qb[b][128 * p0:128 * p0 + 128],
                                Qb[b][128 * p1:128 * p1 + 128]], axis=0)),
            "kprs": np.ascontiguousarray(
                np.concatenate([Kb[b][128 * p0:128 * p0 + 128],
                                Kb[b][128 * p1:128 * p1 + 128]], axis=0)),
            "vprs": np.ascontiguousarray(
                np.concatenate([Vb[b][:, 128 * p0:128 * p0 + 128],
                                Vb[b][:, 128 * p1:128 * p1 + 128]], axis=0)),
        }))
    rC1 = _run("C1", build_launchC1, cfg, inC1)

    attn_b = []
    for b in range(2):
        rows = []
        for p in range(8):
            core = 4 * b + p % 4
            bp = p // 4
            rows.append(np.asarray(rC1[core]["attn4T"])[128 * bp:128 * (bp + 1), :])
        attn_b.append(np.concatenate(rows, axis=0))
    inC2 = [m(["w_o1", "b_o1"],
              {"xT": np.asarray(rB[c]["x2T"]),
               "attnT": np.ascontiguousarray(attn_b[c // 4][:, TOK * (c % 4):TOK * (c % 4 + 1)])})
            for c in range(8)]
    rC2 = _run("C2", build_launchC2, cfg, inC2)

    inD = []
    for c in range(8):
        if c % 4 == 0:
            bnd = np.zeros((128, 8), dtype=_BF)
        else:
            col = np.asarray(rC2[c - 1]["ln4T"])[:, TOK - 1]
            bnd = np.ascontiguousarray(col.reshape(8, 128).T)
        inD.append(m(["w_r1T", "w_k1T", "w_v1T", "mu_r", "mu_k"],
                     {"xT": np.asarray(rC2[c]["x3T"]),
                      "ln4T": np.asarray(rC2[c]["ln4T"]), "bnd": bnd}))
    rD = _run("D", build_launchD, cfg, inD)

    out = np.empty((2, 4 * TOK, 1024), dtype=np.float32)
    for c in range(8):
        b, r = c // 4, c % 4
        out[b, TOK * r:TOK * (r + 1), :] = np.asarray(rD[c]["outT"], dtype=np.float32).T
    return out


def build_fused(cfg):
    from contextlib import ExitStack, ExitStack as ES2
    nc = bass.Bass("TRN2", target_bir_lowering=False, debug=False)
    TOK, TB, QT, KB = cfg["TOK"], cfg["TB"], cfg["QT"], cfg["KB"]
    U32 = mybir.dt.uint32
    with ExitStack() as st:
        P_ = P(nc, cfg, st)
        for n, shp in [("xT", [1024, TOK]),
                       ("w_q0", [1024, 1024]), ("w_k0", [1024, 1024]),
                       ("w_v0", [1024, 1024]), ("w_l0", [1024, 1024]), ("w_o0", [1024, 1024]),
                       ("b_q0", [128, 8]), ("b_k0", [128, 8]), ("b_v0", [128, 8]),
                       ("b_l0", [128, 8]), ("b_o0", [128, 8]),
                       ("w_1T", [1024, 4096]), ("w_2T", [4096, 1024]),
                       ("b_1", [128, 32]), ("b_2", [128, 8]),
                       ("w_q1", [1024, 1024]), ("w_k1", [1024, 1024]), ("w_v1", [1024, 1024]),
                       ("w_o1", [1024, 1024]), ("b_o1", [128, 8]),
                       ("w_r1T", [1024, 1024]), ("w_k1T", [1024, 1024]), ("w_v1T", [1024, 1024]),
                       ("mu_r", [128, 8]), ("mu_k", [128, 8]), ("bnd_valid", [128, 8]),
                       ("c_ident", [128, 128]), ("c_ones_col", [128, 1]),
                       ("c_ones_row", [1, 128]), ("c_masks", [128, 4 * QT])]:
            P_.D[n] = nc.dram_tensor(n, shp, F32, kind="ExternalInput").ap()
        for n, shp in [("l1kq_idx", [128, 24]), ("l1v_idx", [128, 256]),
                       ("cc_idx", [128, 8]), ("bnd_idx", [128, 8])]:
            P_.D[n] = nc.dram_tensor(n, shp, U32, kind="ExternalInput").ap()
        outT = nc.dram_tensor("outT", [1024, TOK], F32, kind="ExternalOutput").ap()
        P_.D["q_dram"] = nc.dram_tensor("qdr", [1024, TOK], BF16).ap()
        VG = TOK // 1024
        ccA_in = nc.dram_tensor("ccA_in", [1024, TOK], BF16).ap()
        ccA_out = nc.dram_tensor("ccA_out", [4096, TOK], BF16).ap()
        ccAv_in = nc.dram_tensor("ccAv_in", [1024, TOK], BF16).ap()
        ccAv_out = nc.dram_tensor("ccAv_out", [4096, TOK], BF16).ap()
        ccBq_in = nc.dram_tensor("ccBq_in", [1024, TOK], BF16).ap()
        ccBq_out = nc.dram_tensor("ccBq_out", [4096, TOK], BF16).ap()
        ccBk_in = nc.dram_tensor("ccBk_in", [1024, TOK], BF16).ap()
        ccBk_out = nc.dram_tensor("ccBk_out", [4096, TOK], BF16).ap()
        ccBv_in = nc.dram_tensor("ccBv_in", [1024, TOK], BF16).ap()
        ccBv_out = nc.dram_tensor("ccBv_out", [4096, TOK], BF16).ap()
        ccC_in = [nc.dram_tensor(f"ccC{j}_in", [128, TB], BF16).ap() for j in (0, 1)]
        ccC_out = [nc.dram_tensor(f"ccC{j}_out", [512, TB], BF16).ap() for j in (0, 1)]
        ccD_in = nc.dram_tensor("ccD_in", [8, 128], BF16).ap()
        ccD_out = nc.dram_tensor("ccD_out", [64, 128], BF16).ap()
        P_.D["w1t"] = nc.dram_tensor("w1t", [4096, 1024], BF16).ap()
        P_.D["w2t"] = nc.dram_tensor("w2t", [1024, 4096], BF16).ap()
        P_.D["gK"] = ccA_out
        P_.D["gKs"] = 1024
        P_.D["gVf"] = lambda rr: ccAv_out[1024 * rr:1024 * (rr + 1), :] \
            .rearrange("r (g d) -> (r g) d", g=VG)

        with nc.Block() as blk:
            P_.block = blk
            _alloc_common(P_, st)
            for s in ("ccA", "ccAv", "ccB", "ccBq", "ccBk", "ccC", "ccC1", "ccD"):
                P_.sem(s)
            ct = _load_consts_biases(
                P_, ["b_q0", "b_k0", "b_v0", "b_l0", "b_o0", "b_1", "b_2", "b_o1",
                     "mu_r", "mu_k", "bnd_valid", "c_ones_col", "c_ones_row"],
                masks=False)
            ix = seg_consts(P_, [("l1kq_idx", "l1kq_idx", [128, 24], U32),
                                 ("l1v_idx", "l1v_idx", [128, 256], U32),
                                 ("cc_idx", "cc_idx", [128, 8], U32),
                                 ("bnd_idx", "bnd_idx", [128, 8], U32)])
            t0 = seg_load_x(P_, P_.D["xT"])
            with ES2() as st_pre:
                _alloc_proj(P_, st_pre, ngrp=4)
                rs = RingSched(P_, [(0, 1024), (2048, 3072)])
                P_.sb("kpre", [128, 8 * TOK], BF16, stack=st_pre)
                with ES2() as st_p1:
                    P_.ps("pg0", [128, min(TOK, 2048)], stack=st_p1)
                    P_.ps("pg1", [128, min(TOK, 2048)], stack=st_p1)
                    tln = seg_prenorm(P_, "x_sb", (t0,) + ct + (ix,))
                    sl, sk, rd = rs.next()
                    tkp, _, mmk = seg_proj(P_, "ln_at", "w_k0", 8, bias="b_k0",
                                           out_sbuf="kpre", out_sbuf_w=TOK, dep_toks=(tln,),
                                           ring_slots=sl, ring_dep=rd)
                    rs.done(sk, mmk)
                    sl, sk, rd = rs.next()
                    tk, _, mml = seg_proj(P_, "kpre", "w_l0", 8, bias="b_l0",
                                          out_dram=ccA_in, dep_toks=(tkp,),
                                          ring_slots=sl, ring_dep=rd)
                    rs.done(sk, mml)
                    @blk.gpsimd
                    def _(g):
                        P_.wait(g, tk, P_.dtok("a"))
                        insg = g.collective_compute(
                            "AllGather", ALU.bypass,
                            replica_groups=[[0, 1, 2, 3], [4, 5, 6, 7]],
                            ins=[ccA_in.opt()], outs=[ccA_out.opt()])
                        insg.then_inc(P_.sems["ccA"])
                        P_.cnt["ccA"] = 1
                    sl, sk, rd = rs.next()
                    tq, _, mmq = seg_proj(P_, "ln_at", "w_q0", 8, bias="b_q0",
                                          out_dram=P_.D["q_dram"], dep_toks=(tln,),
                                          ring_slots=sl, ring_dep=rd)
                    rs.done(sk, mmq)
                    sl, sk, rd = rs.next()
                    tvp, _, mmv = seg_proj(P_, "ln_at", "w_v0", 8, bias="b_v0",
                                           out_sbuf="kpre", out_sbuf_w=TOK, dep_toks=(tln,),
                                           evac_dep=(mml[-1],),
                                           ring_slots=sl, ring_dep=rd)
                    rs.done(sk, mmv)
                with ES2() as st_p2:
                    P_.ps("pg0", [128, min(TOK, 2048)], stack=st_p2)
                    P_.ps("tr_ps", [128, 256], BF16, stack=st_p2)
                    P_.sb("tstage", [128, TOK], BF16, stack=st_p2)
                    P_.sb("tstage2", [128, 128], BF16, stack=st_p2)
                    sl, sk, rd = rs.next()
                    tv, _, _ = seg_proj(
                        P_, "kpre", "w_l0", 8, bias="b_l0",
                        transpose_dram=ccAv_in.rearrange("r (g d) -> (r g) d", g=VG),
                        single_group=True, dep_toks=(tvp,), evac_dep=(tvp,),
                        ring_slots=sl, ring_dep=rd)
            @blk.gpsimd
            def _(g):
                P_.wait(g, tv, P_.dtok("a"))
                ins = g.collective_compute(
                    "AllGather", ALU.bypass,
                    replica_groups=[[0, 1, 2, 3], [4, 5, 6, 7]],
                    ins=[ccAv_in.opt()], outs=[ccAv_out.opt()])
                ins.then_inc(P_.sems["ccAv"])
                P_.cnt["ccAv"] = 1

            with ES2() as st_t:
                P_.sb("tw_sb", [128, 4 * 512], F32, stack=st_t)
                P_.sb("twb_sb", [128, 4 * 512], BF16, stack=st_t)
                P_.sb("pr_sb", [128, 160], F32, stack=st_t)
                P_.sb("tabs_tr", [128, 512], F32, stack=st_t)
                tctx1 = seg_ternary_abs(P_, "w_1T", ct, stage_off=0, po=0)
                tctx2 = seg_ternary_abs(P_, "w_2T", ct, stage_off=2 * 512, po=80)
                with ES2() as st_tf:
                    tern_ps = P_.ps("tern_ps", [128, 144], stack=st_tf)
                    rv1 = seg_ternary_finish(P_, tctx1, tern_ps[:, 0:72], None)
                    rv2 = seg_ternary_finish(P_, tctx2, tern_ps[:, 72:144], rv1)
                tern_stores = [seg_ternary_apply(P_, tctx1, "w1t", rv1),
                               seg_ternary_apply(P_, tctx2, "w2t", rv2)]

                with ES2() as st_a:
                    P_.sb("kpa", [128, KB * 128], BF16, stack=st_a)
                    P_.sb("kpb", [128, KB * 128], BF16, stack=st_a)
                    P_.sb("vp", [128, KB * 130], BF16, stack=st_a)
                    P_.sb("qp", [128, TOK], BF16, stack=st_a)
                    P_.sb("av_sb", [128, 2 * 2 * QT], F32, stack=st_a)
                    P_.B["rec_sb"] = P_.B["st_sb"]
                    P_.sb("ex_sb", [128, 2 * 1536], BF16, stack=st_a)
                    P_.ps("sc0", [128, 1536], stack=st_a)
                    P_.ps("sc1", [128, 1536], stack=st_a)
                    P_.ps("avA", [128, QT], stack=st_a)
                    P_.ps("avB", [128, QT], stack=st_a)
                    zt = []
                    @blk.vector
                    def _(v):
                        zt.append(P_.V(v, "memset", P_.B["kpa"][64:128, :], 0.0,
                                       waits=(P_.dtok("a"), P_.dtok("wb"), P_.tok("pe"))))
                        zt.append(P_.V(v, "memset", P_.B["kpb"][0:64, :], 0.0))
                    ta = seg_attn_l0(P_, (t0, zt[1], ct[0], rv2, P_.tok("ccA"),
                                          P_.tok("ccAv")))
            with ES2() as st_w:
                _alloc_proj(P_, st_w, ring_ch=8, ring_cols=10240)
                rsS = RingSched(P_, [(0, 1024)])
                rsL = RingSched(P_, [(2048, 6144)])
                P_.sb("ev_sb", [128, 8 * 512], F32, stack=st_w)
                P_.ps("pg0", [128, min(TOK, 2048)], stack=st_w)
                P_.ps("pg1", [128, min(TOK, 2048)], stack=st_w)
                tx1 = seg_wo_ln(P_, "w_o0", "b_o0", (ta, t0, P_.dtok("tg"), P_.tok("gp")),
                                rs=rsS)
                tln2 = seg_prenorm(P_, "x_sb", (tx1,))
                tkk = tln2
                for qt in range(cfg["NQT"]):
                    P_.sb("h_sb", [128, 32 * QT], BF16, stack=st_w) if qt == 0 else None
                    sl, sk, rd = rsS.next()
                    th, _, mm1 = seg_proj(P_, "ln_at", "w1t", 32, bias="b_1", act=AF.Gelu,
                                          out_sbuf="h_sb", out_sbuf_w=QT, qw=QT,
                                          src_off=qt * QT, src_w=TOK, w_is_bf16=True,
                                          dep_toks=(tkk,) + tuple(tern_stores),
                                          w_dep=tern_stores, ring_slots=sl, ring_dep=rd)
                    rsS.done(sk, mm1)
                    sl, sk, rd = rsL.next()
                    to, _, mm2 = seg_proj(P_, "h_sb", "w2t", 8, bias="b_2",
                                          out_sbuf="ev_sb", out_sbuf_w=QT, qw=QT,
                                          src_w=QT, w_is_bf16=True, dep_toks=(th,),
                                          evac_dep=(tkk,), w_dep=tern_stores,
                                          ring_slots=sl, ring_dep=rd)
                    rsL.done(sk, mm2)
                    tkk = seg_postnorm_residual(P_, "ev_sb", QT, (to,),
                                                (qt * QT, (qt + 1) * QT))
                tln3 = seg_prenorm(P_, "x_sb", (tkk,))
                sl, sk, rd = rsS.next()
                tq2, _, mmq2 = seg_proj(P_, "ln_at", "w_q1", 8, out_dram=ccBq_in,
                                        dep_toks=(tln3,), ring_slots=sl, ring_dep=rd)
                rsS.done(sk, mmq2)
                @blk.gpsimd
                def _(g):
                    P_.wait(g, tq2, P_.dtok("a"))
                    insg = g.collective_compute(
                        "AllGather", ALU.bypass,
                        replica_groups=[[0, 1, 2, 3], [4, 5, 6, 7]],
                        ins=[ccBq_in.opt()], outs=[ccBq_out.opt()])
                    insg.then_inc(P_.sems["ccBq"])
                    P_.cnt["ccBq"] = 1
                sl, sk, rd = rsS.next()
                tk2, _, mmk2 = seg_proj(P_, "ln_at", "w_k1", 8, out_dram=ccBk_in,
                                        dep_toks=(tln3,), ring_slots=sl, ring_dep=rd)
                rsS.done(sk, mmk2)
                @blk.gpsimd
                def _(g):
                    P_.wait(g, tk2, P_.dtok("a"))
                    insg = g.collective_compute(
                        "AllGather", ALU.bypass,
                        replica_groups=[[0, 1, 2, 3], [4, 5, 6, 7]],
                        ins=[ccBk_in.opt()], outs=[ccBk_out.opt()])
                    insg.then_inc(P_.sems["ccBk"])
                    P_.cnt["ccBk"] = 1
            with ES2() as st_v:
                P_.sb("w_ring", [128, 2 * 8 * 128], BF16, stack=st_v)
                P_.B["_ring_grpw"] = 8 * 128
                P_.sb("tstage", [128, TOK], BF16, stack=st_v)
                P_.sb("tstage2", [128, 128], BF16, stack=st_v)
                P_.ps("pg0", [128, min(TOK, 2048)], stack=st_v)
                P_.ps("tr_ps", [128, 256], BF16, stack=st_v)
                tv2, _, _ = seg_proj(
                    P_, "ln_at", "w_v1", 8,
                    transpose_dram=ccBv_in.rearrange("r (g d) -> (r g) d", g=VG),
                    single_group=True, dep_toks=(tln3,))
            @blk.gpsimd
            def _(g):
                P_.wait(g, tv2, P_.dtok("a"))
                ins = g.collective_compute(
                    "AllGather", ALU.bypass,
                    replica_groups=[[0, 1, 2, 3], [4, 5, 6, 7]],
                    ins=[ccBv_in.opt()], outs=[ccBv_out.opt()])
                ins.then_inc(P_.sems["ccB"])
                P_.cnt["ccB"] = 1

            with ES2() as st_a2:
                P_.sb("kpa", [128, KB * 128], BF16, stack=st_a2)
                P_.sb("kpb", [128, KB * 128], BF16, stack=st_a2)
                P_.sb("vp", [128, KB * 130], BF16, stack=st_a2)
                P_.sb("qp", [128, TB], BF16, stack=st_a2)
                P_.sb("av_sb", [128, 2 * 2 * QT], F32, stack=st_a2)
                P_.B["rec_sb"] = P_.B["st_sb"]
                P_.sb("ex_sb", [128, 2 * 1536], BF16, stack=st_a2)
                P_.ps("sc0", [128, 1536], stack=st_a2)
                P_.ps("sc1", [128, 1536], stack=st_a2)
                P_.ps("avA", [128, QT], stack=st_a2)
                P_.ps("avB", [128, QT], stack=st_a2)
                tmask = seg_castload(P_, "c_masks", "c_masks_bf",
                                     list(P_.D["c_masks"].shape), stack=st_a2)
                zt2 = []
                @blk.vector
                def _(v):
                    zt2.append(P_.V(v, "memset", P_.B["kpa"][64:128, :], 0.0,
                                    waits=(P_.dtok("a"), P_.dtok("wb"), P_.tok("pe"),
                                           tmask)))
                    zt2.append(P_.V(v, "memset", P_.B["kpb"][0:64, :], 0.0))

                kqix = P_.B["l1kq_idx"]
                vix = P_.B["l1v_idx"]
                ccBv = ccBv_out.rearrange("r (g e) -> (r g) e", g=TOK // 64)

                def l1_loader(bp, dep, last):
                    reg = {}
                    @blk.gpsimd
                    def _(g):
                        P_.wait(g, *dep)
                        P_.wait(g, zt2[1], P_.tok("ccBq"), P_.tok("ccBk"))
                        if last is not None:
                            P_.wait(g, last)
                        for rr in range(4):
                            P_.dma_i(g, "w", P_.B["qp"][:, TOK * rr:TOK * (rr + 1)],
                                     ccBq_out, kqix[:, bp * 12 + rr:bp * 12 + rr + 1])
                            P_.dma_i(g, "w", P_.B["kpa"][0:64, TOK * rr:TOK * (rr + 1)],
                                     ccBk_out, kqix[0:64, bp * 12 + 4 + rr:bp * 12 + 5 + rr])
                            P_.dma_i(g, "w", P_.B["kpb"][64:128, TOK * rr:TOK * (rr + 1)],
                                     ccBk_out, kqix[0:64, bp * 12 + 8 + rr:bp * 12 + 9 + rr])
                        P_.wait(g, P_.tok("ccB"))
                        for ka in range(KB):
                            for h in (0, 1):
                                P_.dma_i(
                                    g, "w",
                                    P_.B["vp"][:, 130 * ka + 65 * h:130 * ka + 65 * h + 64],
                                    ccBv,
                                    vix[:, bp * 128 + 2 * ka + h:bp * 128 + 2 * ka + h + 1])
                            if (ka + 1) % 8 == 0 or ka == KB - 1:
                                reg[ka // 8] = P_.dtok("w")

                    def ldtok(k, kind, reg=reg):
                        return reg[min(k // 8, max(reg))]
                    return ldtok

                def l1_mid(last_bp0):
                    @blk.sync
                    def _(s):
                        P_.wait(s, last_bp0)
                        P_.dma(s, "a", ccC_in[0], P_.B["ln_at"][:, 0:TB])
                    @blk.gpsimd
                    def _(g):
                        P_.wait(g, P_.dtok("a"))
                        insg = g.collective_compute(
                            "AllGather", ALU.bypass,
                            replica_groups=[[0, 1, 2, 3], [4, 5, 6, 7]],
                            ins=[ccC_in[0].opt()], outs=[ccC_out[0].opt()])
                        insg.then_inc(P_.sems["ccC"])
                        P_.cnt["ccC"] = 1

                ta2 = seg_attn_l1(P_, (ix, P_.tok("ccB")), loader=l1_loader,
                                  mid_hook=l1_mid)
                @blk.sync
                def _(s):
                    P_.wait(s, ta2)
                    P_.dma(s, "a", ccC_in[1], P_.B["ln_at"][:, TB:2 * TB])
                @blk.gpsimd
                def _(g):
                    P_.wait(g, P_.dtok("a"))
                    ins = g.collective_compute(
                        "AllGather", ALU.bypass,
                        replica_groups=[[0, 1, 2, 3], [4, 5, 6, 7]],
                        ins=[ccC_in[1].opt()], outs=[ccC_out[1].opt()])
                    ins.then_inc(P_.sems["ccC1"])
                    P_.cnt["ccC1"] = 1
                gat = []
                @blk.gpsimd
                def _(g):
                    P_.wait(g, P_.tok("ccC"), P_.tok("ccC1"), ta2, P_.dtok("a"))
                    for ch in range(8):
                        ccCv = ccC_out[ch // 4].rearrange("d (r t) -> (d r) t", r=4)
                        gat.append(P_.dma_i(
                            g, "w", P_.B["ln_at"][:, ch * TOK:(ch + 1) * TOK],
                            ccCv, P_.B["cc_idx"][:, ch:ch + 1]))
            with ES2() as st_w2:
                _alloc_proj(P_, st_w2, dstage=False, ngrp=4)
                rs2 = RingSched(P_, [(0, 1024), (2048, 3072)])
                P_.sb("ev_sb", [128, 8 * 512], F32, stack=st_w2)
                P_.ps("pg0", [128, min(TOK, 2048)], stack=st_w2)
                P_.ps("pg1", [128, min(TOK, 2048)], stack=st_w2)
                tx3 = seg_wo_ln(P_, "w_o1", "b_o1", (P_.dtok("w"), P_.tok("gp")),
                                rs=rs2)
                tln4a = seg_prenorm(P_, "x_sb", (tx3,), qrange=(TOK - 512, TOK))
                @blk.sync
                def _(s):
                    P_.wait(s, tln4a)
                    P_.dma(s, "a", ccD_in.rearrange("c p -> p c"),
                           P_.B["ln_at"][:, TOK - 1::TOK], allow_slow_non_contiguous=True)
                @blk.gpsimd
                def _(g):
                    P_.wait(g, P_.dtok("a"))
                    ins = g.collective_compute(
                        "AllGather", ALU.bypass,
                        replica_groups=[[0, 1, 2, 3, 4, 5, 6, 7]],
                        ins=[ccD_in.opt()], outs=[ccD_out.opt()])
                    ins.then_inc(P_.sems["ccD"])
                    P_.cnt["ccD"] = 1
                bnd_raw = P_.sb("bnd_raw", [128, 8], BF16, stack=st_w2)
                P_.sb("bnd_sb", [128, 8], BF16, stack=st_w2)
                ccDv = ccD_out.rearrange("r (p o) -> (r p) o", o=1)
                bg = []
                @blk.gpsimd
                def _(g):
                    P_.wait(g, P_.tok("ccD"))
                    for cc in range(8):
                        bg.append(P_.dma_i(
                            g, "w", P_.B["bnd_raw"][:, cc:cc + 1],
                            ccDv, P_.B["bnd_idx"][:, cc:cc + 1]))
                bm = []
                @blk.vector
                def _(v):
                    bm.append(P_.V(v, "tensor_tensor", P_.B["bnd_sb"][:], P_.B["bnd_raw"][:],
                                   P_.B["bnd_valid"][:], ALU.mult,
                                   waits=(P_.dtok("w"), ct[0])))
                tln4 = seg_prenorm(P_, "x_sb", (tln4a,), qrange=(0, TOK - 512))
                NQT = cfg["NQT"]
                P_.sb("xr_sb", [128, 8 * QT], BF16, stack=st_w2)
                P_.sb("xk_sb", [128, 8 * QT], BF16, stack=st_w2)
                P_.sb("r_sb", [128, 8 * QT], F32, stack=st_w2)
                P_.sb("ka_sb", [128, 8 * QT], BF16, stack=st_w2)
                P_.sb("tstage", [128, QT], BF16, stack=st_w2)
                P_.sb("mx_sc", [128, 2 * QT], BF16, stack=st_w2)
                tkk = bm[0]
                mixes = [None] * NQT
                mixes[0] = seg_ffn1_mix(P_, 0, (tln4, tln4a, bm[0]))
                hist = [None] * NQT
                prev_to = None
                for qt in range(NQT):
                    tm = mixes[qt]
                    xo = 0
                    sl, sk, rd = rs2.next()
                    tr, _, mmr = seg_proj(P_, "xr_sb", "w_r1T", 8, act=AF.Sigmoid,
                                          out_sbuf="r_sb", out_sbuf_w=QT, qw=QT, src_w=QT,
                                          src_off=xo, dep_toks=(tm,),
                                          evac_dep=(prev_to,) if prev_to else (),
                                          ring_slots=sl, ring_dep=rd)
                    rs2.done(sk, mmr)
                    sl, sk, rd = rs2.next()
                    tka, _, mmka = seg_proj(P_, "xk_sb", "w_k1T", 8, evac_relu_sq=True,
                                            out_sbuf="ka_sb", out_sbuf_w=QT, qw=QT, src_w=QT,
                                            src_off=xo, dep_toks=(tm,), evac_dep=(tkk,),
                                            ring_slots=sl, ring_dep=rd)
                    rs2.done(sk, mmka)
                    sl, sk, rd = rs2.next()
                    to, _, mmv2 = seg_proj(P_, "ka_sb", "w_v1T", 8, evac_mult=("r_sb", QT),
                                           out_sbuf="ev_sb", out_sbuf_w=QT, qw=QT, src_w=QT,
                                           dep_toks=(tka,), evac_dep=((tkk,) if tkk else ()) + (tr,),
                                           ring_slots=sl, ring_dep=rd)
                    rs2.done(sk, mmv2)
                    hist[qt] = (mmr[-1], mmka[-1])
                    prev_to = to
                    if qt + 1 < NQT:
                        mixes[qt + 1] = seg_ffn1_mix(
                            P_, qt + 1, (tln4, tln4a) + tuple(hist[qt]))
                    tkk = seg_postnorm_residual(P_, "ev_sb", QT, (to,),
                                                (qt * QT, (qt + 1) * QT))
                seg_store_x(P_, outT, (tkk,))
            @blk.sync
            def _(s):
                for stem in ("a", "wb", "t"):
                    for i in range(P.NDS):
                        if P_.cnt[stem + str(i)]:
                            s.wait_ge(P_.sems[stem + str(i)], P_.cnt[stem + str(i)])
            @blk.gpsimd
            def _(g):
                for stem in ("w", "tg"):
                    for i in range(P.NDS):
                        if P_.cnt[stem + str(i)]:
                            g.wait_ge(P_.sems[stem + str(i)], P_.cnt[stem + str(i)])
    return nc


# version 10 (final)
# speedup vs baseline: 1.0155x; 1.0155x over previous
import numpy as np
import ml_dtypes
import concourse.bass as bass
import concourse.mybir as mybir
from concourse.bass_utils import run_bass_kernel_spmd

F32 = mybir.dt.float32
BF16 = mybir.dt.bfloat16
AF = mybir.ActivationFunctionType
ALU = mybir.AluOpType

MAGIC = 12582912.0
_BF = ml_dtypes.bfloat16


def make_cfg(tok=2048):
    c = {}
    c["TOK"] = tok
    c["TB"] = 4 * tok
    c["QT"] = 512
    c["NQT"] = tok // 512
    c["NBQT"] = (4 * tok) // 512
    c["KB"] = (4 * tok) // 128
    c["DCH"] = 8
    c["FCH"] = 32
    return c


CFG = make_cfg(2048)
FUSED = True


def _bias8(b):
    return np.ascontiguousarray(np.asarray(b, np.float32).reshape(-1, 128).T)


def _host_consts(cfg):
    QT = cfg["QT"]
    masks = np.zeros((128, 4 * QT), dtype=np.float32)
    for i in range(4):
        d = 128 * i
        for kk in range(128):
            if kk + d < QT:
                masks[kk, QT * i + kk + d:QT * i + QT] = 1.0
    return dict(
        c_ident=np.eye(128, dtype=np.float32),
        c_ones_col=np.ones((128, 1), dtype=np.float32),
        c_ones_row=np.ones((1, 128), dtype=np.float32),
        c_masks=masks,
    )


def _ring(w):
    w = np.asarray(w, np.float32)
    din, dout = w.shape
    nd = dout // 128
    return np.ascontiguousarray(
        w.reshape(din // 128, 128, nd, 128).transpose(2, 1, 0, 3).reshape(nd * 128, din))


def _prep_weights(I):
    d = {}
    d["w_q0"] = _ring(I["L0_attn_Wq"])
    d["w_k0"] = _ring(I["L0_attn_Wk"])
    d["w_v0"] = _ring(I["L0_attn_Wv"])
    d["w_l0"] = _ring(I["L0_attn_Wl"])
    d["w_o0"] = _ring(I["L0_attn_Wo"])
    for nm, k in [("b_q0", "L0_attn_bq"), ("b_k0", "L0_attn_bk"), ("b_v0", "L0_attn_bv"),
                  ("b_l0", "L0_attn_bl"), ("b_o0", "L0_attn_bo"), ("b_o1", "L1_attn_bo")]:
        d[nm] = _bias8(I[k])
    d["w_1T"] = np.ascontiguousarray(np.asarray(I["L0_ffn_W1"], dtype=np.float32).T)
    d["w_2T"] = np.ascontiguousarray(np.asarray(I["L0_ffn_W2"], dtype=np.float32).T)
    d["b_1"] = _bias8(I["L0_ffn_b1"])
    d["b_2"] = _bias8(I["L0_ffn_b2"])
    d["w_q1"] = _ring(
        np.transpose(np.asarray(I["L1_attn_Wq"], dtype=np.float32), (1, 0, 2)).reshape(1024, 1024))
    d["w_k1"] = _ring(
        np.transpose(np.asarray(I["L1_attn_Wk"], dtype=np.float32), (1, 0, 2)).reshape(1024, 1024))
    d["w_v1"] = _ring(
        np.transpose(np.asarray(I["L1_attn_Wv"], dtype=np.float32), (1, 0, 2)).reshape(1024, 1024))
    d["w_o1"] = _ring(I["L1_attn_Wo"])
    d["w_r1T"] = _ring(np.asarray(I["L1_ffn_W_r"], dtype=np.float32).T)
    d["w_k1T"] = _ring(np.asarray(I["L1_ffn_W_k"], dtype=np.float32).T)
    d["w_v1T"] = _ring(np.asarray(I["L1_ffn_W_v"], dtype=np.float32).T)
    d["mu_r"] = _bias8(I["L1_ffn_mu_r"])
    d["mu_k"] = _bias8(I["L1_ffn_mu_k"])
    return d


def _slice_x(x, core, cfg):
    TOK = cfg["TOK"]
    b, r = core // 4, core % 4
    xs = np.asarray(x[b, TOK * r:TOK * (r + 1), :], dtype=np.float32)
    return np.ascontiguousarray(xs.T)


class P:
    def __init__(self, nc, cfg, stack):
        self.nc = nc
        self.cfg = cfg
        self.block = None
        self.stack = stack
        self.sems = {}
        self.cnt = {}
        self.dmak = {}
        self.wmark = {}
        self.B = {}
        self.D = {}

    def sem(self, name):
        h = self.stack.enter_context(self.nc.semaphore(name))
        self.sems[name] = h
        self.cnt[name] = 0
        return h

    def inc(self, instr, s, n=1):
        self.cnt[s] += n
        instr.then_inc(self.sems[s], n)
        return (s, self.cnt[s])

    NDS = 4

    def dma(self, q, stream, out, in_, **kw):
        k = self.dmak.setdefault(stream, 0)
        sem = stream + str(k % self.NDS)
        prev = 16 * (k // self.NDS)
        if prev:
            q.wait_ge(self.sems[sem], prev)
        ins = q.dma_start(out=out, in_=in_, **kw)
        self.dmak[stream] = k + 1
        self.cnt[sem] += 16
        instr_tok = (sem, self.cnt[sem])
        ins.then_inc(self.sems[sem], 16)
        return instr_tok

    def dtok(self, stream):
        return tuple(self.tok(stream + str(i)) for i in range(self.NDS))

    def dma_i(self, g, stream, out, in_, idx_ap):
        from concourse.bass import IndirectOffsetOnAxis
        k = self.dmak.setdefault(stream, 0)
        sem = stream + str(k % self.NDS)
        prev = 16 * (k // self.NDS)
        if prev:
            g.wait_ge(self.sems[sem], prev)
        ins = g.indirect_dma_start(out, None, in_,
                                   IndirectOffsetOnAxis(ap=idx_ap, axis=0))
        self.dmak[stream] = k + 1
        self.cnt[sem] += 16
        ins.then_inc(self.sems[sem], 16)
        return (sem, self.cnt[sem])

    def tok(self, s):
        return (s, self.cnt[s])

    def wait(self, eng, *toks):
        seen = {}
        for t in toks:
            if t is None:
                continue
            if isinstance(t, (list, tuple)) and t and isinstance(t[0], (list, tuple)):
                for tt in t:
                    if tt is None:
                        continue
                    s, v = tt
                    if seen.get(s, -1) < v:
                        seen[s] = v
                continue
            s, v = t
            if seen.get(s, -1) < v:
                seen[s] = v
        try:
            ename = eng.engine
        except AttributeError:
            ename = id(eng)
        wm = self.wmark.setdefault(str(ename), {})
        for s, v in seen.items():
            if v > 0 and wm.get(s, 0) < v:
                eng.wait_ge(self.sems[s], v)
                wm[s] = v

    def V(self, v, op, *args, waits=(), **kw):
        self.wait(v, *waits)
        ins = getattr(v, op)(*args, **kw)
        return self.inc(ins, "dve")

    def S(self, s, op, *args, waits=(), **kw):
        self.wait(s, *waits)
        ins = getattr(s, op)(*args, **kw)
        return self.inc(ins, "act")

    def G(self, g, op, *args, waits=(), **kw):
        self.wait(g, *waits)
        ins = getattr(g, op)(*args, **kw)
        return self.inc(ins, "gp")

    _uid = 0

    def _uname(self, name):
        P._uid += 1
        return f"{name}__{P._uid}"

    def sb(self, name, shape, dtype, stack=None):
        h = (stack or self.stack).enter_context(
            self.nc.sbuf_tensor(self._uname(name), shape, dtype))
        self.B[name] = h
        return h

    def ps(self, name, shape, dtype=F32, stack=None):
        h = (stack or self.stack).enter_context(
            self.nc.psum_tensor(self._uname(name), shape, dtype))
        self.B[name] = h
        return h


def seg_load_x(P_, src_ap, dst="x_sb"):
    b = P_.block
    TOK = P_.cfg["TOK"]

    tk = []
    @b.sync
    def _(s):
        tk.append(P_.dma(s, "a",
                         P_.B[dst][:].rearrange("p (c t) -> p c t", t=TOK),
                         src_ap.rearrange("(c p) t -> p c t", p=128)))
    return tk[0]


def seg_consts(P_, names_shapes):
    b = P_.block
    nc = P_.nc

    tk = []
    @b.sync
    def _(s):
        for dname, sname, shape, dtype in names_shapes:
            h = P_.stack.enter_context(P_.nc.sbuf_tensor(P_._uname("sb_" + sname), shape, dtype))
            P_.B[sname] = h
            tk.append(P_.dma(s, "a", h[:], P_.D[dname]))
    return tk[-1]


def seg_castload(P_, dram_name, sbuf_name, shape, stack=None):
    b = P_.block

    tk = []
    @b.gpsimd
    def _(g):
        P_.sb(sbuf_name, shape, BF16, stack=stack)
        tk.append(P_.dma(g, "w", P_.B[sbuf_name][:], P_.D[dram_name]))
    return tk[0]


def _ln_tile(P_, src_ap_fn, it, dep_toks, out_bf_fn=None, add_x_cols=None):
    b = P_.block
    cfg = P_.cfg
    QT, DCH, TOK = cfg["QT"], cfg["DCH"], cfg["TOK"]
    sq, st, bc = P_.B["sq_sb"], P_.B["st_sb"], P_.B["bc_sb"]
    onesb, onesf = P_.B["c_ones_col_bf"], P_.B["c_ones_row_f"]
    pstat = P_.B["pg0"]
    pbc = P_.B["pg1"]
    x_sb = P_.B.get("x_sb")

    cast_toks = []
    @b.vector
    def _(v):
        for c in range(DCH):
            t1 = P_.V(v, "tensor_copy", sq[:, (2 * c) * QT:(2 * c + 1) * QT], src_ap_fn(c),
                      waits=dep_toks if c == 0 else ())
            t2 = P_.V(v, "tensor_tensor", sq[:, (2 * c + 1) * QT:(2 * c + 2) * QT],
                      src_ap_fn(c), src_ap_fn(c), ALU.mult)
            cast_toks.append(t2)

    mm_tok = []
    @b.tensor
    def _(t):
        P_.wait(t, cast_toks[-1])
        for c in range(DCH):
            t.matmul(pstat[0:1, 0:QT], onesb[:], sq[:, (2 * c) * QT:(2 * c + 1) * QT],
                     start=(c == 0), stop=(c == DCH - 1))
        for c in range(DCH):
            ins = t.matmul(pstat[0:1, QT:2 * QT], onesb[:],
                           sq[:, (2 * c + 1) * QT:(2 * c + 2) * QT],
                           start=(c == 0), stop=(c == DCH - 1))
        mm_tok.append(P_.inc(ins, "pe"))

    sm = []
    @b.vector
    def _(v):
        a = P_.V(v, "tensor_scalar", st[0:1, 0:QT], pstat[0:1, 0:QT],
                 1.0 / 1024.0, None, ALU.mult, waits=(mm_tok[0],))
        bt = P_.V(v, "tensor_scalar", st[0:1, QT:2 * QT], pstat[0:1, QT:2 * QT],
                  1.0 / 1024.0, None, ALU.mult)
        c2 = P_.V(v, "tensor_tensor", st[0:1, 2 * QT:3 * QT], st[0:1, 0:QT],
                  st[0:1, 0:QT], ALU.mult, waits=(a,))
        d2 = P_.V(v, "tensor_tensor", st[0:1, 2 * QT:3 * QT], st[0:1, QT:2 * QT],
                  st[0:1, 2 * QT:3 * QT], ALU.subtract, waits=(bt, c2))
        sm.append(d2)

    rs = []
    @b.scalar
    def _(s):
        t1 = P_.S(s, "activation", st[0:1, 3 * QT:4 * QT], st[0:1, 2 * QT:3 * QT], AF.Ln,
                  bias=P_.B["eps_sb"][0:1, 0:1], waits=(sm[0],))
        rs.append(P_.S(s, "activation", st[0:1, 3 * QT:4 * QT], st[0:1, 3 * QT:4 * QT],
                       AF.Exp, scale=-0.5, waits=(t1,)))

    bc_tok = []
    @b.tensor
    def _(t):
        P_.wait(t, rs[0], P_.tok("dve"))
        t.matmul(pbc[0:128, 0:QT], onesf[:], st[0:1, 0:QT], start=True, stop=True)
        ins = t.matmul(pbc[0:128, QT:2 * QT], onesf[:], st[0:1, 3 * QT:4 * QT],
                       start=True, stop=True)
        bc_tok.append(P_.inc(ins, "pe"))

    last = []
    @b.vector
    def _(v):
        mb = P_.V(v, "tensor_copy", bc[:, 0:QT], pbc[0:128, 0:QT], waits=(bc_tok[0],))
        rb = P_.V(v, "tensor_copy", bc[:, QT:2 * QT], pbc[0:128, QT:2 * QT])
        tk = rb
        for c in range(DCH):
            t1 = P_.V(v, "tensor_tensor", bc[:, 2 * QT:3 * QT], src_ap_fn(c), bc[:, 0:QT],
                      ALU.subtract, waits=(mb, rb, tk))
            if out_bf_fn is not None:
                tk = P_.V(v, "tensor_tensor", out_bf_fn(c), bc[:, 2 * QT:3 * QT],
                          bc[:, QT:2 * QT], ALU.mult, waits=(t1,))
            else:
                t2 = P_.V(v, "tensor_tensor", bc[:, 3 * QT:4 * QT], bc[:, 2 * QT:3 * QT],
                          bc[:, QT:2 * QT], ALU.mult, waits=(t1,))
                xa = x_sb[:, c * TOK + add_x_cols[0] + it * QT:c * TOK + add_x_cols[0] + (it + 1) * QT]
                tk = P_.V(v, "tensor_tensor", xa, xa, bc[:, 3 * QT:4 * QT], ALU.add, waits=(t2,))
        last.append(tk)
    return last[0], bc_tok[0]


def seg_prenorm(P_, src_name, dep_toks, qrange=None, dst_name="ln_at", dst_off=0):
    cfg = P_.cfg
    TOK, QT = cfg["TOK"], cfg["QT"]
    if qrange is None:
        qrange = (0, TOK)
    src = P_.B[src_name]
    dst = P_.B[dst_name]
    tk = None
    for it in range((qrange[1] - qrange[0]) // QT):
        q0 = qrange[0] + it * QT
        tk, _ = _ln_tile(
            P_,
            lambda c, q0=q0: src[:, c * TOK + q0:c * TOK + q0 + QT],
            it,
            dep_toks,
            out_bf_fn=lambda c, q0=q0: dst[:, dst_off + c * TOK + q0:dst_off + c * TOK + q0 + QT],
        )
        dep_toks = (tk,)
    return tk


def seg_postnorm_residual(P_, src_name, src_w, dep_toks, qcols):
    cfg = P_.cfg
    QT = cfg["QT"]
    src = P_.B[src_name]
    tk = None
    for it in range((qcols[1] - qcols[0]) // QT):
        tk, _ = _ln_tile(
            P_,
            lambda c, it=it: src[:, c * src_w + it * QT:c * src_w + (it + 1) * QT],
            it,
            dep_toks,
            out_bf_fn=None,
            add_x_cols=(qcols[0],),
        )
        dep_toks = (tk,)
    return tk


def seg_proj(P_, src_name, w_dram_name, n_dout, *, src_w=None, bias=None, act=None,
             out_sbuf=None, out_sbuf_w=None, out_f32=False, out_dram=None,
             dep_toks=(), w_is_bf16=False, single_group=False, transpose_dram=None,
             evac_mult=None, evac_relu_sq=None, src_off=0, qw=None, evac_dep=(),
             w_dep=(), w_sbuf=None, w_tok=None, ring_slots=None, ring_dep=None):
    b = P_.block
    cfg = P_.cfg
    TOK, QT, DCH = cfg["TOK"], cfg["QT"], cfg["DCH"]
    if src_w is None:
        src_w = TOK
    if qw is None:
        qw = src_w
    nq = qw // QT
    src = P_.B[src_name]
    wd = P_.D[w_dram_name]
    din_ch = wd.shape[1] // 128
    assert wd.shape[0] == n_dout * 128, (w_dram_name, wd.shape, n_dout)
    wr = P_.B["w_ring"] if w_sbuf is None else None
    wrW = P_.B.get("_ring_grpw", 0)
    wsb = P_.B[w_sbuf] if w_sbuf is not None else None
    pg = [P_.B["pg0"], P_.B["pg0" if single_group else "pg1"]]
    ident = P_.B.get("c_ident_bf")
    ngroups = 1 if single_group else 2
    if ring_slots is None:
        ring_slots = (0, 0) if single_group else (0, wrW)
    if ring_dep is None:
        ring_dep = P_.tok("pe")

    wtoks = [None] * n_dout
    mmtoks = [None] * n_dout
    evtoks = [None] * n_dout
    stoks = [None] * n_dout
    last_tok = [None]
    psum_dep = (P_.tok("dve"), P_.tok("act"))
    dstage_dep = P_.dtok("a")

    for j in range(n_dout):
        g = j % ngroups
        so_ = ring_slots[g]
        if wr is not None:
            assert so_ + din_ch * 128 <= wr.shape[1], (w_dram_name, so_, din_ch)
        if w_sbuf is not None:
            wtoks[j] = w_tok
        elif w_is_bf16:
            @b.sync
            def _(s, j=j, g=g, so_=so_):
                if mmtoks[j - ngroups] is not None:
                    P_.wait(s, mmtoks[j - ngroups])
                else:
                    P_.wait(s, ring_dep)
                P_.wait(s, *w_dep)
                wtoks[j] = P_.dma(
                    s, "wb",
                    wr[:, so_:so_ + din_ch * 128],
                    wd[128 * j:128 * (j + 1), :])
        else:
            @b.gpsimd
            def _(gq, j=j, g=g, so_=so_):
                if mmtoks[j - ngroups] is not None:
                    P_.wait(gq, mmtoks[j - ngroups])
                else:
                    P_.wait(gq, ring_dep)
                P_.wait(gq, *w_dep)
                wtoks[j] = P_.dma(
                    gq, "w",
                    wr[:, so_:so_ + din_ch * 128],
                    wd[128 * j:128 * (j + 1), :])

        @b.tensor
        def _(t, j=j, g=g, so_=so_):
            P_.wait(t, wtoks[j], *dep_toks)
            if evtoks[j - ngroups] is not None:
                P_.wait(t, evtoks[j - ngroups])
            else:
                P_.wait(t, *psum_dep)
            for c in range(din_ch):
                if w_sbuf is not None:
                    lhs = wsb[:, j * din_ch * 128 + c * 128:j * din_ch * 128 + (c + 1) * 128]
                else:
                    lhs = wr[:, so_ + c * 128:so_ + (c + 1) * 128]
                for qb in range(nq):
                    ins = t.matmul(pg[g][:, qb * QT:(qb + 1) * QT],
                                   lhs,
                                   src[:, src_off + c * src_w:src_off + c * src_w + qw][:, qb * QT:(qb + 1) * QT],
                                   start=(c == 0), stop=(c == din_ch - 1))
            mmtoks[j] = P_.inc(ins, "pe")

        bias_ap = P_.B[bias][:, j:j + 1] if bias else None
        if act is not None:
            @b.scalar
            def _(s, j=j, g=g):
                P_.wait(s, mmtoks[j], *evac_dep)
                kw = {}
                if bias_ap is not None:
                    kw["bias"] = bias_ap
                dst = P_.B[out_sbuf][:, j * out_sbuf_w:j * out_sbuf_w + qw]
                ins = s.activation(dst, pg[g][:, 0:qw], act, **kw)
                evtoks[j] = P_.inc(ins, "act")
                last_tok[0] = evtoks[j]
        elif transpose_dram is not None:
            stg = P_.B["tstage"]
            trp = P_.B["tr_ps"]
            @b.vector
            def _(v, j=j, g=g):
                w = (mmtoks[j],) + tuple(evac_dep) + (P_.dtok("a"),)
                if bias_ap is not None:
                    evtoks[j] = P_.V(v, "tensor_scalar", stg[:, 0:qw], pg[g][:, 0:qw],
                                     bias_ap, None, ALU.add, waits=w)
                else:
                    evtoks[j] = P_.V(v, "tensor_copy", stg[:, 0:qw], pg[g][:, 0:qw], waits=w)
            ntile = qw // 128
            prev_store = [None]
            for i in range(ntile):
                trt = []
                @b.tensor
                def _(t, i=i, j=j):
                    P_.wait(t, evtoks[j], P_.tok("dve"))
                    ins = t.transpose(trp[:, 0:128], P_.B["tstage"][:, 128 * i:128 * (i + 1)],
                                      ident[:])
                    trt.append(P_.inc(ins, "pe"))
                cp = []
                @b.vector
                def _(v, i=i):
                    w = (trt[0],) + ((prev_store[0],) if prev_store[0] else ())
                    cp.append(P_.V(v, "tensor_copy", P_.B["tstage2"][:, 0:128], trp[:, 0:128],
                                   waits=w))
                @b.sync
                def _(s, i=i, j=j):
                    P_.wait(s, cp[0])
                    prev_store[0] = P_.dma(
                        s, "a", transpose_dram[128 * i:128 * (i + 1), 128 * j:128 * (j + 1)],
                        P_.B["tstage2"][:, 0:128])
                    last_tok[0] = prev_store[0]
        elif evac_mult is not None or evac_relu_sq is not None:
            @b.vector
            def _(v, j=j, g=g):
                w = (mmtoks[j],) + tuple(evac_dep)
                if out_dram is not None:
                    w = w + ((stoks[j - 2],) if stoks[j - 2] is not None else (dstage_dep,))
                if evac_mult is not None:
                    mbuf, mw = evac_mult
                    evtoks[j] = P_.V(v, "tensor_tensor", _out_ap(P_, out_sbuf, out_sbuf_w, j, qw, out_dram),
                                     pg[g][:, 0:qw], P_.B[mbuf][:, j * mw:j * mw + qw],
                                     ALU.mult, waits=w)
                else:
                    w2 = w + ((evtoks[j - 1],) if (j >= 1 and evtoks[j - 1]) else ())
                    t1 = P_.V(v, "tensor_scalar", P_.B["tstage"][:, 0:qw], pg[g][:, 0:qw],
                              0.0, None, ALU.max, waits=w2)
                    evtoks[j] = P_.V(v, "tensor_tensor", _out_ap(P_, out_sbuf, out_sbuf_w, j, qw, out_dram),
                                     P_.B["tstage"][:, 0:qw], P_.B["tstage"][:, 0:qw],
                                     ALU.mult, waits=(t1,))
                last_tok[0] = evtoks[j]
        else:
            @b.scalar
            def _(s, j=j, g=g):
                w = (mmtoks[j],) + tuple(evac_dep)
                if out_dram is not None:
                    w = w + ((stoks[j - 2],) if stoks[j - 2] is not None else (dstage_dep,))
                kw = {}
                if bias_ap is not None:
                    kw["bias"] = bias_ap
                evtoks[j] = P_.S(s, "activation",
                                 _out_ap(P_, out_sbuf, out_sbuf_w, j, qw, out_dram),
                                 pg[g][:, 0:qw], AF.Identity, waits=w, **kw)
                last_tok[0] = evtoks[j]
            if out_dram is not None:
                @b.sync
                def _(s, j=j):
                    P_.wait(s, evtoks[j])
                    sl = (j % 2) * (P_.B["dstage"].shape[1] // 2)
                    stoks[j] = P_.dma(s, "a", out_dram[128 * j:128 * (j + 1), :],
                                      P_.B["dstage"][:, sl:sl + qw])
                    last_tok[0] = stoks[j]
    return last_tok[0], evtoks, mmtoks


def _out_ap(P_, out_sbuf, out_sbuf_w, j, qw, out_dram):
    if out_dram is not None:
        sl = (j % 2) * (P_.B["dstage"].shape[1] // 2)
        return P_.B["dstage"][:, sl:sl + qw]
    return P_.B[out_sbuf][:, j * out_sbuf_w:j * out_sbuf_w + qw]


def _attn_pair(P_, *, QP, KPA, KPB, VP, qtiles, out_fn, ldtok_fn, pre_tok):
    b = P_.block
    QT = P_.cfg["QT"]
    sc = [P_.B["sc0"], P_.B["sc1"]]
    av = [P_.B["avA"], P_.B["avB"]]
    ex = P_.B["ex_sb"]
    avs = P_.B["av_sb"]
    rec = P_.B["rec_sb"]
    bcs = P_.B["bc_sb"]
    masks = P_.B.get("c_masks_bf")
    onesf = P_.B["c_ones_row_f"]
    GW = avs.shape[1] // 2
    gper = GW // QT

    gbase = [0]
    exp_raw = {}
    exp_fin = {}
    sc_tok = {}
    av_tok = {}

    ngrp = (len(qtiles) + gper - 1) // gper
    flush_tok = pre_tok
    for fg in range(ngrp):
        grp = qtiles[fg * gper:(fg + 1) * gper]
        evac_tok = None
        for qi, qt_ in enumerate(grp):
            slots_all = [(k, h) for k in range(qt_["kmax"]) for h in (0, 1)]
            groups = [slots_all[i:i + 3] for i in range(0, len(slots_all), 3)]
            Gn = len(groups)
            g0 = gbase[0]
            guard = evac_tok if evac_tok is not None else flush_tok
            d0 = qt_["diag0"]

            order = []
            pend = []
            for gl in range(Gn):
                order.append(("sc", gl))
                pend.append(gl)
                if len(pend) == 2:
                    order.append(("av", pend.pop(0)))
            for gl in pend:
                order.append(("av", gl))
            pe0 = P_.cnt["pe"]
            for i, (kind, gl) in enumerate(order):
                if kind == "sc":
                    sc_tok[g0 + gl] = ("pe", pe0 + i + 1)
                else:
                    av_tok[g0 + gl] = ("pe", pe0 + i + 1)
            act0 = P_.cnt["act"]
            dve0 = P_.cnt["dve"]
            ndve = 0
            for gl, slots in enumerate(groups):
                exp_raw[g0 + gl] = ("act", act0 + gl + 1)
                nm = sum(1 for k, h in slots if d0 is not None and k >= d0)
                if nm:
                    ndve += nm
                    exp_fin[g0 + gl] = ("dve", dve0 + ndve)
                else:
                    exp_fin[g0 + gl] = exp_raw[g0 + gl]

            @b.tensor
            def _(t):
                for kind, gl in order:
                    gi = g0 + gl
                    slots = groups[gl]
                    mk = max(k for k, h in slots)
                    if kind == "sc":
                        w = [ldtok_fn(mk, "sc")]
                        if gi - 2 >= 0:
                            w.append(exp_fin.get(gi - 2))
                        P_.wait(t, *w)
                        for si, (k, h) in enumerate(slots):
                            ins = t.matmul(sc[gi % 2][:, si * QT:(si + 1) * QT],
                                           (KPA if h == 0 else KPB)[:, 128 * k:128 * (k + 1)],
                                           QP[:, qt_["qcols"]:qt_["qcols"] + QT],
                                           start=True, stop=True)
                        tok = P_.inc(ins, "pe")
                        assert tok == sc_tok[gi], (tok, sc_tok[gi])
                    else:
                        P_.wait(t, exp_fin[gi], guard, ldtok_fn(mk, "av"))
                        for si, (k, h) in enumerate(slots):
                            ins = t.matmul(av[h][0:65, :],
                                           VP[:, 130 * k + 65 * h:130 * k + 65 * h + 65],
                                           ex[:, (gi % 2) * 1536 + si * QT:(gi % 2) * 1536 + (si + 1) * QT],
                                           start=(k == 0), stop=(k == qt_["kmax"] - 1))
                        tok = P_.inc(ins, "pe")
                        assert tok == av_tok[gi], (tok, av_tok[gi])

            @b.scalar
            def _(s):
                for gl, slots in enumerate(groups):
                    gi = g0 + gl
                    w = [sc_tok[gi]]
                    if gi - 2 >= 0:
                        w.append(av_tok.get(gi - 2))
                    tok = P_.S(s, "activation",
                               ex[:, (gi % 2) * 1536:(gi % 2) * 1536 + len(slots) * QT],
                               sc[gi % 2][:, 0:len(slots) * QT], AF.Exp, scale=0.125,
                               waits=w)
                    assert tok == exp_raw[gi], (tok, exp_raw[gi])

            if d0 is not None:
                @b.vector
                def _(v):
                    for gl, slots in enumerate(groups):
                        gi = g0 + gl
                        tk = None
                        for si, (k, h) in enumerate(slots):
                            if k >= d0:
                                sl = ex[:, (gi % 2) * 1536 + si * QT:(gi % 2) * 1536 + (si + 1) * QT]
                                tk = P_.V(v, "tensor_tensor", sl, sl,
                                          masks[:, (k - d0) * QT:(k - d0 + 1) * QT],
                                          ALU.mult, waits=(exp_raw[gi],))
                        if tk is not None:
                            assert tk == exp_fin[gi], (tk, exp_fin[gi])

            gbase[0] += Gn
            col = qi * QT
            et = []
            @b.vector
            def _(v):
                P_.wait(v, av_tok[gbase[0] - 1], guard)
                P_.V(v, "tensor_copy", avs[0:65, col:col + QT], av[0][0:65, :])
                et.append(P_.V(v, "tensor_copy", avs[0:65, GW + col:GW + col + QT],
                               av[1][0:65, :]))
            evac_tok = et[0]
        gw = len(grp) * QT
        rt = []
        @b.scalar
        def _(s):
            t1 = P_.S(s, "activation", rec[0:1, 0:gw], avs[64:65, 0:gw], AF.Ln,
                      waits=(evac_tok,))
            t2 = P_.S(s, "activation", rec[0:1, GW:GW + gw], avs[64:65, GW:GW + gw],
                      AF.Ln, waits=(t1,))
            t3 = P_.S(s, "activation", rec[0:1, 0:gw], rec[0:1, 0:gw], AF.Exp,
                      scale=-1.0, waits=(t2,))
            rt.append(P_.S(s, "activation", rec[0:1, GW:GW + gw], rec[0:1, GW:GW + gw],
                           AF.Exp, scale=-1.0, waits=(t3,)))
        peF = P_.cnt["pe"]
        items = [(h, qi) for h in (0, 1) for qi in range(len(grp))]
        bct = {}
        for i, (h, qi) in enumerate(items):
            bct[(h, qi)] = ("pe", peF + i + 1)
        dveF = P_.cnt["dve"]
        c1_tok = {}
        for i, (h, qi) in enumerate(items):
            c1_tok[(h, qi)] = ("dve", dveF + 2 * i + 1)
        @b.tensor
        def _(t):
            P_.wait(t, rt[0], P_.tok("dve"))
            for i, (h, qi) in enumerate(items):
                if qi > 0:
                    P_.wait(t, c1_tok[(h, qi - 1)])
                ins = t.matmul(av[h][0:64, :], onesf[:, 0:64],
                               rec[0:1, h * GW + qi * QT:h * GW + qi * QT + QT],
                               start=True, stop=True)
                tok = P_.inc(ins, "pe")
                assert tok == bct[(h, qi)]
        @b.vector
        def _(v):
            prev = None
            for h in (0, 1):
                for qi, qt_ in enumerate(grp):
                    w = [bct[(h, qi)]] + ([prev] if prev else [])
                    c1 = P_.V(v, "tensor_copy", bcs[0:64, 0:QT], av[h][0:64, :], waits=w)
                    assert c1 == c1_tok[(h, qi)]
                    prev = P_.V(
                        v, "tensor_tensor", out_fn(h, qt_["out0"]),
                        avs[0:64, h * GW + qi * QT:h * GW + qi * QT + QT],
                        bcs[0:64, 0:QT], ALU.mult, waits=(c1,))
            P_.B["_flushtok"] = prev
        flush_tok = P_.B["_flushtok"]
    return flush_tok


def seg_attn_l0(P_, dep_toks, mid_hook=None):
    b = P_.block
    cfg = P_.cfg
    TOK, QT, KB, NQT = cfg["TOK"], cfg["QT"], cfg["KB"], cfg["NQT"]
    gK, gV, qd = P_.D["gK"], P_.D.get("gV"), P_.D["q_dram"]
    gks = P_.D.get("gKs", 1024)
    gvf = P_.D.get("gVf", None)
    kpa, kpb, vp, qp = P_.B["kpa"], P_.B["kpb"], P_.B["vp"], P_.B["qp"]
    ln_at = P_.B["ln_at"]
    kpc = KB // 4

    prev_pair_tok = dep_toks[0] if dep_toks else None
    last = None
    ons = []
    depk = tuple(t for t in dep_toks if t is not None and t[0] != "ccAv")
    depv = tuple(t for t in dep_toks if t is not None and t[0] == "ccAv")
    @b.vector
    def _(v):
        ons.append(P_.V(v, "memset",
                        vp[:].rearrange("p (c s) -> p c s", s=130)[:, :, 64:65], 1.0,
                        waits=depk))
        ons.append(P_.V(v, "memset",
                        vp[:].rearrange("p (c s) -> p c s", s=130)[:, :, 129:130], 1.0))
    for p in range(8):
        ldt = {}
        @b.sync
        def _(s):
            P_.wait(s, *depk)
            if last is not None:
                P_.wait(s, last)
            else:
                P_.wait(s, ons[1])
            P_.dma(s, "a", qp[:, 0:TOK], qd[128 * p:128 * (p + 1), :])
            for rr in range(4):
                P_.dma(s, "a", kpa[0:64, TOK * rr:TOK * (rr + 1)],
                       gK[gks * rr + 128 * p:gks * rr + 128 * p + 64, :])
                P_.dma(s, "a", kpb[64:128, TOK * rr:TOK * (rr + 1)],
                       gK[gks * rr + 128 * p + 64:gks * rr + 128 * p + 128, :])
            ldt["k"] = P_.dtok("a")
            P_.wait(s, *depv)
            for rr in range(4):
                gv_rr = gvf(rr) if gvf is not None else gV[TOK * rr:TOK * (rr + 1), :]
                for h in (0, 1):
                    P_.dma(
                        s, "a",
                        vp[:].rearrange("p (c s) -> p c s", s=130)
                            [:, rr * kpc:(rr + 1) * kpc, 65 * h:65 * h + 64],
                        gv_rr[:, 128 * p + 64 * h:128 * p + 64 * h + 64]
                            .rearrange("(c q) e -> q c e", q=128))
                ldt[rr] = P_.dtok("a")

        def ldtok(k, kind, ldt=ldt):
            if kind == "sc":
                return ldt["k"]
            return ldt[min(k // kpc, 3)]

        qtiles = [dict(qcols=qt * QT, kmax=KB, diag0=None, out0=p * TOK + qt * QT)
                  for qt in range(NQT)]

        def out_fn(h, out0):
            base = 0 if h == 0 else 64
            return ln_at[base:base + 64, out0:out0 + QT]

        last = _attn_pair(P_, QP=qp, KPA=kpa, KPB=kpb, VP=vp, qtiles=qtiles,
                          out_fn=out_fn, ldtok_fn=ldtok, pre_tok=prev_pair_tok)
        prev_pair_tok = last
        if p == 0 and mid_hook is not None:
            prev_pair_tok = mid_hook(last)
        dep_toks = ()
    return last


def seg_attn_l1(P_, dep_toks, loader=None, mid_hook=None):
    b = P_.block
    cfg = P_.cfg
    TB, QT, NBQT = cfg["TB"], cfg["QT"], cfg["NBQT"]
    KB = cfg["KB"]
    kpa, kpb, vp, qp = P_.B["kpa"], P_.B["kpb"], P_.B["vp"], P_.B["qp"]
    ln_at = P_.B["ln_at"]

    last = None
    ons = []
    @b.vector
    def _(v):
        P_.V(v, "memset", vp[:].rearrange("p (c s) -> p c s", s=130)[:, :, 64:65], 1.0,
             waits=dep_toks)
        ons.append(P_.V(v, "memset",
                        vp[:].rearrange("p (c s) -> p c s", s=130)[:, :, 129:130], 1.0))
    for bp in range(2):
        if loader is not None:
            ldtok = loader(bp, dep_toks + (ons[-1],), last)
        else:
            @b.sync
            def _(s):
                P_.wait(s, *dep_toks)
                P_.wait(s, last if last is not None else ons[-1])
                P_.dma(s, "a", qp[:, 0:TB], P_.D["qprs"][128 * bp:128 * (bp + 1), :])
                P_.dma(s, "a", kpa[0:64, 0:TB], P_.D["kprs"][128 * bp:128 * bp + 64, :])
                P_.dma(s, "a", kpb[64:128, 0:TB], P_.D["kprs"][128 * bp + 64:128 * bp + 128, :])
                for h in (0, 1):
                    P_.dma(
                        s, "a",
                        vp[:].rearrange("p (c s) -> p c s", s=130)[:, :, 65 * h:65 * h + 64],
                        P_.D["vprs"][TB * bp:TB * (bp + 1), 64 * h:64 * h + 64]
                            .rearrange("(c q) e -> q c e", q=128))
            tk0 = P_.dtok("a")
            ldtok = lambda k, kind, tk0=tk0: tk0

        qtiles = [dict(qcols=qt * QT, kmax=4 * (qt + 1), diag0=4 * qt,
                       out0=bp * TB + qt * QT) for qt in range(NBQT)]

        def out_fn(h, out0):
            base = 0 if h == 0 else 64
            return ln_at[base:base + 64, out0:out0 + QT]

        last = _attn_pair(P_, QP=qp, KPA=kpa, KPB=kpb, VP=vp, qtiles=qtiles,
                          out_fn=out_fn, ldtok_fn=ldtok, pre_tok=None)
        if bp == 0 and mid_hook is not None:
            mid_hook(last)
        dep_toks = ()
    return last


def seg_ternary_abs(P_, w_dram_name, dep_toks, stage_off=0, po=0):
    b = P_.block
    wd = P_.D[w_dram_name]
    W = wd.shape[1]
    NCH = wd.shape[0] // 128
    tw, pr = P_.B["tw_sb"], P_.B["pr_sb"]
    so = stage_off
    CW = min(W, 512)
    subs = [(r, c0) for r in range(NCH) for c0 in range(0, W, CW)]
    ptoks = []
    for i, (r, c0) in enumerate(subs):
        lt = []
        @b.sync
        def _(s):
            if i >= 2:
                P_.wait(s, ptoks[i - 2])
            P_.wait(s, *dep_toks)
            lt.append(P_.dma(s, "t", tw[:, so + (i % 2) * CW:so + (i % 2) * CW + CW],
                             wd[128 * r:128 * (r + 1), c0:c0 + CW]))
        @b.vector
        def _(v):
            ptoks.append(P_.V(v, "reduce_sum", pr[:, po + i:po + i + 1],
                              tw[:, so + (i % 2) * CW:so + (i % 2) * CW + CW],
                              axis=mybir.AxisListType.X, apply_absolute_value=True,
                              waits=(lt[0],)))
    return dict(ptok=ptoks[-1], subs=subs, NCH=NCH, W=W, CW=CW, so=so, po=po,
                wd=wd)


def seg_ternary_finish(P_, ctx, mm_ps, guard):
    b = P_.block
    pr = P_.B["pr_sb"]
    onesf32c = P_.B["c_ones_col_f"]
    onesf = P_.B["c_ones_row_f"]
    po, NS = ctx["po"], len(ctx["subs"])
    N = float(ctx["NCH"]) * 128.0 * ctx["W"]
    mm = []
    @b.tensor
    def _(t):
        P_.wait(t, ctx["ptok"], guard)
        ins = t.matmul(mm_ps[0:1, 0:NS], onesf32c[:, 0:1], pr[:, po:po + NS],
                       start=True, stop=True)
        mm.append(P_.inc(ins, "pe"))
    sc = []
    @b.vector
    def _(v):
        cm = P_.V(v, "reduce_sum", pr[0:1, po + 33:po + 34], mm_ps[0:1, 0:NS],
                  axis=mybir.AxisListType.X, waits=(mm[0],))
        t1 = P_.V(v, "tensor_scalar", pr[0:1, po + 34:po + 35], pr[0:1, po + 33:po + 34],
                  1.0 / N, 1e-8, ALU.mult, ALU.max, waits=(cm,))
        sc.append(P_.V(v, "reciprocal", pr[0:1, po + 35:po + 36],
                       pr[0:1, po + 34:po + 35], waits=(t1,)))
    bc = []
    @b.tensor
    def _(t):
        P_.wait(t, sc[0])
        ins = t.matmul(mm_ps[0:128, 64:65], onesf[:], pr[0:1, po + 35:po + 36],
                       start=True, stop=True)
        bc.append(P_.inc(ins, "pe"))
    rv = []
    @b.vector
    def _(v):
        rv.append(P_.V(v, "tensor_copy", pr[:, po + 36:po + 37], mm_ps[0:128, 64:65],
                       waits=(bc[0],)))
    return rv[0]


def seg_ternary_apply(P_, ctx, wt_dram_name, rv):
    b = P_.block
    wtd = P_.D[wt_dram_name]
    tw, twb, pr = P_.B["tw_sb"], P_.B["twb_sb"], P_.B["pr_sb"]
    so, po, CW = ctx["so"], ctx["po"], ctx["CW"]
    wd = ctx["wd"]
    subs = ctx["subs"]
    rvec = pr[:, po + 36:po + 37]
    stores = []
    lts = []

    @b.gpsimd
    def _(g):
        P_.wait(g, rv)
        for i in range(min(2, len(subs))):
            r, c0 = subs[i]
            lts.append(P_.dma(g, "tg", tw[:, so + (i % 2) * CW:so + (i % 2) * CW + CW],
                              wd[128 * r:128 * (r + 1), c0:c0 + CW]))
        for i, (r, c0) in enumerate(subs):
            sl = tw[:, so + (i % 2) * CW:so + (i % 2) * CW + CW]
            slb = twb[:, so + (i % 2) * CW:so + (i % 2) * CW + CW]
            t1 = P_.G(g, "tensor_scalar", sl, sl, rvec, MAGIC, ALU.mult, ALU.add,
                      waits=(lts[i],))
            t2 = P_.G(g, "tensor_scalar", sl, sl, MAGIC, 1.0, ALU.subtract, ALU.min,
                      waits=(t1,))
            w3 = (t2,) if i < 2 else (t2, stores[i - 2])
            t3 = P_.G(g, "tensor_scalar", slb, sl, -1.0, None, ALU.max, waits=w3)
            st4 = []
            for jj in range(CW // 128):
                rr0 = (c0 // 128 + jj) * 128
                st4.append(P_.dma(g, "tg", wtd[rr0:rr0 + 128, 128 * r:128 * (r + 1)],
                                  twb[:, so + (i % 2) * CW + 128 * jj:so + (i % 2) * CW + 128 * (jj + 1)]))
            stores.append(tuple(st4))
            if i + 2 < len(subs):
                r2, c02 = subs[i + 2]
                lts.append(P_.dma(g, "tg", tw[:, so + (i % 2) * CW:so + (i % 2) * CW + CW],
                                  wd[128 * r2:128 * (r2 + 1), c02:c02 + CW]))
    return P_.dtok("tg")


def seg_ffn1_mix(P_, qt, dep_toks, xoff=0):
    b = P_.block
    cfg = P_.cfg
    TOK, QT, DCH = cfg["TOK"], cfg["QT"], cfg["DCH"]
    ln = P_.B["ln_at"]
    xr, xk = P_.B["xr_sb"], P_.B["xk_sb"]
    bnd = P_.B["bnd_sb"]
    mu_r, mu_k = P_.B["mu_r"], P_.B["mu_k"]
    sc1 = P_.B["mx_sc"]
    q0 = qt * QT
    toks = []
    @b.vector
    def _(v):
        tk = None
        for c in range(DCH):
            a = ln[:, c * TOK + q0:c * TOK + q0 + QT]
            xrs = xr[:, xoff + c * QT:xoff + (c + 1) * QT]
            xks = xk[:, xoff + c * QT:xoff + (c + 1) * QT]
            if q0 > 0:
                ap = ln[:, c * TOK + q0 - 1:c * TOK + q0 + QT - 1]
                d = P_.V(v, "tensor_tensor", sc1[:, 0:QT], a, ap, ALU.subtract,
                         waits=tuple(dep_toks) + ((tk,) if tk else ()))
                m1 = P_.V(v, "tensor_scalar", sc1[:, QT:2 * QT], sc1[:, 0:QT],
                          mu_r[:, c:c + 1], None, ALU.mult, waits=(d,))
                a1 = P_.V(v, "tensor_tensor", xrs, sc1[:, QT:2 * QT],
                          ap, ALU.add, waits=(m1,))
                m2 = P_.V(v, "tensor_scalar", sc1[:, QT:2 * QT], sc1[:, 0:QT],
                          mu_k[:, c:c + 1], None, ALU.mult, waits=(a1,))
                tk = P_.V(v, "tensor_tensor", xks, sc1[:, QT:2 * QT],
                          ap, ALU.add, waits=(m2,))
            else:
                ap = ln[:, c * TOK:c * TOK + QT - 1]
                aa = ln[:, c * TOK + 1:c * TOK + QT]
                d = P_.V(v, "tensor_tensor", sc1[:, 1:QT], aa, ap, ALU.subtract,
                         waits=tuple(dep_toks) + ((tk,) if tk else ()))
                m1 = P_.V(v, "tensor_scalar", sc1[:, QT + 1:2 * QT], sc1[:, 1:QT],
                          mu_r[:, c:c + 1], None, ALU.mult, waits=(d,))
                a1 = P_.V(v, "tensor_tensor", xrs[:, 1:QT], sc1[:, QT + 1:2 * QT],
                          ap, ALU.add, waits=(m1,))
                m2 = P_.V(v, "tensor_scalar", sc1[:, QT + 1:2 * QT], sc1[:, 1:QT],
                          mu_k[:, c:c + 1], None, ALU.mult, waits=(a1,))
                tk = P_.V(v, "tensor_tensor", xks[:, 1:QT], sc1[:, QT + 1:2 * QT],
                          ap, ALU.add, waits=(m2,))
                a0 = ln[:, c * TOK:c * TOK + 1]
                d0 = P_.V(v, "tensor_tensor", sc1[:, 0:1], a0, bnd[:, c:c + 1],
                          ALU.subtract, waits=(tk,))
                n1 = P_.V(v, "tensor_scalar", sc1[:, 1:2], sc1[:, 0:1], mu_r[:, c:c + 1],
                          None, ALU.mult, waits=(d0,))
                a0 = P_.V(v, "tensor_tensor", xrs[:, 0:1], sc1[:, 1:2],
                          bnd[:, c:c + 1], ALU.add, waits=(n1,))
                n2 = P_.V(v, "tensor_scalar", sc1[:, 1:2], sc1[:, 0:1], mu_k[:, c:c + 1],
                          None, ALU.mult, waits=(a0,))
                tk = P_.V(v, "tensor_tensor", xks[:, 0:1], sc1[:, 1:2],
                          bnd[:, c:c + 1], ALU.add, waits=(n2,))
        toks.append(tk)
    return toks[0]


def seg_store_x(P_, dst_ap, dep_toks, src="x_sb"):
    b = P_.block
    TOK = P_.cfg["TOK"]

    tk = []
    @b.sync
    def _(s):
        P_.wait(s, *dep_toks)
        tk.append(P_.dma(s, "a", dst_ap.rearrange("(c p) t -> p c t", p=128),
                         P_.B[src][:].rearrange("p (c t) -> p c t", t=TOK)))
    return tk[0]


def seg_wo_ln(P_, w_name, bias_name, dep_toks, rs=None):
    cfg = P_.cfg
    TOK = cfg["TOK"]
    QW = 512
    tk = None
    for qp in range(TOK // QW):
        sl = rd = None
        if rs is not None:
            sl, sk, rd = rs.next()
        lt, _, mmo = seg_proj(P_, "ln_at", w_name, 8, bias=bias_name,
                              out_sbuf="ev_sb", out_sbuf_w=QW, qw=QW,
                              src_off=qp * QW, src_w=TOK,
                              dep_toks=dep_toks if qp == 0 else (tk,),
                              evac_dep=(tk,) if tk else (),
                              ring_slots=sl, ring_dep=rd)
        if rs is not None:
            rs.done(sk, mmo)
        tk = seg_postnorm_residual(P_, "ev_sb", QW, (lt,), (qp * QW, (qp + 1) * QW))
    return tk


def _alloc_common(P_, st):
    cfg = P_.cfg
    TOK, QT = cfg["TOK"], cfg["QT"]
    P_.sb("x_sb", [128, 8 * TOK], F32, stack=st)
    P_.sb("ln_at", [128, 8 * TOK], BF16, stack=st)
    P_.sb("st_sb", [1, 4 * QT], F32, stack=st)
    P_.sb("bc_sb", [128, 4 * QT], F32, stack=st)
    for stem in ("a", "wb", "t", "w", "tg"):
        for i in range(P.NDS):
            P_.sem(stem + str(i))
    for s in ("pe", "act", "dve", "gp"):
        P_.sem(s)


def _alloc_proj(P_, st, ring_ch=8, dstage=True, ngrp=2, ring_cols=None):
    cfg = P_.cfg
    TOK, QT = cfg["TOK"], cfg["QT"]
    P_.sb("sq_sb", [128, 16 * QT], BF16, stack=st)
    if ring_cols is None:
        ring_cols = ngrp * ring_ch * 128
    P_.sb("w_ring", [128, ring_cols], BF16, stack=st)
    P_.B["_ring_grpw"] = ring_ch * 128
    if dstage:
        P_.sb("dstage", [128, 2 * TOK], BF16, stack=st)


class RingSched:

    def __init__(self, P_, pairs):
        self.P = P_
        self.pairs = pairs
        self.last = [None] * len(pairs)
        self.i = 0

    def next(self):
        k = self.i % len(self.pairs)
        self.i += 1
        return self.pairs[k], k, self.last[k]

    def done(self, k, mmtoks):
        self.last[k] = mmtoks[-1]


def _load_consts_biases(P_, names, masks=True):
    nl = []
    for n in names:
        shp = list(P_.D[n].shape)
        nl.append((n, n, shp, F32))
    t1 = seg_consts(P_, nl)
    t2 = seg_castload(P_, "c_ones_col", "c_ones_col_bf", [128, 1])
    t3 = seg_castload(P_, "c_ident", "c_ident_bf", [128, 128])
    t4 = t3
    if masks:
        t4 = seg_castload(P_, "c_masks", "c_masks_bf", list(P_.D["c_masks"].shape))
    P_.B["c_ones_row_f"] = P_.B["c_ones_row"]
    P_.B["c_ones_col_f"] = P_.B["c_ones_col"]
    ep = []
    b = P_.block
    @b.vector
    def _(v):
        P_.sb("eps_sb", [1, 1], F32)
        ep.append(P_.V(v, "memset", P_.B["eps_sb"][:], 1e-5))
    return (t1, t2, t3, t4, ep[0])


def _din(name_shape):
    return name_shape[0] // 128


def build_launchA(cfg):
    from contextlib import ExitStack
    nc = bass.Bass("TRN2", target_bir_lowering=False, debug=False)
    TOK = cfg["TOK"]
    with ExitStack() as st:
        P_ = P(nc, cfg, st)
        for n, shp in [("xT", [1024, TOK]), ("w_q0", [1024, 1024]), ("w_k0", [1024, 1024]),
                       ("w_v0", [1024, 1024]), ("w_l0", [1024, 1024]),
                       ("b_q0", [128, 8]), ("b_k0", [128, 8]), ("b_v0", [128, 8]),
                       ("b_l0", [128, 8]), ("c_ident", [128, 128]), ("c_ones_col", [128, 1]),
                       ("c_ones_row", [1, 128]), ("c_masks", [128, 4 * cfg["QT"]])]:
            P_.D[n] = nc.dram_tensor(n, shp, F32, kind="ExternalInput").ap()
        qT = nc.dram_tensor("qT", [1024, TOK], BF16, kind="ExternalOutput").ap()
        kT = nc.dram_tensor("kT", [1024, TOK], BF16, kind="ExternalOutput").ap()
        vTok = nc.dram_tensor("vTok", [TOK, 1024], BF16, kind="ExternalOutput").ap()
        with nc.Block() as blk:
            P_.block = blk
            _alloc_common(P_, st)
            _alloc_proj(P_, st)
            P_.sb("kpre", [128, 8 * TOK], BF16)
            P_.sb("tstage", [128, TOK], BF16)
            P_.sb("tstage2", [128, 128], BF16)
            ct = _load_consts_biases(
                P_, ["b_q0", "b_k0", "b_v0", "b_l0", "c_ones_col", "c_ones_row"],
                masks=False)
            t0 = seg_load_x(P_, P_.D["xT"])
            from contextlib import ExitStack as ES2
            with ES2() as st_p1:
                P_.ps("pg0", [128, min(TOK, 2048)], stack=st_p1)
                P_.ps("pg1", [128, min(TOK, 2048)], stack=st_p1)
                tln = seg_prenorm(P_, "x_sb", (t0,) + ct)
                tq, _, _ = seg_proj(P_, "ln_at", "w_q0", 8, bias="b_q0", out_dram=qT,
                                    dep_toks=(tln,))
                tkp, _, mmk = seg_proj(P_, "ln_at", "w_k0", 8, bias="b_k0",
                                       out_sbuf="kpre", out_sbuf_w=TOK, dep_toks=(tln,))
                tk, _, mml = seg_proj(P_, "kpre", "w_l0", 8, bias="b_l0", out_dram=kT,
                                      dep_toks=(tkp,))
                tvp, _, _ = seg_proj(P_, "ln_at", "w_v0", 8, bias="b_v0",
                                     out_sbuf="kpre", out_sbuf_w=TOK, dep_toks=(tln,),
                                     evac_dep=(mml[-1],))
            with ES2() as st_p2:
                P_.ps("pg0", [128, min(TOK, 2048)], stack=st_p2)
                P_.ps("tr_ps", [128, 256], BF16, stack=st_p2)
                tv, _, _ = seg_proj(P_, "kpre", "w_l0", 8, bias="b_l0",
                                    transpose_dram=vTok, single_group=True,
                                    dep_toks=(tvp,), evac_dep=(tvp,))
            @blk.sync
            def _(s):
                for stem in ("a", "wb", "t"):
                    for i in range(P.NDS):
                        if P_.cnt[stem + str(i)]:
                            s.wait_ge(P_.sems[stem + str(i)], P_.cnt[stem + str(i)])
            @blk.gpsimd
            def _(g):
                for i in range(P.NDS):
                    if P_.cnt["w" + str(i)]:
                        g.wait_ge(P_.sems["w" + str(i)], P_.cnt["w" + str(i)])
    return nc


def build_launchB(cfg):
    from contextlib import ExitStack
    nc = bass.Bass("TRN2", target_bir_lowering=False, debug=False)
    TOK, TB, QT, KB = cfg["TOK"], cfg["TB"], cfg["QT"], cfg["KB"]
    with ExitStack() as st:
        P_ = P(nc, cfg, st)
        for n, shp in [("xT", [1024, TOK]), ("qT", None), ("gK", [4096, TOK]),
                       ("gV", [TB, 1024]),
                       ("w_o0", [1024, 1024]), ("b_o0", [128, 8]),
                       ("w_1T", [1024, 4096]), ("w_2T", [4096, 1024]),
                       ("b_1", [128, 32]), ("b_2", [128, 8]),
                       ("w_q1", [1024, 1024]), ("w_k1", [1024, 1024]), ("w_v1", [1024, 1024]),
                       ("c_ident", [128, 128]), ("c_ones_col", [128, 1]),
                       ("c_ones_row", [1, 128]), ("c_masks", [128, 4 * QT])]:
            if n in ("qT", "gK", "gV"):
                shp = [1024, TOK] if n == "qT" else shp
                P_.D[n] = nc.dram_tensor(n, shp, BF16, kind="ExternalInput").ap()
            else:
                P_.D[n] = nc.dram_tensor(n, shp, F32, kind="ExternalInput").ap()
        P_.D["q_dram"] = P_.D["qT"]
        x2T = nc.dram_tensor("x2T", [1024, TOK], F32, kind="ExternalOutput").ap()
        qT2 = nc.dram_tensor("qT2", [1024, TOK], BF16, kind="ExternalOutput").ap()
        kT2 = nc.dram_tensor("kT2", [1024, TOK], BF16, kind="ExternalOutput").ap()
        vTok2 = nc.dram_tensor("vTok2", [TOK, 1024], BF16, kind="ExternalOutput").ap()
        P_.D["w1t"] = nc.dram_tensor("w1t", [1024, 4096], BF16).ap()
        P_.D["w2t"] = nc.dram_tensor("w2t", [4096, 1024], BF16).ap()
        with nc.Block() as blk:
            P_.block = blk
            _alloc_common(P_, st)
            ct = _load_consts_biases(
                P_, ["b_o0", "b_1", "b_2", "c_ones_col", "c_ones_row"])
            t0 = seg_load_x(P_, P_.D["xT"])
            from contextlib import ExitStack as ES2
            with ES2() as st_t:
                P_.sb("tw_sb", [128, 4 * 512], F32, stack=st_t)
                P_.sb("twb_sb", [128, 4 * 512], BF16, stack=st_t)
                P_.sb("pr_sb", [128, 160], F32, stack=st_t)
                tctx1 = seg_ternary_abs(P_, "w_1T", ct, stage_off=0, po=0)
                tctx2 = seg_ternary_abs(P_, "w_2T", ct, stage_off=2 * 512, po=80)
                tern_stores = []

                def tern_hook(last):
                    rv1 = seg_ternary_finish(P_, tctx1, P_.B["avA"], last)
                    rv2 = seg_ternary_finish(P_, tctx2, P_.B["avB"], rv1)
                    tern_stores.append(seg_ternary_apply(P_, tctx1, "w1t", rv1))
                    tern_stores.append(seg_ternary_apply(P_, tctx2, "w2t", rv2))
                    return rv2

                with ES2() as st_a:
                    P_.sb("kpa", [128, KB * 128], BF16, stack=st_a)
                    P_.sb("kpb", [128, KB * 128], BF16, stack=st_a)
                    P_.sb("vp", [128, KB * 130], BF16, stack=st_a)
                    P_.sb("qp", [128, TOK], BF16, stack=st_a)
                    P_.sb("av_sb", [128, 2 * 2 * QT], F32, stack=st_a)
                    P_.B["rec_sb"] = P_.B["st_sb"]
                    P_.sb("ex_sb", [128, 2 * 1536], BF16, stack=st_a)
                    P_.ps("sc0", [128, 1536], stack=st_a)
                    P_.ps("sc1", [128, 1536], stack=st_a)
                    P_.ps("avA", [128, QT], stack=st_a)
                    P_.ps("avB", [128, QT], stack=st_a)
                    zt = []
                    @blk.vector
                    def _(v):
                        zt.append(P_.V(v, "memset", P_.B["kpa"][64:128, :], 0.0,
                                       waits=(P_.dtok("a"), P_.dtok("wb"), P_.tok("pe"))))
                        zt.append(P_.V(v, "memset", P_.B["kpb"][0:64, :], 0.0))
                    ta = seg_attn_l0(P_, (t0, zt[1], ct[0]), mid_hook=tern_hook)
            with ES2() as st_w:
                _alloc_proj(P_, st_w, ring_ch=32)
                P_.sb("ev_sb", [128, 8 * 512], F32, stack=st_w)
                P_.ps("pg0", [128, min(TOK, 2048)], stack=st_w)
                P_.ps("pg1", [128, min(TOK, 2048)], stack=st_w)
                tx1 = seg_wo_ln(P_, "w_o0", "b_o0", (ta, t0, P_.dtok("t"), P_.tok("gp")))
                tln2 = seg_prenorm(P_, "x_sb", (tx1,))
                tk = tln2
                for qt in range(cfg["NQT"]):
                    P_.sb("h_sb", [128, 32 * QT], BF16, stack=st_w) if qt == 0 else None
                    th, _, _ = seg_proj(P_, "ln_at", "w1t", 32, bias="b_1", act=AF.Gelu,
                                        out_sbuf="h_sb", out_sbuf_w=QT, qw=QT,
                                        src_off=qt * QT, src_w=TOK, w_is_bf16=True,
                                        dep_toks=(tk,) + tuple(tern_stores))
                    to, _, _ = seg_proj(P_, "h_sb", "w2t", 8, bias="b_2",
                                        out_sbuf="ev_sb", out_sbuf_w=QT, qw=QT,
                                        src_w=QT, w_is_bf16=True, dep_toks=(th,),
                                        evac_dep=(tk,))
                    tk = seg_postnorm_residual(P_, "ev_sb", QT, (to,),
                                               (qt * QT, (qt + 1) * QT))
                tln3 = seg_prenorm(P_, "x_sb", (tk,))
                tx2 = seg_store_x(P_, x2T, (tk,))
                tq2, _, _ = seg_proj(P_, "ln_at", "w_q1", 8, out_dram=qT2, dep_toks=(tln3,))
                tk2, _, mmk2 = seg_proj(P_, "ln_at", "w_k1", 8, out_dram=kT2, dep_toks=(tln3,))
            with ES2() as st_v:
                P_.sb("w_ring", [128, 2 * 8 * 128], BF16, stack=st_v)
                P_.sb("tstage", [128, TOK], BF16, stack=st_v)
                P_.sb("tstage2", [128, 128], BF16, stack=st_v)
                P_.ps("pg0", [128, min(TOK, 2048)], stack=st_v)
                P_.ps("tr_ps", [128, 256], BF16, stack=st_v)
                tv2, _, _ = seg_proj(P_, "ln_at", "w_v1", 8, transpose_dram=vTok2,
                                     single_group=True, dep_toks=(tln3,))
            @blk.sync
            def _(s):
                for stem in ("a", "wb", "t"):
                    for i in range(P.NDS):
                        if P_.cnt[stem + str(i)]:
                            s.wait_ge(P_.sems[stem + str(i)], P_.cnt[stem + str(i)])
            @blk.gpsimd
            def _(g):
                for i in range(P.NDS):
                    if P_.cnt["w" + str(i)]:
                        g.wait_ge(P_.sems["w" + str(i)], P_.cnt["w" + str(i)])
    return nc


def build_launchC1(cfg):
    from contextlib import ExitStack
    nc = bass.Bass("TRN2", target_bir_lowering=False, debug=False)
    TOK, TB, QT, KB = cfg["TOK"], cfg["TB"], cfg["QT"], cfg["KB"]
    with ExitStack() as st:
        P_ = P(nc, cfg, st)
        P_.D["qprs"] = nc.dram_tensor("qprs", [256, TB], BF16, kind="ExternalInput").ap()
        P_.D["kprs"] = nc.dram_tensor("kprs", [256, TB], BF16, kind="ExternalInput").ap()
        P_.D["vprs"] = nc.dram_tensor("vprs", [2 * TB, 128], BF16, kind="ExternalInput").ap()
        for n, shp in [("c_ident", [128, 128]), ("c_ones_col", [128, 1]),
                       ("c_ones_row", [1, 128]), ("c_masks", [128, 4 * QT])]:
            P_.D[n] = nc.dram_tensor(n, shp, F32, kind="ExternalInput").ap()
        a4 = nc.dram_tensor("attn4T", [256, TB], BF16, kind="ExternalOutput").ap()
        with nc.Block() as blk:
            P_.block = blk
            TOKSAVE = cfg["TOK"]
            _alloc_common_small(P_, st, cfg)
            ct = _load_consts_biases(P_, ["c_ones_col", "c_ones_row"])
            P_.sb("kpa", [128, KB * 128], BF16)
            P_.sb("kpb", [128, KB * 128], BF16)
            P_.sb("vp", [128, KB * 130], BF16)
            P_.sb("qp", [128, TB], BF16)
            P_.sb("av_sb", [128, 2 * 2 * QT], F32)
            P_.sb("rec_sb", [1, 2 * 2 * QT], F32)
            P_.sb("ex_sb", [128, 2 * 1536], BF16)
            P_.sb("ln_at", [128, 2 * TB], BF16)
            P_.ps("sc0", [128, 1536])
            P_.ps("sc1", [128, 1536])
            P_.ps("avA", [128, QT])
            P_.ps("avB", [128, QT])
            zt = []
            @blk.vector
            def _(v):
                zt.append(P_.V(v, "memset", P_.B["kpa"][64:128, :], 0.0))
                zt.append(P_.V(v, "memset", P_.B["kpb"][0:64, :], 0.0))
            ta = seg_attn_l1(P_, (zt[1],) + ct)
            for bp in range(2):
                @blk.sync
                def _(s):
                    P_.wait(s, ta)
                    P_.dma(s, "a", a4[128 * bp:128 * (bp + 1), :],
                           P_.B["ln_at"][:, bp * TB:(bp + 1) * TB])
            @blk.sync
            def _(s):
                for stem in ("a", "wb", "t"):
                    for i in range(P.NDS):
                        if P_.cnt[stem + str(i)]:
                            s.wait_ge(P_.sems[stem + str(i)], P_.cnt[stem + str(i)])
            @blk.gpsimd
            def _(g):
                for i in range(P.NDS):
                    if P_.cnt["w" + str(i)]:
                        g.wait_ge(P_.sems["w" + str(i)], P_.cnt["w" + str(i)])
    return nc


def _alloc_common_small(P_, st, cfg):
    QT = cfg["QT"]
    P_.sb("bc_sb", [128, 4 * QT], F32, stack=st)
    for stem in ("a", "wb", "t", "w"):
        for i in range(P.NDS):
            P_.sem(stem + str(i))
    for s in ("pe", "act", "dve", "gp"):
        P_.sem(s)


def build_launchC2(cfg):
    from contextlib import ExitStack
    nc = bass.Bass("TRN2", target_bir_lowering=False, debug=False)
    TOK, QT = cfg["TOK"], cfg["QT"]
    with ExitStack() as st:
        P_ = P(nc, cfg, st)
        P_.D["xT"] = nc.dram_tensor("xT", [1024, TOK], F32, kind="ExternalInput").ap()
        P_.D["attnT"] = nc.dram_tensor("attnT", [1024, TOK], BF16, kind="ExternalInput").ap()
        for n, shp in [("w_o1", [1024, 1024]), ("b_o1", [128, 8]),
                       ("c_ident", [128, 128]), ("c_ones_col", [128, 1]),
                       ("c_ones_row", [1, 128]), ("c_masks", [128, 4 * QT])]:
            P_.D[n] = nc.dram_tensor(n, shp, F32, kind="ExternalInput").ap()
        x3T = nc.dram_tensor("x3T", [1024, TOK], F32, kind="ExternalOutput").ap()
        ln4T = nc.dram_tensor("ln4T", [1024, TOK], BF16, kind="ExternalOutput").ap()
        with nc.Block() as blk:
            P_.block = blk
            _alloc_common(P_, st)
            _alloc_proj(P_, st)
            P_.sb("ev_sb", [128, 8 * 512], F32)
            P_.ps("pg0", [128, min(TOK, 2048)])
            P_.ps("pg1", [128, min(TOK, 2048)])
            ct = _load_consts_biases(P_, ["b_o1", "c_ones_col", "c_ones_row"],
                                     masks=False)
            t0 = seg_load_x(P_, P_.D["xT"])
            tat = []
            @blk.sync
            def _(s):
                tat.append(P_.dma(s, "a",
                                  P_.B["ln_at"][:].rearrange("p (c t) -> p c t", t=TOK),
                                  P_.D["attnT"].rearrange("(c p) t -> p c t", p=128)))
            tx3 = seg_wo_ln(P_, "w_o1", "b_o1", (t0, tat[0]) + ct)
            ts3 = seg_store_x(P_, x3T, (tx3,))
            tln4 = seg_prenorm(P_, "x_sb", (tx3,))
            @blk.sync
            def _(s):
                P_.wait(s, tln4)
                P_.dma(s, "a", ln4T.rearrange("(c p) t -> p c t", p=128),
                       P_.B["ln_at"][:].rearrange("p (c t) -> p c t", t=TOK))
            @blk.sync
            def _(s):
                for stem in ("a", "wb", "t"):
                    for i in range(P.NDS):
                        if P_.cnt[stem + str(i)]:
                            s.wait_ge(P_.sems[stem + str(i)], P_.cnt[stem + str(i)])
            @blk.gpsimd
            def _(g):
                for i in range(P.NDS):
                    if P_.cnt["w" + str(i)]:
                        g.wait_ge(P_.sems["w" + str(i)], P_.cnt["w" + str(i)])
    return nc


def build_launchD(cfg):
    from contextlib import ExitStack
    nc = bass.Bass("TRN2", target_bir_lowering=False, debug=False)
    TOK, QT = cfg["TOK"], cfg["QT"]
    with ExitStack() as st:
        P_ = P(nc, cfg, st)
        P_.D["xT"] = nc.dram_tensor("xT", [1024, TOK], F32, kind="ExternalInput").ap()
        P_.D["ln4T"] = nc.dram_tensor("ln4T", [1024, TOK], BF16, kind="ExternalInput").ap()
        P_.D["bnd"] = nc.dram_tensor("bnd", [128, 8], BF16, kind="ExternalInput").ap()
        for n, shp in [("w_r1T", [1024, 1024]), ("w_k1T", [1024, 1024]),
                       ("w_v1T", [1024, 1024]), ("mu_r", [128, 8]), ("mu_k", [128, 8]),
                       ("c_ident", [128, 128]), ("c_ones_col", [128, 1]),
                       ("c_ones_row", [1, 128]), ("c_masks", [128, 4 * QT])]:
            P_.D[n] = nc.dram_tensor(n, shp, F32, kind="ExternalInput").ap()
        outT = nc.dram_tensor("outT", [1024, TOK], F32, kind="ExternalOutput").ap()
        with nc.Block() as blk:
            P_.block = blk
            _alloc_common(P_, st)
            _alloc_proj(P_, st, dstage=False)
            P_.sb("ev_sb", [128, 8 * QT], F32)
            P_.sb("xr_sb", [128, 8 * QT], BF16)
            P_.sb("xk_sb", [128, 8 * QT], BF16)
            P_.sb("r_sb", [128, 8 * QT], F32)
            P_.sb("ka_sb", [128, 8 * QT], BF16)
            P_.sb("tstage", [128, QT], BF16)
            P_.ps("pg0", [128, min(TOK, 2048)])
            P_.ps("pg1", [128, min(TOK, 2048)])
            ct = _load_consts_biases(P_, ["mu_r", "mu_k", "c_ones_col", "c_ones_row"],
                                          masks=False)
            t0 = seg_load_x(P_, P_.D["xT"])
            lt = []
            @blk.sync
            def _(s):
                lt.append(P_.dma(s, "a",
                                 P_.B["ln_at"][:].rearrange("p (c t) -> p c t", t=TOK),
                                 P_.D["ln4T"].rearrange("(c p) t -> p c t", p=128)))
            bl = []
            @blk.sync
            def _(s):
                P_.sb("bnd_sb", [128, 8], BF16)
                bl.append(P_.dma(s, "a", P_.B["bnd_sb"][:], P_.D["bnd"]))
            tk = None
            for qt in range(cfg["NQT"]):
                tm = seg_ffn1_mix(P_, qt, (lt[0], bl[0]) + ct + ((tk,) if tk else ()))
                tr, _, _ = seg_proj(P_, "xr_sb", "w_r1T", 8, act=AF.Sigmoid,
                                    out_sbuf="r_sb", out_sbuf_w=QT, qw=QT, src_w=QT,
                                    dep_toks=(tm,))
                tka, _, _ = seg_proj(P_, "xk_sb", "w_k1T", 8, evac_relu_sq=True,
                                     out_sbuf="ka_sb", out_sbuf_w=QT, qw=QT, src_w=QT,
                                     dep_toks=(tm,), evac_dep=(tk,) if tk else ())
                to, _, _ = seg_proj(P_, "ka_sb", "w_v1T", 8, evac_mult=("r_sb", QT),
                                    out_sbuf="ev_sb", out_sbuf_w=QT, qw=QT, src_w=QT,
                                    dep_toks=(tka,), evac_dep=((tk,) if tk else ()) + (tr,))
                tk = seg_postnorm_residual(P_, "ev_sb", QT, (to,),
                                           (qt * QT, (qt + 1) * QT))
            tso = seg_store_x(P_, outT, (tk,))
            @blk.sync
            def _(s):
                for stem in ("a", "wb", "t"):
                    for i in range(P.NDS):
                        if P_.cnt[stem + str(i)]:
                            s.wait_ge(P_.sems[stem + str(i)], P_.cnt[stem + str(i)])
            @blk.gpsimd
            def _(g):
                for i in range(P.NDS):
                    if P_.cnt["w" + str(i)]:
                        g.wait_ge(P_.sems["w" + str(i)], P_.cnt["w" + str(i)])
    return nc


_NC_CACHE = {}
_EXEC = None


def _run(key, build_fn, cfg, in_maps):
    nc = _NC_CACHE.get(key)
    if nc is None:
        nc = build_fn(cfg)
        _NC_CACHE[key] = nc
    if _EXEC is not None:
        return _EXEC(nc, in_maps)
    res = run_bass_kernel_spmd(nc, in_maps, core_ids=list(range(8)))
    return res.results


def _pairs_of(core):
    return (core % 4, 4 + core % 4)


def _fused_idx(cfg, c):
    TOK, TB, KB = cfg["TOK"], cfg["TB"], cfg["KB"]
    VG = TOK // 1024
    GPB = TOK // 64
    p0 = c % 4
    kq = np.zeros((128, 24), dtype=np.uint32)
    for bp in range(2):
        pair = p0 + 4 * bp
        for rr in range(4):
            kq[:, bp * 12 + rr] = 1024 * rr + 128 * pair + np.arange(128)
            kq[0:64, bp * 12 + 4 + rr] = 1024 * rr + 128 * pair + np.arange(64)
            kq[64:128, bp * 12 + 4 + rr] = kq[0:64, bp * 12 + 4 + rr]
            kq[0:64, bp * 12 + 8 + rr] = 1024 * rr + 128 * pair + 64 + np.arange(64)
            kq[64:128, bp * 12 + 8 + rr] = kq[0:64, bp * 12 + 8 + rr]
    vv = np.zeros((128, 256), dtype=np.uint32)
    for bp in range(2):
        pair = p0 + 4 * bp
        for ka in range(KB):
            t = 128 * ka + np.arange(128)
            rr = t // TOK
            tl = t % TOK
            row = 1024 * rr + tl // VG
            for h in (0, 1):
                vv[:, bp * 128 + 2 * ka + h] = GPB * row + 16 * (tl % VG) + 2 * pair + h
    cc = np.zeros((128, 8), dtype=np.uint32)
    for ch in range(8):
        row = 128 * (ch % 4) + np.arange(128)
        cc[:, ch] = 4 * row + (c % 4)
    bi = np.zeros((128, 8), dtype=np.uint32)
    if c % 4 != 0:
        for ccx in range(8):
            bi[:, ccx] = 1024 * (c - 1) + 128 * ccx + np.arange(128)
    bv = np.zeros((128, 8), dtype=np.float32) if c % 4 == 0 else np.ones((128, 8), dtype=np.float32)
    return dict(l1kq_idx=kq, l1v_idx=vv, cc_idx=cc, bnd_idx=bi, bnd_valid=bv)


def kernel(**inputs):
    cfg = CFG
    TOK, TB = cfg["TOK"], cfg["TB"]
    W = _prep_weights(inputs)
    C = _host_consts(cfg)
    x = np.asarray(inputs["L0_attn_x"], dtype=np.float32)
    xTs = [_slice_x(x, c, cfg) for c in range(8)]

    if FUSED:
        wnames = ["w_q0", "w_k0", "w_v0", "w_l0", "w_o0", "b_q0", "b_k0", "b_v0",
                  "b_l0", "b_o0", "w_1T", "w_2T", "b_1", "b_2", "w_q1", "w_k1",
                  "w_v1", "w_o1", "b_o1", "w_r1T", "w_k1T", "w_v1T", "mu_r", "mu_k"]
        inF = []
        for c in range(8):
            d = {k: W[k] for k in wnames}
            d.update({k: C[k] for k in ("c_ident", "c_ones_col", "c_ones_row", "c_masks")})
            d["xT"] = xTs[c]
            d.update(_fused_idx(cfg, c))
            inF.append(d)
        rF = _run("F", build_fused, cfg, inF)
        out = np.empty((2, 4 * TOK, 1024), dtype=np.float32)
        for c in range(8):
            b, r = c // 4, c % 4
            out[b, TOK * r:TOK * (r + 1), :] = np.asarray(rF[c]["outT"], dtype=np.float32).T
        return out

    def m(names, extra):
        base = {k: W[k] for k in names if k in W}
        base.update({k: C[k] for k in ("c_ident", "c_ones_col", "c_ones_row", "c_masks")})
        base.update(extra)
        return base

    inA = [m(["w_q0", "w_k0", "w_v0", "w_l0", "b_q0", "b_k0", "b_v0", "b_l0"],
             {"xT": xTs[c]}) for c in range(8)]
    rA = _run("A", build_launchA, cfg, inA)

    gK = [np.concatenate([np.asarray(rA[4 * b + rr]["kT"]) for rr in range(4)], axis=0)
          for b in range(2)]
    gV = [np.concatenate([np.asarray(rA[4 * b + rr]["vTok"]) for rr in range(4)], axis=0)
          for b in range(2)]

    inB = [m(["w_o0", "b_o0", "w_1T", "w_2T", "b_1", "b_2", "w_q1", "w_k1", "w_v1"],
             {"xT": xTs[c], "qT": np.asarray(rA[c]["qT"]),
              "gK": gK[c // 4], "gV": gV[c // 4]}) for c in range(8)]
    rB = _run("B", build_launchB, cfg, inB)

    Kb = [np.concatenate([np.asarray(rB[4 * b + rr]["kT2"]) for rr in range(4)], axis=1)
          for b in range(2)]
    Qb = [np.concatenate([np.asarray(rB[4 * b + rr]["qT2"]) for rr in range(4)], axis=1)
          for b in range(2)]
    Vb = [np.concatenate([np.asarray(rB[4 * b + rr]["vTok2"]) for rr in range(4)], axis=0)
          for b in range(2)]
    inC1 = []
    for c in range(8):
        b = c // 4
        p0, p1 = _pairs_of(c)
        inC1.append(m([], {
            "qprs": np.ascontiguousarray(
                np.concatenate([Qb[b][128 * p0:128 * p0 + 128],
                                Qb[b][128 * p1:128 * p1 + 128]], axis=0)),
            "kprs": np.ascontiguousarray(
                np.concatenate([Kb[b][128 * p0:128 * p0 + 128],
                                Kb[b][128 * p1:128 * p1 + 128]], axis=0)),
            "vprs": np.ascontiguousarray(
                np.concatenate([Vb[b][:, 128 * p0:128 * p0 + 128],
                                Vb[b][:, 128 * p1:128 * p1 + 128]], axis=0)),
        }))
    rC1 = _run("C1", build_launchC1, cfg, inC1)

    attn_b = []
    for b in range(2):
        rows = []
        for p in range(8):
            core = 4 * b + p % 4
            bp = p // 4
            rows.append(np.asarray(rC1[core]["attn4T"])[128 * bp:128 * (bp + 1), :])
        attn_b.append(np.concatenate(rows, axis=0))
    inC2 = [m(["w_o1", "b_o1"],
              {"xT": np.asarray(rB[c]["x2T"]),
               "attnT": np.ascontiguousarray(attn_b[c // 4][:, TOK * (c % 4):TOK * (c % 4 + 1)])})
            for c in range(8)]
    rC2 = _run("C2", build_launchC2, cfg, inC2)

    inD = []
    for c in range(8):
        if c % 4 == 0:
            bnd = np.zeros((128, 8), dtype=_BF)
        else:
            col = np.asarray(rC2[c - 1]["ln4T"])[:, TOK - 1]
            bnd = np.ascontiguousarray(col.reshape(8, 128).T)
        inD.append(m(["w_r1T", "w_k1T", "w_v1T", "mu_r", "mu_k"],
                     {"xT": np.asarray(rC2[c]["x3T"]),
                      "ln4T": np.asarray(rC2[c]["ln4T"]), "bnd": bnd}))
    rD = _run("D", build_launchD, cfg, inD)

    out = np.empty((2, 4 * TOK, 1024), dtype=np.float32)
    for c in range(8):
        b, r = c // 4, c % 4
        out[b, TOK * r:TOK * (r + 1), :] = np.asarray(rD[c]["outT"], dtype=np.float32).T
    return out


def build_fused(cfg):
    from contextlib import ExitStack, ExitStack as ES2
    nc = bass.Bass("TRN2", target_bir_lowering=False, debug=False)
    TOK, TB, QT, KB = cfg["TOK"], cfg["TB"], cfg["QT"], cfg["KB"]
    U32 = mybir.dt.uint32
    with ExitStack() as st:
        P_ = P(nc, cfg, st)
        for n, shp in [("xT", [1024, TOK]),
                       ("w_q0", [1024, 1024]), ("w_k0", [1024, 1024]),
                       ("w_v0", [1024, 1024]), ("w_l0", [1024, 1024]), ("w_o0", [1024, 1024]),
                       ("b_q0", [128, 8]), ("b_k0", [128, 8]), ("b_v0", [128, 8]),
                       ("b_l0", [128, 8]), ("b_o0", [128, 8]),
                       ("w_1T", [1024, 4096]), ("w_2T", [4096, 1024]),
                       ("b_1", [128, 32]), ("b_2", [128, 8]),
                       ("w_q1", [1024, 1024]), ("w_k1", [1024, 1024]), ("w_v1", [1024, 1024]),
                       ("w_o1", [1024, 1024]), ("b_o1", [128, 8]),
                       ("w_r1T", [1024, 1024]), ("w_k1T", [1024, 1024]), ("w_v1T", [1024, 1024]),
                       ("mu_r", [128, 8]), ("mu_k", [128, 8]), ("bnd_valid", [128, 8]),
                       ("c_ident", [128, 128]), ("c_ones_col", [128, 1]),
                       ("c_ones_row", [1, 128]), ("c_masks", [128, 4 * QT])]:
            P_.D[n] = nc.dram_tensor(n, shp, F32, kind="ExternalInput").ap()
        for n, shp in [("l1kq_idx", [128, 24]), ("l1v_idx", [128, 256]),
                       ("cc_idx", [128, 8]), ("bnd_idx", [128, 8])]:
            P_.D[n] = nc.dram_tensor(n, shp, U32, kind="ExternalInput").ap()
        outT = nc.dram_tensor("outT", [1024, TOK], F32, kind="ExternalOutput").ap()
        P_.D["q_dram"] = nc.dram_tensor("qdr", [1024, TOK], BF16).ap()
        VG = TOK // 1024
        ccA_in = nc.dram_tensor("ccA_in", [1024, TOK], BF16).ap()
        ccA_out = nc.dram_tensor("ccA_out", [4096, TOK], BF16).ap()
        ccAv_in = nc.dram_tensor("ccAv_in", [1024, TOK], BF16).ap()
        ccAv_out = nc.dram_tensor("ccAv_out", [4096, TOK], BF16).ap()
        ccBq_in = nc.dram_tensor("ccBq_in", [1024, TOK], BF16).ap()
        ccBq_out = nc.dram_tensor("ccBq_out", [4096, TOK], BF16).ap()
        ccBk_in = nc.dram_tensor("ccBk_in", [1024, TOK], BF16).ap()
        ccBk_out = nc.dram_tensor("ccBk_out", [4096, TOK], BF16).ap()
        ccBv_in = nc.dram_tensor("ccBv_in", [1024, TOK], BF16).ap()
        ccBv_out = nc.dram_tensor("ccBv_out", [4096, TOK], BF16).ap()
        ccC_in = [nc.dram_tensor(f"ccC{j}_in", [128, TB], BF16).ap() for j in (0, 1)]
        ccC_out = [nc.dram_tensor(f"ccC{j}_out", [512, TB], BF16).ap() for j in (0, 1)]
        ccD_in = nc.dram_tensor("ccD_in", [8, 128], BF16).ap()
        ccD_out = nc.dram_tensor("ccD_out", [64, 128], BF16).ap()
        P_.D["w1t"] = nc.dram_tensor("w1t", [4096, 1024], BF16).ap()
        P_.D["w2t"] = nc.dram_tensor("w2t", [1024, 4096], BF16).ap()
        P_.D["gK"] = ccA_out
        P_.D["gKs"] = 1024
        P_.D["gVf"] = lambda rr: ccAv_out[1024 * rr:1024 * (rr + 1), :] \
            .rearrange("r (g d) -> (r g) d", g=VG)

        with nc.Block() as blk:
            P_.block = blk
            _alloc_common(P_, st)
            for s in ("ccA", "ccAv", "ccB", "ccBq", "ccBk", "ccC", "ccC1", "ccD"):
                P_.sem(s)
            ct = _load_consts_biases(
                P_, ["b_q0", "b_k0", "b_v0", "b_l0", "b_o0", "b_1", "b_2", "b_o1",
                     "mu_r", "mu_k", "bnd_valid", "c_ones_col", "c_ones_row"],
                masks=False)
            ix = seg_consts(P_, [("l1kq_idx", "l1kq_idx", [128, 24], U32),
                                 ("l1v_idx", "l1v_idx", [128, 256], U32),
                                 ("cc_idx", "cc_idx", [128, 8], U32),
                                 ("bnd_idx", "bnd_idx", [128, 8], U32)])
            t0 = seg_load_x(P_, P_.D["xT"])
            with ES2() as st_pre:
                _alloc_proj(P_, st_pre, ngrp=4)
                rs = RingSched(P_, [(0, 1024), (2048, 3072)])
                P_.sb("kpre", [128, 8 * TOK], BF16, stack=st_pre)
                with ES2() as st_p1:
                    P_.ps("pg0", [128, min(TOK, 2048)], stack=st_p1)
                    P_.ps("pg1", [128, min(TOK, 2048)], stack=st_p1)
                    tln = seg_prenorm(P_, "x_sb", (t0,) + ct + (ix,))
                    sl, sk, rd = rs.next()
                    tkp, _, mmk = seg_proj(P_, "ln_at", "w_k0", 8, bias="b_k0",
                                           out_sbuf="kpre", out_sbuf_w=TOK, dep_toks=(tln,),
                                           ring_slots=sl, ring_dep=rd)
                    rs.done(sk, mmk)
                    sl, sk, rd = rs.next()
                    tk, _, mml = seg_proj(P_, "kpre", "w_l0", 8, bias="b_l0",
                                          out_dram=ccA_in, dep_toks=(tkp,),
                                          ring_slots=sl, ring_dep=rd)
                    rs.done(sk, mml)
                    @blk.gpsimd
                    def _(g):
                        P_.wait(g, tk, P_.dtok("a"))
                        insg = g.collective_compute(
                            "AllGather", ALU.bypass,
                            replica_groups=[[0, 1, 2, 3], [4, 5, 6, 7]],
                            ins=[ccA_in.opt()], outs=[ccA_out.opt()])
                        insg.then_inc(P_.sems["ccA"])
                        P_.cnt["ccA"] = 1
                    sl, sk, rd = rs.next()
                    tq, _, mmq = seg_proj(P_, "ln_at", "w_q0", 8, bias="b_q0",
                                          out_dram=P_.D["q_dram"], dep_toks=(tln,),
                                          ring_slots=sl, ring_dep=rd)
                    rs.done(sk, mmq)
                    sl, sk, rd = rs.next()
                    tvp, _, mmv = seg_proj(P_, "ln_at", "w_v0", 8, bias="b_v0",
                                           out_sbuf="kpre", out_sbuf_w=TOK, dep_toks=(tln,),
                                           evac_dep=(mml[-1],),
                                           ring_slots=sl, ring_dep=rd)
                    rs.done(sk, mmv)
                with ES2() as st_p2:
                    P_.ps("pg0", [128, min(TOK, 2048)], stack=st_p2)
                    P_.ps("tr_ps", [128, 256], BF16, stack=st_p2)
                    P_.sb("tstage", [128, TOK], BF16, stack=st_p2)
                    P_.sb("tstage2", [128, 128], BF16, stack=st_p2)
                    sl, sk, rd = rs.next()
                    tv, _, _ = seg_proj(
                        P_, "kpre", "w_l0", 8, bias="b_l0",
                        transpose_dram=ccAv_in.rearrange("r (g d) -> (r g) d", g=VG),
                        single_group=True, dep_toks=(tvp,), evac_dep=(tvp,),
                        ring_slots=sl, ring_dep=rd)
            @blk.gpsimd
            def _(g):
                P_.wait(g, tv, P_.dtok("a"))
                ins = g.collective_compute(
                    "AllGather", ALU.bypass,
                    replica_groups=[[0, 1, 2, 3], [4, 5, 6, 7]],
                    ins=[ccAv_in.opt()], outs=[ccAv_out.opt()])
                ins.then_inc(P_.sems["ccAv"])
                P_.cnt["ccAv"] = 1

            with ES2() as st_t:
                P_.sb("tw_sb", [128, 4 * 512], F32, stack=st_t)
                P_.sb("twb_sb", [128, 4 * 512], BF16, stack=st_t)
                P_.sb("pr_sb", [128, 160], F32, stack=st_t)
                tctx1 = seg_ternary_abs(P_, "w_1T", ct, stage_off=0, po=0)
                tctx2 = seg_ternary_abs(P_, "w_2T", ct, stage_off=2 * 512, po=80)
                tern_stores = []

                def tern_hook(last):
                    rv1 = seg_ternary_finish(P_, tctx1, P_.B["avA"], last)
                    rv2 = seg_ternary_finish(P_, tctx2, P_.B["avB"], rv1)
                    tern_stores.append(seg_ternary_apply(P_, tctx1, "w1t", rv1))
                    tern_stores.append(seg_ternary_apply(P_, tctx2, "w2t", rv2))
                    return rv2

                with ES2() as st_a:
                    P_.sb("kpa", [128, KB * 128], BF16, stack=st_a)
                    P_.sb("kpb", [128, KB * 128], BF16, stack=st_a)
                    P_.sb("vp", [128, KB * 130], BF16, stack=st_a)
                    P_.sb("qp", [128, TOK], BF16, stack=st_a)
                    P_.sb("av_sb", [128, 2 * 2 * QT], F32, stack=st_a)
                    P_.B["rec_sb"] = P_.B["st_sb"]
                    P_.sb("ex_sb", [128, 2 * 1536], BF16, stack=st_a)
                    P_.ps("sc0", [128, 1536], stack=st_a)
                    P_.ps("sc1", [128, 1536], stack=st_a)
                    P_.ps("avA", [128, QT], stack=st_a)
                    P_.ps("avB", [128, QT], stack=st_a)
                    zt = []
                    @blk.vector
                    def _(v):
                        zt.append(P_.V(v, "memset", P_.B["kpa"][64:128, :], 0.0,
                                       waits=(P_.dtok("a"), P_.dtok("wb"), P_.tok("pe"))))
                        zt.append(P_.V(v, "memset", P_.B["kpb"][0:64, :], 0.0))
                    ta = seg_attn_l0(P_, (t0, zt[1], ct[0], P_.tok("ccA"),
                                          P_.tok("ccAv")), mid_hook=tern_hook)
            with ES2() as st_w:
                _alloc_proj(P_, st_w, ring_ch=8, ring_cols=10240)
                rsS = RingSched(P_, [(0, 1024)])
                rsL = RingSched(P_, [(2048, 6144)])
                P_.sb("ev_sb", [128, 8 * 512], F32, stack=st_w)
                P_.ps("pg0", [128, min(TOK, 2048)], stack=st_w)
                P_.ps("pg1", [128, min(TOK, 2048)], stack=st_w)
                tx1 = seg_wo_ln(P_, "w_o0", "b_o0", (ta, t0, P_.dtok("tg"), P_.tok("gp")),
                                rs=rsS)
                tln2 = seg_prenorm(P_, "x_sb", (tx1,))
                tkk = tln2
                for qt in range(cfg["NQT"]):
                    P_.sb("h_sb", [128, 32 * QT], BF16, stack=st_w) if qt == 0 else None
                    sl, sk, rd = rsS.next()
                    th, _, mm1 = seg_proj(P_, "ln_at", "w1t", 32, bias="b_1", act=AF.Gelu,
                                          out_sbuf="h_sb", out_sbuf_w=QT, qw=QT,
                                          src_off=qt * QT, src_w=TOK, w_is_bf16=True,
                                          dep_toks=(tkk,) + tuple(tern_stores),
                                          w_dep=tern_stores, ring_slots=sl, ring_dep=rd)
                    rsS.done(sk, mm1)
                    sl, sk, rd = rsL.next()
                    to, _, mm2 = seg_proj(P_, "h_sb", "w2t", 8, bias="b_2",
                                          out_sbuf="ev_sb", out_sbuf_w=QT, qw=QT,
                                          src_w=QT, w_is_bf16=True, dep_toks=(th,),
                                          evac_dep=(tkk,), w_dep=tern_stores,
                                          ring_slots=sl, ring_dep=rd)
                    rsL.done(sk, mm2)
                    tkk = seg_postnorm_residual(P_, "ev_sb", QT, (to,),
                                                (qt * QT, (qt + 1) * QT))
                tln3 = seg_prenorm(P_, "x_sb", (tkk,))
                sl, sk, rd = rsS.next()
                tq2, _, mmq2 = seg_proj(P_, "ln_at", "w_q1", 8, out_dram=ccBq_in,
                                        dep_toks=(tln3,), ring_slots=sl, ring_dep=rd)
                rsS.done(sk, mmq2)
                @blk.gpsimd
                def _(g):
                    P_.wait(g, tq2, P_.dtok("a"))
                    insg = g.collective_compute(
                        "AllGather", ALU.bypass,
                        replica_groups=[[0, 1, 2, 3], [4, 5, 6, 7]],
                        ins=[ccBq_in.opt()], outs=[ccBq_out.opt()])
                    insg.then_inc(P_.sems["ccBq"])
                    P_.cnt["ccBq"] = 1
                sl, sk, rd = rsS.next()
                tk2, _, mmk2 = seg_proj(P_, "ln_at", "w_k1", 8, out_dram=ccBk_in,
                                        dep_toks=(tln3,), ring_slots=sl, ring_dep=rd)
                rsS.done(sk, mmk2)
                @blk.gpsimd
                def _(g):
                    P_.wait(g, tk2, P_.dtok("a"))
                    insg = g.collective_compute(
                        "AllGather", ALU.bypass,
                        replica_groups=[[0, 1, 2, 3], [4, 5, 6, 7]],
                        ins=[ccBk_in.opt()], outs=[ccBk_out.opt()])
                    insg.then_inc(P_.sems["ccBk"])
                    P_.cnt["ccBk"] = 1
            with ES2() as st_v:
                P_.sb("w_ring", [128, 2 * 8 * 128], BF16, stack=st_v)
                P_.B["_ring_grpw"] = 8 * 128
                P_.sb("tstage", [128, TOK], BF16, stack=st_v)
                P_.sb("tstage2", [128, 128], BF16, stack=st_v)
                P_.ps("pg0", [128, min(TOK, 2048)], stack=st_v)
                P_.ps("tr_ps", [128, 256], BF16, stack=st_v)
                tv2, _, _ = seg_proj(
                    P_, "ln_at", "w_v1", 8,
                    transpose_dram=ccBv_in.rearrange("r (g d) -> (r g) d", g=VG),
                    single_group=True, dep_toks=(tln3,))
            @blk.gpsimd
            def _(g):
                P_.wait(g, tv2, P_.dtok("a"))
                ins = g.collective_compute(
                    "AllGather", ALU.bypass,
                    replica_groups=[[0, 1, 2, 3], [4, 5, 6, 7]],
                    ins=[ccBv_in.opt()], outs=[ccBv_out.opt()])
                ins.then_inc(P_.sems["ccB"])
                P_.cnt["ccB"] = 1

            with ES2() as st_a2:
                P_.sb("kpa", [128, KB * 128], BF16, stack=st_a2)
                P_.sb("kpb", [128, KB * 128], BF16, stack=st_a2)
                P_.sb("vp", [128, KB * 130], BF16, stack=st_a2)
                P_.sb("qp", [128, TB], BF16, stack=st_a2)
                P_.sb("av_sb", [128, 2 * 2 * QT], F32, stack=st_a2)
                P_.B["rec_sb"] = P_.B["st_sb"]
                P_.sb("ex_sb", [128, 2 * 1536], BF16, stack=st_a2)
                P_.ps("sc0", [128, 1536], stack=st_a2)
                P_.ps("sc1", [128, 1536], stack=st_a2)
                P_.ps("avA", [128, QT], stack=st_a2)
                P_.ps("avB", [128, QT], stack=st_a2)
                tmask = seg_castload(P_, "c_masks", "c_masks_bf",
                                     list(P_.D["c_masks"].shape), stack=st_a2)
                zt2 = []
                @blk.vector
                def _(v):
                    zt2.append(P_.V(v, "memset", P_.B["kpa"][64:128, :], 0.0,
                                    waits=(P_.dtok("a"), P_.dtok("wb"), P_.tok("pe"),
                                           tmask)))
                    zt2.append(P_.V(v, "memset", P_.B["kpb"][0:64, :], 0.0))

                kqix = P_.B["l1kq_idx"]
                vix = P_.B["l1v_idx"]
                ccBv = ccBv_out.rearrange("r (g e) -> (r g) e", g=TOK // 64)

                def l1_loader(bp, dep, last):
                    reg = {}
                    @blk.gpsimd
                    def _(g):
                        P_.wait(g, *dep)
                        P_.wait(g, zt2[1], P_.tok("ccBq"), P_.tok("ccBk"))
                        if last is not None:
                            P_.wait(g, last)
                        for rr in range(4):
                            P_.dma_i(g, "w", P_.B["qp"][:, TOK * rr:TOK * (rr + 1)],
                                     ccBq_out, kqix[:, bp * 12 + rr:bp * 12 + rr + 1])
                            P_.dma_i(g, "w", P_.B["kpa"][0:64, TOK * rr:TOK * (rr + 1)],
                                     ccBk_out, kqix[0:64, bp * 12 + 4 + rr:bp * 12 + 5 + rr])
                            P_.dma_i(g, "w", P_.B["kpb"][64:128, TOK * rr:TOK * (rr + 1)],
                                     ccBk_out, kqix[0:64, bp * 12 + 8 + rr:bp * 12 + 9 + rr])
                        P_.wait(g, P_.tok("ccB"))
                        for ka in range(KB):
                            for h in (0, 1):
                                P_.dma_i(
                                    g, "w",
                                    P_.B["vp"][:, 130 * ka + 65 * h:130 * ka + 65 * h + 64],
                                    ccBv,
                                    vix[:, bp * 128 + 2 * ka + h:bp * 128 + 2 * ka + h + 1])
                            if (ka + 1) % 8 == 0 or ka == KB - 1:
                                reg[ka // 8] = P_.dtok("w")

                    def ldtok(k, kind, reg=reg):
                        return reg[min(k // 8, max(reg))]
                    return ldtok

                def l1_mid(last_bp0):
                    @blk.sync
                    def _(s):
                        P_.wait(s, last_bp0)
                        P_.dma(s, "a", ccC_in[0], P_.B["ln_at"][:, 0:TB])
                    @blk.gpsimd
                    def _(g):
                        P_.wait(g, P_.dtok("a"))
                        insg = g.collective_compute(
                            "AllGather", ALU.bypass,
                            replica_groups=[[0, 1, 2, 3], [4, 5, 6, 7]],
                            ins=[ccC_in[0].opt()], outs=[ccC_out[0].opt()])
                        insg.then_inc(P_.sems["ccC"])
                        P_.cnt["ccC"] = 1

                ta2 = seg_attn_l1(P_, (ix, P_.tok("ccB")), loader=l1_loader,
                                  mid_hook=l1_mid)
                @blk.sync
                def _(s):
                    P_.wait(s, ta2)
                    P_.dma(s, "a", ccC_in[1], P_.B["ln_at"][:, TB:2 * TB])
                @blk.gpsimd
                def _(g):
                    P_.wait(g, P_.dtok("a"))
                    ins = g.collective_compute(
                        "AllGather", ALU.bypass,
                        replica_groups=[[0, 1, 2, 3], [4, 5, 6, 7]],
                        ins=[ccC_in[1].opt()], outs=[ccC_out[1].opt()])
                    ins.then_inc(P_.sems["ccC1"])
                    P_.cnt["ccC1"] = 1
                gat = []
                @blk.gpsimd
                def _(g):
                    P_.wait(g, P_.tok("ccC"), P_.tok("ccC1"), ta2, P_.dtok("a"))
                    for ch in range(8):
                        ccCv = ccC_out[ch // 4].rearrange("d (r t) -> (d r) t", r=4)
                        gat.append(P_.dma_i(
                            g, "w", P_.B["ln_at"][:, ch * TOK:(ch + 1) * TOK],
                            ccCv, P_.B["cc_idx"][:, ch:ch + 1]))
            with ES2() as st_w2:
                _alloc_proj(P_, st_w2, dstage=False, ngrp=4)
                rs2 = RingSched(P_, [(0, 1024), (2048, 3072)])
                P_.sb("ev_sb", [128, 8 * 512], F32, stack=st_w2)
                P_.ps("pg0", [128, min(TOK, 2048)], stack=st_w2)
                P_.ps("pg1", [128, min(TOK, 2048)], stack=st_w2)
                tx3 = seg_wo_ln(P_, "w_o1", "b_o1", (P_.dtok("w"), P_.tok("gp")),
                                rs=rs2)
                tln4a = seg_prenorm(P_, "x_sb", (tx3,), qrange=(TOK - 512, TOK))
                @blk.sync
                def _(s):
                    P_.wait(s, tln4a)
                    P_.dma(s, "a", ccD_in.rearrange("c p -> p c"),
                           P_.B["ln_at"][:, TOK - 1::TOK], allow_slow_non_contiguous=True)
                @blk.gpsimd
                def _(g):
                    P_.wait(g, P_.dtok("a"))
                    ins = g.collective_compute(
                        "AllGather", ALU.bypass,
                        replica_groups=[[0, 1, 2, 3, 4, 5, 6, 7]],
                        ins=[ccD_in.opt()], outs=[ccD_out.opt()])
                    ins.then_inc(P_.sems["ccD"])
                    P_.cnt["ccD"] = 1
                bnd_raw = P_.sb("bnd_raw", [128, 8], BF16, stack=st_w2)
                P_.sb("bnd_sb", [128, 8], BF16, stack=st_w2)
                ccDv = ccD_out.rearrange("r (p o) -> (r p) o", o=1)
                bg = []
                @blk.gpsimd
                def _(g):
                    P_.wait(g, P_.tok("ccD"))
                    for cc in range(8):
                        bg.append(P_.dma_i(
                            g, "w", P_.B["bnd_raw"][:, cc:cc + 1],
                            ccDv, P_.B["bnd_idx"][:, cc:cc + 1]))
                bm = []
                @blk.vector
                def _(v):
                    bm.append(P_.V(v, "tensor_tensor", P_.B["bnd_sb"][:], P_.B["bnd_raw"][:],
                                   P_.B["bnd_valid"][:], ALU.mult,
                                   waits=(P_.dtok("w"), ct[0])))
                tln4 = seg_prenorm(P_, "x_sb", (tln4a,), qrange=(0, TOK - 512))
                NQT = cfg["NQT"]
                P_.sb("xr_sb", [128, 8 * QT], BF16, stack=st_w2)
                P_.sb("xk_sb", [128, 8 * QT], BF16, stack=st_w2)
                P_.sb("r_sb", [128, 8 * QT], F32, stack=st_w2)
                P_.sb("ka_sb", [128, 8 * QT], BF16, stack=st_w2)
                P_.sb("tstage", [128, QT], BF16, stack=st_w2)
                P_.sb("mx_sc", [128, 2 * QT], BF16, stack=st_w2)
                tkk = bm[0]
                mixes = [None] * NQT
                mixes[0] = seg_ffn1_mix(P_, 0, (tln4, tln4a, bm[0]))
                hist = [None] * NQT
                prev_to = None
                for qt in range(NQT):
                    tm = mixes[qt]
                    xo = 0
                    sl, sk, rd = rs2.next()
                    tr, _, mmr = seg_proj(P_, "xr_sb", "w_r1T", 8, act=AF.Sigmoid,
                                          out_sbuf="r_sb", out_sbuf_w=QT, qw=QT, src_w=QT,
                                          src_off=xo, dep_toks=(tm,),
                                          evac_dep=(prev_to,) if prev_to else (),
                                          ring_slots=sl, ring_dep=rd)
                    rs2.done(sk, mmr)
                    sl, sk, rd = rs2.next()
                    tka, _, mmka = seg_proj(P_, "xk_sb", "w_k1T", 8, evac_relu_sq=True,
                                            out_sbuf="ka_sb", out_sbuf_w=QT, qw=QT, src_w=QT,
                                            src_off=xo, dep_toks=(tm,), evac_dep=(tkk,),
                                            ring_slots=sl, ring_dep=rd)
                    rs2.done(sk, mmka)
                    sl, sk, rd = rs2.next()
                    to, _, mmv2 = seg_proj(P_, "ka_sb", "w_v1T", 8, evac_mult=("r_sb", QT),
                                           out_sbuf="ev_sb", out_sbuf_w=QT, qw=QT, src_w=QT,
                                           dep_toks=(tka,), evac_dep=((tkk,) if tkk else ()) + (tr,),
                                           ring_slots=sl, ring_dep=rd)
                    rs2.done(sk, mmv2)
                    hist[qt] = (mmr[-1], mmka[-1])
                    prev_to = to
                    if qt + 1 < NQT:
                        mixes[qt + 1] = seg_ffn1_mix(
                            P_, qt + 1, (tln4, tln4a) + tuple(hist[qt]))
                    tkk = seg_postnorm_residual(P_, "ev_sb", QT, (to,),
                                                (qt * QT, (qt + 1) * QT))
                seg_store_x(P_, outT, (tkk,))
            @blk.sync
            def _(s):
                for stem in ("a", "wb", "t"):
                    for i in range(P.NDS):
                        if P_.cnt[stem + str(i)]:
                            s.wait_ge(P_.sems[stem + str(i)], P_.cnt[stem + str(i)])
            @blk.gpsimd
            def _(g):
                for stem in ("w", "tg"):
                    for i in range(P.NDS):
                        if P_.cnt[stem + str(i)]:
                            g.wait_ge(P_.sems[stem + str(i)], P_.cnt[stem + str(i)])
    return nc
